# version 01
# baseline (speedup 1.0000x reference)
import numpy as np
from contextlib import ExitStack

import concourse.bass as bass
import concourse.mybir as mybir
from concourse.bass_utils import run_bass_kernel_spmd

F32 = mybir.dt.float32
BF16 = mybir.dt.bfloat16
AF = mybir.ActivationFunctionType
ALU = mybir.AluOpType
AX = mybir.AxisListType

NCORES = 8
B = 2
T = 8192
DM = 1024
TOK = 2048
NSUB = 8
FC = 8
KVROWS = 2056
ENGINES = ("sync", "vector", "scalar", "tensor", "gpsimd")



_POS_CACHE = {}


def pos_arrays(core):
    if core not in _POS_CACHE:
        bidx, pidx = [], []
        for b in range(B):
            for jj in range(NSUB):
                s = 8 * jj + core
                for r in range(128):
                    bidx.append(b)
                    pidx.append(s * 128 + r)
        _POS_CACHE[core] = (np.array(bidx), np.array(pidx))
    return _POS_CACHE[core]


def shard_stream(x):
    out = []
    for core in range(NCORES):
        bidx, pidx = pos_arrays(core)
        out.append(np.ascontiguousarray(x[bidx, pidx, :].T.astype(np.float32)))
    return out


def unshard_stream(parts):
    x = np.empty((B, T, DM), dtype=np.float32)
    for core in range(NCORES):
        bidx, pidx = pos_arrays(core)
        x[bidx, pidx, :] = parts[core].T
    return x


def rope_tables(core, HD, kind):
    base = 10000.0
    half = HD // 2
    if kind == "half":
        ang = np.power(base, -(np.arange(half).astype(np.float64)) / half)
    else:
        ang = base ** (-2.0 * (np.arange(half).astype(np.float64) / HD))
    ang = ang.astype(np.float32)
    _, pidx = pos_arrays(core)
    pos = pidx.astype(np.float32)
    rad = (pos[None, :] * ang[:, None]).astype(np.float32)
    cos = np.cos(rad).astype(np.float32)
    sin = np.sin(rad).astype(np.float32)
    reps = 64 // half
    cos64 = np.concatenate([cos] * reps, axis=0)
    sin64 = np.concatenate([sin] * reps, axis=0)
    cs1 = np.concatenate([cos64, sin64], axis=0)
    cs2 = np.concatenate([sin64, -cos64], axis=0)
    return np.ascontiguousarray(cs1), np.ascontiguousarray(cs2)


def qk_perm(HD, kind):
    perm = []
    hpc = 128 // HD
    half = HD // 2
    for c in range(DM // 128):
        hs = [c * hpc + k for k in range(hpc)]
        ev_cols, od_cols = [], []
        for h in hs:
            if kind == "int":
                ev = [h * HD + 2 * i for i in range(half)]
                od = [h * HD + 2 * i + 1 for i in range(half)]
            else:
                ev = [h * HD + i for i in range(half)]
                od = [h * HD + half + i for i in range(half)]
            ev_cols += ev
            od_cols += od
        perm += ev_cols + od_cols
    return np.array(perm)


def make_causal_mask(core):
    import ml_dtypes
    p = np.arange(128)[:, None, None]
    u = np.arange(8)[None, :, None]
    q = np.arange(128)[None, None, :]
    m = np.where((u * 128 + p) <= (core * 128 + q), 0.0, -1e30).astype(np.float32)
    return np.ascontiguousarray(m.astype(ml_dtypes.bfloat16))


def chunked_w(w):
    return np.ascontiguousarray(np.asarray(w, dtype=np.float32).reshape(FC, 128, -1))


def colsplit(v):
    n = np.asarray(v).shape[0] // 128
    return np.ascontiguousarray(np.asarray(v, dtype=np.float32).reshape(n, 128).T)



class Sem:
    __slots__ = ("h", "n", "name")


CHAINED = ("vector", "scalar", "gpsimd")


class PB:
    def __init__(self, nc, ctx):
        self.nc = nc
        self.ctx = ctx
        self.q = {e: [] for e in ENGINES}
        self.chain = {}

    def init_chains(self):
        for e in CHAINED:
            self.chain[e] = self.sem("chain_" + e)

    def sem(self, name):
        s = Sem()
        s.h = self.ctx.enter_context(self.nc.semaphore(name))
        s.n = 0
        s.name = name
        return s

    def emit(self, eng, fn, wait=(), inc=(), inc_amt=1, chain=True):
        wait = list(wait)
        inc = list(inc)
        if chain and eng in self.chain:
            cs = self.chain[eng]
            wait.append((cs, cs.n))
            cs.n += 1
            for s in inc:
                s.h = cs.h
                s.n = cs.n
            real_incs = [(cs.h, 1)]
        else:
            real_incs = []
            for s in inc:
                s.n += inc_amt
                real_incs.append((s.h, inc_amt))
        waitv = [(s.h, v) for (s, v) in wait if v > 0]

        def run(E, fn=fn, waitv=waitv, incs=real_incs):
            for (h, v) in waitv:
                E.wait_ge(h, v)
            ins = fn(E)
            for (h, a) in incs:
                ins = ins.then_inc(h, a)
        self.q[eng].append(run)

    def dma(self, eng, out, in_, wait=(), inc=(), **kw):
        self.emit(eng, lambda E: E.dma_start(out=out, in_=in_, **kw),
                  wait=wait, inc=inc, inc_amt=16, chain=False)

    def at(self, sem):
        return (sem, sem.n)

    def realize(self):
        blk = self.ctx.enter_context(self.nc.Block())
        for e in ENGINES:
            steps = self.q[e]
            getattr(blk, e)(lambda E, steps=steps: [f(E) for f in steps])



def build_attn(cfg):
    H, HD = cfg["H"], cfg["HD"]
    HPC = 128 // HD
    scale = float(cfg["scale"])
    rope = cfg["rope"]
    two = cfg["two_norms"]
    qk_bias = cfg["qk_bias"]
    o_bias = cfg["o_bias"]
    dbg = set(cfg.get("debug", ()))

    nc = bass.Bass("TRN2", target_bir_lowering=False)
    xin = nc.dram_tensor("xin", [DM, TOK], F32, kind="ExternalInput").ap()
    wqkv = nc.dram_tensor("wqkv", [FC, 128, 3 * DM], F32, kind="ExternalInput").ap()
    wo_d = nc.dram_tensor("wo", [FC, 128, DM], F32, kind="ExternalInput").ap()
    maskd = nc.dram_tensor("maskin", [128, 8, 128], BF16, kind="ExternalInput").ap()
    xout = nc.dram_tensor("xout", [DM, TOK], F32, kind="ExternalOutput").ap()
    zout = nc.dram_tensor("zout", [DM, TOK], F32, kind="ExternalOutput").ap() if two else None
    if rope:
        cs1_d = nc.dram_tensor("cossin1", [128, TOK], F32, kind="ExternalInput").ap()
        cs2_d = nc.dram_tensor("cossin2", [128, TOK], F32, kind="ExternalInput").ap()
    if qk_bias:
        qkb_d = nc.dram_tensor("qkbias", [128, 16], F32, kind="ExternalInput").ap()
        vb_d = nc.dram_tensor("vbias", [1, DM], F32, kind="ExternalInput").ap()
    if o_bias:
        ob_d = nc.dram_tensor("obias", [128, FC], F32, kind="ExternalInput").ap()
    if two:
        gain_d = nc.dram_tensor("gain", [128, FC], F32, kind="ExternalInput").ap()

    kv_in = nc.dram_tensor("kv_in", [KVROWS, TOK], BF16, kind="Internal").ap()
    kv_all = nc.dram_tensor("kv_all", [NCORES * KVROWS, TOK], BF16, kind="Internal").ap()
    kv_all_r = kv_all.rearrange("(r k) t -> r k t", r=NCORES)

    dbg_t = {}
    def dbg_mk(name, shape):
        dbg_t[name] = nc.dram_tensor(name, shape, F32, kind="ExternalOutput").ap()
    if "xn" in dbg: dbg_mk("dbg_xn", [DM, TOK])
    if "q" in dbg: dbg_mk("dbg_q", [DM, TOK])
    if "kv" in dbg: dbg_mk("dbg_kv", [KVROWS, TOK])
    if "c" in dbg: dbg_mk("dbg_c", [128, 16, 16])
    if "o" in dbg: dbg_mk("dbg_o", [128, B * NSUB, DM])
    if "l" in dbg: dbg_mk("dbg_l", [128, B * NSUB, 16])

    ctx = ExitStack()
    with ctx:
        pb = PB(nc, ctx)
        pb.init_chains()
        sbt = lambda n, s, d: ctx.enter_context(nc.sbuf_tensor(n, s, d))

        q_sb = sbt("q_sb", [128, FC, TOK], BF16)
        c_all = sbt("c_all", [128, 16, 16], F32)
        qmn = sbt("qmn", [1, 256], F32)
        kmaxrow = sbt("kmaxrow", [1, 32], F32)
        kn_sb = sbt("kn_sb", [1, 32], F32)
        ones_f = sbt("ones_f", [128, 1], F32)
        ones_bf = sbt("ones_bf", [128, 1], BF16)
        onesr_f = sbt("onesr_f", [1, 128], F32)
        nonesr_f = sbt("nonesr_f", [1, 128], F32)
        ident = sbt("ident", [128, 128], BF16)
        eps_c = sbt("eps_c", [1, 1], F32)
        eps2_c = sbt("eps2_c", [1, 1], F32)
        if qk_bias:
            qkb_sb = sbt("qkb_sb", [128, 16], F32)
        if o_bias:
            ob_sb = sbt("ob_sb", [128, FC], F32)
        if two:
            g_sb = sbt("g_sb", [128, FC], F32)

        s_in = pb.sem("s_in")
        s_gp = pb.sem("s_gp")
        s_x = pb.sem("s_x")
        s_ln = pb.sem("s_ln")
        s_lnmm = pb.sem("s_lnmm")
        s_lnact = pb.sem("s_lnact")
        s_w = pb.sem("s_w")
        s_mm = pb.sem("s_mm")
        s_ev = pb.sem("s_ev")
        s_act = pb.sem("s_act")
        s_st = pb.sem("s_st")
        s_ag = pb.sem("s_ag")
        s_kt = pb.sem("s_kt")
        s_fl = pb.sem("s_fl")
        s_fa = pb.sem("s_fa")
        s_fe = pb.sem("s_fe")
        s_out = pb.sem("s_out")
        s_gpd = pb.sem("s_gpd")
        s_xt = [pb.sem("s_xt0"), pb.sem("s_xt1")]
        s_zex = pb.sem("s_zex")
        s_cs = pb.sem("s_cs")
        s_mk = pb.sem("s_mk")
        s_ktp = [pb.sem("s_ktp0"), pb.sem("s_ktp1")]
        s_wv = [pb.sem("s_wv0"), pb.sem("s_wv1")]
        s_bas = [pb.sem("s_bas0"), pb.sem("s_bas1")]
        s_so = [pb.sem("s_so0"), pb.sem("s_so1")]
        s_wb = [pb.sem("s_wb0"), pb.sem("s_wb1")]
        s_kst = pb.sem("s_kst")
        s_vst = pb.sem("s_vst")

        pb.emit("gpsimd", lambda E: E.memset(ones_f[:], 1.0))
        pb.emit("gpsimd", lambda E: E.memset(ones_bf[:], 1.0))
        pb.emit("gpsimd", lambda E: E.memset(onesr_f[:], 1.0))
        pb.emit("gpsimd", lambda E: E.memset(nonesr_f[:], -1.0))
        pb.emit("gpsimd", lambda E: E.memset(eps_c[:], 1e-5))
        pb.emit("gpsimd", lambda E: E.memset(eps2_c[:], 1e-6))
        pb.emit("gpsimd", lambda E: E.memset(kn_sb[:], 0.0))
        pb.emit("gpsimd", lambda E: E.memset(qmn[:], 0.0))
        pb.emit("gpsimd", lambda E: E.memset(kmaxrow[:], 0.0))
        pb.emit("gpsimd", lambda E: E.memset(c_all[:], 0.0))
        pb.emit("gpsimd", lambda E: E.memset(ident[:], 1.0))
        pb.emit("gpsimd", lambda E: E.affine_select(
            out=ident[:], in_=ident[:], pattern=[[-1, 128]],
            compare_op=ALU.is_equal, fill=0.0, base=0, channel_multiplier=1),
            inc=[s_gp])
        CONST = pb.at(s_gp)
        if qk_bias:
            pb.dma("sync", qkb_sb[:], qkb_d, inc=[s_in])
        if o_bias:
            pb.dma("sync", ob_sb[:], ob_d, inc=[s_in])
        if two:
            pb.dma("sync", g_sb[:], gain_d, inc=[s_in])
        IN0 = pb.at(s_in)

        phx = ExitStack()
        xn_sb = phx.enter_context(nc.sbuf_tensor("xn_sb", [128, FC, TOK], BF16))
        def ln_stats(src_ap, rms_only, tt):
            (xss, xqs, zsl, srow, mrow, rrow, mb, rb) = tt[:8]
            mean_ps, sq_ps = tt[8], tt[9]
            cons_mm = {}
            cons_act = {}
            for idx in range(32):
                fc, tc = idx // 4, idx % 4
                tl = xss[idx % 2]
                xq = xqs[idx % 2]
                if idx >= 2:
                    w = [(s_lnmm, cons_mm[idx - 2]), (s_lnact, cons_act[idx - 2])]
                else:
                    w = [pb.at(s_xt[idx % 2]), pb.at(s_ln), pb.at(s_zex)]
                pb.dma("sync", tl[:],
                       src_ap[fc * 128:(fc + 1) * 128, tc * 512:(tc + 1) * 512],
                       wait=w, inc=[s_xt[idx % 2]])
                XL = pb.at(s_xt[idx % 2])
                if not rms_only:
                    pb.emit("tensor", lambda E, tl=tl, fc=fc, tc=tc: E.matmul(
                        out=mean_ps[:, tc * 512:(tc + 1) * 512],
                        lhsT=ones_f[:], rhs=tl[:],
                        start=(fc == 0), stop=(fc == FC - 1)),
                        wait=[XL, CONST], inc=[s_lnmm])
                pb.emit("scalar", lambda E, tl=tl, xq=xq: E.activation(
                    out=xq[:], in_=tl[:], func=AF.Square),
                    wait=[XL] + ([(s_lnmm, cons_mm[idx - 2])] if idx >= 2 else []),
                    inc=[s_lnact])
                pb.emit("tensor", lambda E, xq=xq, fc=fc, tc=tc: E.matmul(
                    out=sq_ps[:, tc * 512:(tc + 1) * 512],
                    lhsT=ones_f[:], rhs=xq[:],
                    start=(fc == 0), stop=(fc == FC - 1)),
                    wait=[pb.at(s_lnact), CONST], inc=[s_lnmm])
                cons_mm[idx] = s_lnmm.n
                cons_act[idx] = s_lnact.n
            if not rms_only:
                pb.emit("vector", lambda E: E.tensor_scalar_mul(
                    mrow[:], mean_ps[:], 1.0 / DM),
                    wait=[pb.at(s_lnmm)], inc=[s_ln])
                pb.emit("vector", lambda E: E.tensor_tensor(
                    out=srow[:], in0=mrow[:], in1=mrow[:], op=ALU.mult),
                    wait=[pb.at(s_ln)], inc=[s_ln])
                pb.emit("vector", lambda E: E.scalar_tensor_tensor(
                    out=srow[:], in0=sq_ps[:], scalar=1.0 / DM, in1=srow[:],
                    op0=ALU.mult, op1=ALU.subtract),
                    wait=[pb.at(s_ln)], inc=[s_ln])
            else:
                pb.emit("vector", lambda E: E.tensor_scalar_mul(
                    srow[:], sq_ps[:], 1.0 / DM),
                    wait=[pb.at(s_lnmm)], inc=[s_ln])
            eps_use = eps2_c if rms_only else eps_c
            pb.emit("scalar", lambda E, e=eps_use: E.activation(
                out=rrow[:], in_=srow[:], func=AF.Sqrt, bias=e[:]),
                wait=[pb.at(s_ln), CONST], inc=[s_lnact])
            pb.emit("vector", lambda E: E.reciprocal(out=rrow[:], in_=rrow[:]),
                    wait=[pb.at(s_lnact)], inc=[s_ln])

        def ln_apply(src_ap, rms_only, out_bf, export_dram, tt):
            (xss, xqs, zsl, srow, mrow, rrow, mb, rb) = tt[:8]
            bc1_ps, bc2_ps = tt[10], tt[11]
            RR = pb.at(s_ln)
            for tc in range(4):
                sl = slice(tc * 512, (tc + 1) * 512)
                pb.emit("tensor", lambda E, sl=sl: E.matmul(
                    out=bc1_ps[:], lhsT=onesr_f[:], rhs=rrow[:, sl],
                    start=True, stop=True),
                    wait=[RR, pb.at(s_ln), CONST], inc=[s_lnmm])
                pb.emit("vector", lambda E: E.tensor_copy(out=rb[:], in_=bc1_ps[:]),
                        wait=[pb.at(s_lnmm), pb.at(s_ln)], inc=[s_ln])
                if not rms_only:
                    pb.emit("tensor", lambda E, sl=sl: E.matmul(
                        out=bc2_ps[:], lhsT=onesr_f[:], rhs=mrow[:, sl],
                        start=True, stop=True),
                        wait=[pb.at(s_ln)], inc=[s_lnmm])
                    pb.emit("vector", lambda E: E.tensor_copy(out=mb[:], in_=bc2_ps[:]),
                            wait=[pb.at(s_lnmm)], inc=[s_ln])
                BC = pb.at(s_ln)
                for fc in range(FC):
                    idx = tc * 8 + fc
                    tl = xss[idx % 2]
                    xq = xqs[idx % 2]
                    pb.dma("sync", tl[:], src_ap[fc * 128:(fc + 1) * 128, sl],
                           wait=[pb.at(s_ln), pb.at(s_xt[idx % 2]), pb.at(s_zex)],
                           inc=[s_xt[idx % 2]])
                    XL = pb.at(s_xt[idx % 2])
                    if not rms_only:
                        pb.emit("vector", lambda E, tl=tl, xq=xq: E.tensor_tensor(
                            out=xq[:], in0=tl[:], in1=mb[:], op=ALU.subtract),
                            wait=[XL, BC, pb.at(s_ln)], inc=[s_ln])
                        srct = xq
                    else:
                        srct = tl
                    if export_dram is not None:
                        pb.emit("vector", lambda E, st=srct: E.tensor_tensor(
                            out=zsl[:], in0=st[:], in1=rb[:], op=ALU.mult),
                            wait=[pb.at(s_ln), BC, XL, pb.at(s_zex)], inc=[s_ln])
                        pb.dma("sync", export_dram[fc * 128:(fc + 1) * 128, sl],
                               zsl[:], wait=[pb.at(s_ln), pb.at(s_zex)],
                               inc=[s_zex])
                    if out_bf is not None:
                        if rms_only and two:
                            pb.emit("vector", lambda E, st=srct: E.tensor_tensor(
                                out=zsl[:], in0=st[:], in1=rb[:], op=ALU.mult),
                                wait=[pb.at(s_ln), BC, XL, pb.at(s_zex)], inc=[s_ln])
                            pb.emit("vector", lambda E, fc=fc, sl=sl: E.tensor_scalar_mul(
                                out_bf[:, fc, sl], zsl[:], g_sb[:, fc:fc + 1]),
                                wait=[pb.at(s_ln), IN0], inc=[s_ln])
                        else:
                            pb.emit("vector", lambda E, st=srct, fc=fc, sl=sl: E.tensor_tensor(
                                out=out_bf[:, fc, sl], in0=st[:], in1=rb[:],
                                op=ALU.mult),
                                wait=[pb.at(s_ln), BC, XL], inc=[s_ln])

        with nc.sbuf_tensor("xs0", [128, 512], F32) as xs0, \
             nc.sbuf_tensor("xs1", [128, 512], F32) as xs1, \
             nc.sbuf_tensor("xq0", [128, 512], F32) as xq0, \
             nc.sbuf_tensor("xq1", [128, 512], F32) as xq1, \
             nc.sbuf_tensor("zsl", [128, 512], F32) as zsl, \
             nc.sbuf_tensor("srow", [1, TOK], F32) as srow, \
             nc.sbuf_tensor("mrow", [1, TOK], F32) as mrow, \
             nc.sbuf_tensor("rrow", [1, TOK], F32) as rrow, \
             nc.sbuf_tensor("mb", [128, 512], F32) as mb, \
             nc.sbuf_tensor("rb", [128, 512], F32) as rb:
            tt = ([xs0, xs1], [xq0, xq1], zsl, srow, mrow, rrow, mb, rb,
                  None, None, None, None)
            tt = list(tt)
            passes = [(xin, False, xn_sb, None)] if not two else \
                     [(xin, False, None, zout), (zout, True, xn_sb, None)]
            for pi, (srcp, rmsp, outp, expp) in enumerate(passes):
                with nc.psum_tensor("mean_ps%d" % pi, [1, TOK], F32) as mean_ps, \
                     nc.psum_tensor("sq_ps%d" % pi, [1, TOK], F32) as sq_ps:
                    tt[8], tt[9] = mean_ps, sq_ps
                    ln_stats(srcp, rmsp, tt)
                with nc.psum_tensor("bc1_ps%d" % pi, [128, 512], F32) as bc1_ps, \
                     nc.psum_tensor("bc2_ps%d" % pi, [128, 512], F32) as bc2_ps:
                    tt[10], tt[11] = bc1_ps, bc2_ps
                    ln_apply(srcp, rmsp, outp, expp, tt)

        if "dbg_xn" in dbg_t:
            for fc in range(FC):
                pb.dma("gpsimd", dbg_t["dbg_xn"][fc * 128:(fc + 1) * 128, :],
                       xn_sb[:, fc, :], wait=[pb.at(s_ln)], inc=[s_out])
        LN_V = pb.at(s_ln)

        with nc.sbuf_tensor("cs1_sb", [128, TOK], F32) as cs1_sb, \
             nc.sbuf_tensor("cs2_sb", [128, TOK], F32) as cs2_sb, \
             nc.sbuf_tensor("wbuf0", [128, FC, 128], BF16) as wbuf0, \
             nc.sbuf_tensor("wbuf1", [128, FC, 128], BF16) as wbuf1, \
             nc.sbuf_tensor("kstage", [128, TOK], BF16) as kstage, \
             nc.sbuf_tensor("ksq", [128, TOK], BF16) as ksq, \
             nc.sbuf_tensor("statpad", [KVROWS - 2048, TOK], BF16) as statpad, \
             nc.sbuf_tensor("r1", [64, 512], F32) as r1, \
             nc.sbuf_tensor("r2", [64, 512], F32) as r2, \
             nc.sbuf_tensor("r3", [64, 512], F32) as r3, \
             nc.sbuf_tensor("r4", [64, 512], F32) as r4, \
             nc.psum_tensor("qkv_ps0", [128, 512], F32) as qkv_ps0, \
             nc.psum_tensor("qkv_ps1", [128, 512], F32) as qkv_ps1, \
             nc.psum_tensor("kn_ps", [1, TOK], F32) as kn_ps:
            wbufs = [wbuf0, wbuf1]
            pss = [qkv_ps0, qkv_ps1]
            P1D = [LN_V, pb.at(s_zex), pb.at(s_lnmm), pb.at(s_lnact)]
            if rope:
                pb.dma("sync", cs1_sb[:], cs1_d, wait=P1D, inc=[s_cs])
                pb.dma("sync", cs2_sb[:], cs2_d, inc=[s_cs])
            CS0 = pb.at(s_cs)
            for oc in range(16):
                wb = wbufs[oc % 2]
                w = [pb.at(s_mm)] if oc >= 2 else P1D
                pb.dma("gpsimd", wb[:], wqkv[:, :, oc * 128:(oc + 1) * 128]
                       .rearrange("c p n -> p c n"), wait=w, inc=[s_wb[oc % 2]])
                WB = pb.at(s_wb[oc % 2])
                dest_q = oc < 8
                for tc in range(4):
                    sl = slice(tc * 512, (tc + 1) * 512)
                    pp = pss[tc % 2]
                    for mc in range(FC):
                        pb.emit("tensor", lambda E, wb=wb, mc=mc, sl=sl, pp=pp: E.matmul(
                            out=pp[:], lhsT=wb[:, mc, :], rhs=xn_sb[:, mc, sl],
                            start=(mc == 0), stop=(mc == FC - 1)),
                            wait=([WB, LN_V, pb.at(s_ev), pb.at(s_act)]
                                  if mc == 0 else []),
                            inc=[s_mm] if mc == FC - 1 else [])
                    MM = pb.at(s_mm)
                    dst = q_sb[:, oc, sl] if dest_q else kstage[:, sl]
                    if rope:
                        pb.emit("vector", lambda E, pp=pp, sl=sl: E.tensor_tensor(
                            out=r1[:], in0=pp[0:64, :], in1=cs1_sb[0:64, sl],
                            op=ALU.mult),
                            wait=[MM, CS0, pb.at(s_ev), pb.at(s_kst)], inc=[s_ev])
                        pb.emit("vector", lambda E, pp=pp, sl=sl: E.tensor_tensor(
                            out=r2[:], in0=pp[64:128, :], in1=cs1_sb[64:128, sl],
                            op=ALU.mult), inc=[s_ev])
                        pb.emit("vector", lambda E, pp=pp, sl=sl: E.tensor_tensor(
                            out=r3[:], in0=pp[0:64, :], in1=cs2_sb[0:64, sl],
                            op=ALU.mult), inc=[s_ev])
                        pb.emit("vector", lambda E, pp=pp, sl=sl: E.tensor_tensor(
                            out=r4[:], in0=pp[64:128, :], in1=cs2_sb[64:128, sl],
                            op=ALU.mult), inc=[s_ev])
                        EVR = pb.at(s_ev)
                        pb.emit("vector", lambda E, dst=dst: E.tensor_tensor(
                            out=dst[0:64, :], in0=r1[:], in1=r2[:],
                            op=ALU.subtract), wait=[EVR], inc=[s_ev])
                        pb.emit("vector", lambda E, dst=dst: E.tensor_tensor(
                            out=dst[64:128, :], in0=r3[:], in1=r4[:],
                            op=ALU.subtract), inc=[s_ev])
                    elif qk_bias:
                        pb.emit("vector", lambda E, pp=pp, dst=dst, oc=oc: E.tensor_scalar_add(
                            dst, pp[:], qkb_sb[:, oc:oc + 1]),
                            wait=[MM, IN0, pb.at(s_kst), pb.at(s_ev)], inc=[s_ev])
                    else:
                        pb.emit("vector", lambda E, pp=pp, dst=dst: E.tensor_copy(
                            out=dst, in_=pp[:]),
                            wait=[MM, pb.at(s_kst), pb.at(s_ev)], inc=[s_ev])
                if not dest_q:
                    kc = oc - 8
                    EVK = [pb.at(s_ev), pb.at(s_act)]
                    pb.dma("sync", kv_in[kc * 128:(kc + 1) * 128, :], kstage[:],
                           wait=EVK + [pb.at(s_kst)], inc=[s_kst])
                    pb.emit("scalar", lambda E: E.activation(
                        out=ksq[:], in_=kstage[:], func=AF.Square),
                        wait=EVK + [pb.at(s_mm)], inc=[s_act])
                    KSQ = pb.at(s_act)
                    for hh in range(HPC):
                        hsl = slice(hh * HD, (hh + 1) * HD)
                        head = kc * HPC + hh
                        for t2 in range(4):
                            sl2 = slice(t2 * 512, (t2 + 1) * 512)
                            pb.emit("tensor", lambda E, hsl=hsl, sl2=sl2: E.matmul(
                                out=kn_ps[:, sl2], lhsT=ones_bf[hsl, :],
                                rhs=ksq[hsl, sl2], start=True, stop=True),
                                wait=([KSQ, pb.at(s_ev), CONST] if t2 == 0 else []),
                                inc=[s_mm] if t2 == 3 else [])
                        pb.emit("vector", lambda E, head=head: E.reduce_max(
                            out=kn_sb[:, 2 * head:2 * head + 2],
                            in_=kn_ps[:].rearrange("p (b t) -> p b t", b=2),
                            axis=AX.X),
                            wait=[pb.at(s_mm)], inc=[s_ev])
            pb.emit("gpsimd", lambda E: E.memset(statpad[:], 0.0), inc=[s_gp])
            pb.emit("vector", lambda E: E.tensor_copy(
                out=statpad[0:1, 0:32], in_=kn_sb[:]),
                wait=[pb.at(s_ev), pb.at(s_gp)], inc=[s_ev])
            pb.dma("gpsimd", kv_in[2048:KVROWS, :], statpad[:],
                   wait=[pb.at(s_ev)], inc=[s_st])
        Q_V = pb.at(s_ev)
        Q_A = pb.at(s_act)

        with nc.sbuf_tensor("vbr_sb", [1, DM], F32) as vbr_sb, \
             nc.sbuf_tensor("vbb_sb", [128, DM], F32) as vbb_sb, \
             nc.sbuf_tensor("wv0", [128, FC, 512], BF16) as wv0, \
             nc.sbuf_tensor("wv1", [128, FC, 512], BF16) as wv1, \
             nc.sbuf_tensor("vstage", [128, 512], BF16) as vstage, \
             nc.psum_tensor("v_ps0", [128, 512], F32) as v_ps0, \
             nc.psum_tensor("v_ps1", [128, 512], F32) as v_ps1, \
             nc.psum_tensor("vb_ps", [128, 512], F32) as vb_ps:
            if qk_bias:
                pb.dma("sync", vbr_sb[:], vb_d, inc=[s_in])
                VBR = pb.at(s_in)
                for vc in range(2):
                    sl = slice(vc * 512, (vc + 1) * 512)
                    pb.emit("tensor", lambda E, sl=sl: E.matmul(
                        out=vb_ps[:], lhsT=onesr_f[:], rhs=vbr_sb[:, sl],
                        start=True, stop=True),
                        wait=[VBR, pb.at(s_ev), CONST], inc=[s_mm])
                    pb.emit("vector", lambda E, sl=sl: E.tensor_copy(
                        out=vbb_sb[:, sl], in_=vb_ps[:]),
                        wait=[pb.at(s_mm)], inc=[s_ev])
            wvs = [wv0, wv1]
            vpss = [v_ps0, v_ps1]
            for vc in range(2):
                wb = wvs[vc]
                pb.dma("gpsimd", wb[:],
                       wqkv[:, :, 2 * DM + vc * 512: 2 * DM + (vc + 1) * 512]
                       .rearrange("c p n -> p c n"),
                       wait=[Q_V, Q_A, pb.at(s_mm), pb.at(s_st), pb.at(s_kst)],
                       inc=[s_wv[vc]])
                WB = pb.at(s_wv[vc])
                for tt_ in range(16):
                    pp = vpss[tt_ % 2]
                    for mc in range(FC):
                        pb.emit("tensor", lambda E, wb=wb, mc=mc, tt_=tt_, pp=pp: E.matmul(
                            out=pp[:], lhsT=xn_sb[:, mc, tt_ * 128:(tt_ + 1) * 128],
                            rhs=wb[:, mc, :], start=(mc == 0), stop=(mc == FC - 1)),
                            wait=([WB, LN_V, pb.at(s_ev)] if mc == 0 else []),
                            inc=[s_mm] if mc == FC - 1 else [])
                    MM = pb.at(s_mm)
                    if qk_bias:
                        pb.emit("vector", lambda E, pp=pp, vc=vc: E.tensor_tensor(
                            out=vstage[:], in0=pp[:],
                            in1=vbb_sb[:, vc * 512:(vc + 1) * 512], op=ALU.add),
                            wait=[MM, pb.at(s_vst), pb.at(s_ev)], inc=[s_ev])
                    else:
                        pb.emit("vector", lambda E, pp=pp: E.tensor_copy(
                            out=vstage[:], in_=pp[:]),
                            wait=[MM, pb.at(s_vst), pb.at(s_ev)], inc=[s_ev])
                    dstv = kv_in[1024 + tt_ * 64: 1024 + (tt_ + 1) * 64, :] \
                        .rearrange("r (s d) -> (r s) d", s=2)[:, vc * 512:(vc + 1) * 512]
                    pb.dma("sync", dstv, vstage[:],
                           wait=[pb.at(s_ev), pb.at(s_vst)], inc=[s_vst])

        with nc.sbuf_tensor("qs_sb", [128, 128], BF16) as qs_sb, \
             nc.psum_tensor("qn_ps", [1, TOK], F32) as qn_ps:
            for bjj in range(16):
                b, jj = bjj // 8, bjj % 8
                cols = slice(b * 1024 + jj * 128, b * 1024 + (jj + 1) * 128)
                RED = pb.at(s_ev)
                for oc in range(FC):
                    pb.emit("scalar", lambda E, oc=oc, cols=cols: E.activation(
                        out=qs_sb[:], in_=q_sb[:, oc, cols], func=AF.Square),
                        wait=[Q_V, pb.at(s_mm), RED], inc=[s_act])
                    QSQ = pb.at(s_act)
                    for hh in range(HPC):
                        hsl = slice(hh * HD, (hh + 1) * HD)
                        head = oc * HPC + hh
                        pb.emit("tensor", lambda E, hsl=hsl, head=head: E.matmul(
                            out=qn_ps[:, head * 128:(head + 1) * 128],
                            lhsT=ones_bf[hsl, :], rhs=qs_sb[hsl, :],
                            start=True, stop=True),
                            wait=[QSQ, RED, CONST], inc=[s_mm])
                pb.emit("vector", lambda E, bjj=bjj: E.reduce_max(
                    out=qmn[:, bjj * 16: bjj * 16 + H],
                    in_=qn_ps[:, 0:H * 128].rearrange("p (h t) -> p h t", h=H),
                    axis=AX.X),
                    wait=[pb.at(s_mm)], inc=[s_ev])
            pb.emit("scalar", lambda E: E.activation(
                out=qmn[:], in_=qmn[:], func=AF.Sqrt),
                wait=[pb.at(s_ev)], inc=[s_act])
        QN_A = pb.at(s_act)
        phx.close()

        pb.emit("gpsimd", lambda E: E.collective_compute(
            "AllGather", ALU.bypass, replica_groups=[list(range(NCORES))],
            ins=[kv_in.opt()], outs=[kv_all.opt()]),
            wait=[pb.at(s_st), pb.at(s_kst), pb.at(s_vst), pb.at(s_gp)],
            inc=[s_ag], chain=False)
        AGD = pb.at(s_ag)

        if "dbg_kv" in dbg_t:
            with nc.sbuf_tensor("dkv", [128, TOK], F32) as dkv:
                for rr in range(16):
                    pb.dma("gpsimd", dkv[:], kv_in[rr * 128:(rr + 1) * 128, :],
                           wait=[AGD, pb.at(s_out)], inc=[s_gpd])
                    pb.dma("sync", dbg_t["dbg_kv"][rr * 128:(rr + 1) * 128, :],
                           dkv[:], wait=[pb.at(s_gpd)], inc=[s_out])
                pb.dma("gpsimd", dkv[0:1, 0:32], kv_in[2048:2049, 0:32],
                       wait=[pb.at(s_out)], inc=[s_gpd])
                pb.dma("sync", dbg_t["dbg_kv"][2048:2049, 0:32], dkv[0:1, 0:32],
                       wait=[pb.at(s_gpd)], inc=[s_out])
        if "dbg_q" in dbg_t:
            for fc in range(FC):
                pb.dma("gpsimd", dbg_t["dbg_q"][fc * 128:(fc + 1) * 128, :],
                       q_sb[:, fc, :], wait=[Q_V, Q_A], inc=[s_out])

        with nc.sbuf_tensor("kst", [1, NCORES, 32], F32) as kst, \
             nc.sbuf_tensor("crow", [1, 16], F32) as crow, \
             nc.psum_tensor("c_ps", [128, 16], F32) as c_ps:
            pb.dma("gpsimd", kst[:],
                   kv_all_r[:, 2048:2049, 0:32].rearrange("r k s -> k r s"),
                   wait=[AGD], inc=[s_gpd])
            pb.emit("vector", lambda E: E.reduce_max(
                out=kmaxrow[:], in_=kst[:].rearrange("p r s -> p s r"),
                axis=AX.X),
                wait=[pb.at(s_gpd)], inc=[s_ev])
            pb.emit("scalar", lambda E: E.activation(
                out=kmaxrow[:], in_=kmaxrow[:], func=AF.Sqrt),
                wait=[pb.at(s_ev)], inc=[s_act])
            KMX = pb.at(s_act)
            kmx_r = kmaxrow[:].rearrange("p (h b) -> p b h", b=2)
            for bjj in range(16):
                b = bjj // 8
                pb.emit("vector", lambda E, bjj=bjj, b=b: E.tensor_tensor(
                    out=crow[:, 0:H], in0=qmn[:, bjj * 16:bjj * 16 + H],
                    in1=kmx_r[:, b, 0:H], op=ALU.mult),
                    wait=[KMX, QN_A, pb.at(s_mm)], inc=[s_ev])
                pb.emit("vector", lambda E: E.tensor_scalar_mul(
                    crow[:, 0:H], crow[:, 0:H], scale),
                    wait=[pb.at(s_ev)], inc=[s_ev])
                CRW = pb.at(s_ev)
                pb.emit("tensor", lambda E: E.matmul(
                    out=c_ps[:, 0:H], lhsT=nonesr_f[:], rhs=crow[:, 0:H],
                    start=True, stop=True), wait=[CRW, CONST], inc=[s_mm])
                pb.emit("vector", lambda E, bjj=bjj: E.tensor_copy(
                    out=c_all[:, bjj, 0:H], in_=c_ps[:, 0:H]),
                    wait=[pb.at(s_mm)], inc=[s_ev])
        C_V = pb.at(s_ev)
        if "dbg_c" in dbg_t:
            pb.dma("sync", dbg_t["dbg_c"], c_all[:], wait=[C_V], inc=[s_out])

        base_d = zout if two else xin
        ph5 = ExitStack()
        maskt_sb = ph5.enter_context(nc.sbuf_tensor("maskt_sb", [128, 8, 128], BF16))
        wo_sb = ph5.enter_context(nc.sbuf_tensor("wo_sb", [128, FC, DM], BF16))
        rl_sb = ph5.enter_context(nc.sbuf_tensor("rl_sb", [128, 1024], F32))
        otn = ph5.enter_context(nc.sbuf_tensor("otn", [128, FC, 512], BF16))
        bas0 = ph5.enter_context(nc.sbuf_tensor("bas0", [128, 512], F32))
        bas1 = ph5.enter_context(nc.sbuf_tensor("bas1", [128, 512], F32))
        ktile0 = ph5.enter_context(nc.sbuf_tensor("ktile0", [128, FC, 1024], BF16))
        ktile1 = ph5.enter_context(nc.sbuf_tensor("ktile1", [128, FC, 1024], BF16))
        vtile0 = ph5.enter_context(nc.sbuf_tensor("vtile0", [128, 8, 1024], BF16))
        vtile1 = ph5.enter_context(nc.sbuf_tensor("vtile1", [128, 8, 1024], BF16))
        pexp = [ph5.enter_context(nc.sbuf_tensor("pexp%d" % i, [128, 1024], BF16))
                for i in range(4)]
        allones = ph5.enter_context(nc.sbuf_tensor("allones", [128, 128], BF16))
        mhalf1 = ph5.enter_context(nc.sbuf_tensor("mhalf1", [128, 128], BF16))
        mhalf2 = ph5.enter_context(nc.sbuf_tensor("mhalf2", [128, 128], BF16))
        o_sb = ph5.enter_context(nc.sbuf_tensor("o_sb", [128, NSUB, 2048], F32))
        st_ps0 = ph5.enter_context(nc.psum_tensor("st_ps0", [128, 1024], F32))
        st_ps1 = ph5.enter_context(nc.psum_tensor("st_ps1", [128, 1024], F32))
        ot_ps = ph5.enter_context(nc.psum_tensor("ot_ps", [128, 2048], F32))
        if True:
            stpss = [st_ps0, st_ps1]
            op_ps = st_ps0[:, 0:512]
            P2D = [AGD, QN_A, pb.at(s_mm), pb.at(s_ev)]
            pb.dma("sync", maskt_sb[:], maskd, wait=P2D, inc=[s_mk])
            pb.dma("gpsimd", wo_sb[:], wo_d.rearrange("c p n -> p c n"),
                   wait=P2D, inc=[s_w])
            pb.emit("gpsimd", lambda E: E.memset(allones[:], 1.0))
            pb.emit("gpsimd", lambda E: E.memset(mhalf1[:], 1.0))
            pb.emit("gpsimd", lambda E: E.affine_select(
                out=mhalf1[:], in_=mhalf1[:], pattern=[[-1, 128]],
                compare_op=ALU.is_ge, fill=0.0, base=63, channel_multiplier=0))
            pb.emit("gpsimd", lambda E: E.memset(mhalf2[:], 1.0))
            pb.emit("gpsimd", lambda E: E.affine_select(
                out=mhalf2[:], in_=mhalf2[:], pattern=[[1, 128]],
                compare_op=ALU.is_ge, fill=0.0, base=-64, channel_multiplier=0),
                inc=[s_gp])
            IN5 = pb.at(s_mk)
            WO_L = pb.at(s_w)
            CONST5 = pb.at(s_gp)
            ktiles = [ktile0, ktile1]
            vtiles = [vtile0, vtile1]
            FL_tile_done = {}
            exp_war = [CONST5] * 4
            st_war = [CONST5, CONST5]
            ablate = cfg.get("ablate", "")
            merge_prev = pb.at(s_fe)
            fin_done = pb.at(s_fe)
            opd_prev = pb.at(s_fe)
            if ablate == "exp":
                for i in range(4):
                    pb.emit("gpsimd", lambda E, i=i: E.memset(pexp[i][:], 0.001),
                            inc=[s_gp] if i == 3 else [])
            for b in range(B):
                pb.emit("gpsimd", lambda E: E.memset(o_sb[:], 0.0),
                        wait=[fin_done] if b else [AGD], inc=[s_gp])
                ZRO = pb.at(s_gp)
                for j in range(NSUB):
                    par = j % 2
                    kt = ktiles[par]
                    vt = vtiles[par]
                    key = (b, j - 2) if j >= 2 else ((b - 1, j + NSUB - 2) if b else None)
                    wkt = [FL_tile_done[key]] if key in FL_tile_done else [AGD]
                    bc0 = b * 1024 + j * 128
                    for fc in range(FC):
                        pb.dma("sync",
                               kt[:, fc, :].rearrange("p (r t) -> p r t", r=NCORES),
                               kv_all_r[:, fc * 128:(fc + 1) * 128, bc0:bc0 + 128]
                               .rearrange("r k t -> k r t"),
                               wait=wkt if fc == 0 else [], inc=[s_ktp[par]])
                    vrow = 1024 + bc0 // 2
                    for u in range(NCORES):
                        pb.dma("sync", vt[:, u, :],
                               kv_all_r[u, vrow:vrow + 64, :]
                               .rearrange("k (s d) -> (k s) d", s=2),
                               wait=wkt if u == 0 else [], inc=[s_ktp[par]])
                    KT = pb.at(s_ktp[par])
                    for jj in (() if ablate == "flash" else range(j, NSUB)):
                        bjj = b * 8 + jj
                        qcols = slice(b * 1024 + jj * 128, b * 1024 + (jj + 1) * 128)
                        diag = (jj == j) and ablate != "mask"
                        first_ot = True
                        for dc in range(FC):
                            for hh in range(HPC):
                                h = dc * HPC + hh
                                hsl = slice(hh * HD, (hh + 1) * HD)
                                sp = stpss[h % 2]
                                px = pexp[h % 4]
                                for u in range(8):
                                    last = (u == 7) and not diag
                                    pb.emit("tensor", lambda E, sp=sp, kt=kt, u=u,
                                            oc=dc, hsl=hsl, qcols=qcols,
                                            diag=diag: E.matmul(
                                        out=sp[:, u * 128:(u + 1) * 128],
                                        lhsT=kt[hsl, oc, u * 128:(u + 1) * 128],
                                        rhs=q_sb[hsl, oc, qcols],
                                        start=True, stop=not diag),
                                        wait=([KT, Q_V, Q_A, st_war[h % 2]]
                                              if u == 0 else []),
                                        inc=[s_fl] if last else [])
                                    if diag:
                                        pb.emit("tensor", lambda E, sp=sp, u=u: E.matmul(
                                            out=sp[:, u * 128:(u + 1) * 128],
                                            lhsT=ident[:],
                                            rhs=maskt_sb[:, u, :],
                                            start=False, stop=True),
                                            wait=[IN5, CONST] if u == 0 else [],
                                            inc=[s_fl] if u == 7 else [])
                                SMM = pb.at(s_fl)
                                if ablate == "exp":
                                    pb.emit("scalar", lambda E, px=px: E.activation(
                                        out=px[:, 0:1], in_=c_all[:, 0, 0:1],
                                        func=AF.Exp),
                                        wait=[SMM, C_V, exp_war[h % 4]], inc=[s_fa])
                                else:
                                    pb.emit("scalar", lambda E, sp=sp, px=px, bjj=bjj,
                                            h=h: E.activation(
                                        out=px[:], in_=sp[:], func=AF.Exp, scale=scale,
                                        bias=c_all[:, bjj, h:h + 1]),
                                        wait=[SMM, C_V, exp_war[h % 4]],
                                        inc=[s_fa])
                                st_war[h % 2] = pb.at(s_fa)
                            EXPD = pb.at(s_fa)
                            for hh in (() if ablate == "ot" else range(HPC)):
                                h = dc * HPC + hh
                                px = pexp[h % 4]
                                osl = slice(hh * HD, (hh + 1) * HD) if HPC == 2 \
                                    else slice(0, 128)
                                for u in range(8):
                                    pb.emit("tensor", lambda E, vt=vt, u=u, dc=dc,
                                            px=px, osl=osl, hh=hh: E.matmul(
                                        out=ot_ps[osl, dc * 128:(dc + 1) * 128],
                                        lhsT=vt[:, u, dc * 128 + osl.start:
                                                dc * 128 + osl.stop],
                                        rhs=px[:, u * 128:(u + 1) * 128],
                                        start=(u == 0), stop=(u == 7)),
                                        wait=([EXPD, KT] +
                                              ([merge_prev] if first_ot else []))
                                        if u == 0 else [])
                                    first_ot = False
                            for hh in (() if ablate == "l" else range(HPC)):
                                h = dc * HPC + hh
                                px = pexp[h % 4]
                                lsel = allones if HPC == 1 else \
                                    (mhalf1 if hh == 0 else mhalf2)
                                for u in range(8):
                                    last = (hh == HPC - 1 and u == 7)
                                    pb.emit("tensor", lambda E, px=px, u=u, dc=dc,
                                            lsel=lsel, hh=hh: E.matmul(
                                        out=ot_ps[:, (FC + dc) * 128:
                                                  (FC + dc + 1) * 128],
                                        lhsT=lsel[:], rhs=px[:, u * 128:(u + 1) * 128],
                                        start=(hh == 0 and u == 0),
                                        stop=(hh == HPC - 1 and u == 7)),
                                        wait=[CONST5] if (hh == 0 and u == 0) else [],
                                        inc=[s_fl] if last else [])
                            exp_war[(dc * HPC) % 4] = pb.at(s_fl)
                            if HPC == 2:
                                exp_war[(dc * HPC + 1) % 4] = pb.at(s_fl)
                        OTD = pb.at(s_fl)
                        pb.emit("vector", lambda E, jj=jj: E.tensor_tensor(
                            out=o_sb[:, jj, :], in0=o_sb[:, jj, :], in1=ot_ps[:],
                            op=ALU.add),
                            wait=[OTD, ZRO, pb.at(s_fe)], inc=[s_fe])
                        merge_prev = pb.at(s_fe)
                    FL_tile_done[(b, j)] = pb.at(s_fe)
                FLB = pb.at(s_fe)
                if "dbg_o" in dbg_t:
                    pb.dma("sync", dbg_t["dbg_o"][:, b * 8:(b + 1) * 8, :],
                           o_sb[:, :, 0:1024], wait=[FLB], inc=[s_out])
                if "dbg_l" in dbg_t:
                    pb.dma("sync", dbg_t["dbg_l"][:, b * 8:(b + 1) * 8, :],
                           o_sb[:, :, 1024:1024 + 16].rearrange("p s t -> p s t")
                           if False else o_sb[:, :, 1024:1040], wait=[FLB],
                           inc=[s_out])
                resbufs = [bas0, bas1]
                for g in range(2):
                    bcols = slice(b * 1024 + g * 512, b * 1024 + (g + 1) * 512)
                    for jji in range(4):
                        jj = g * 4 + jji
                        if ablate in ("l", "rl"):
                            pb.emit("gpsimd", lambda E: E.memset(rl_sb[:], 1.0),
                                    wait=[FLB], inc=[s_fa])
                        else:
                            pb.emit("scalar", lambda E, jj=jj: E.activation(
                                out=rl_sb[:], in_=o_sb[:, jj, 1024:2048],
                                func=AF.Ln),
                                wait=[FLB, pb.at(s_fa), pb.at(s_fe), opd_prev])
                            pb.emit("scalar", lambda E: E.activation(
                                out=rl_sb[:], in_=rl_sb[:], func=AF.Exp,
                                scale=-1.0), inc=[s_fa])
                        pb.emit("vector", lambda E, jj=jj, jji=jji: E.tensor_tensor(
                            out=otn[:, :, jji * 128:(jji + 1) * 128],
                            in0=o_sb[:, jj, 0:1024]
                            .rearrange("p (c n) -> p c n", c=FC),
                            in1=rl_sb[:].rearrange("p (c n) -> p c n", c=FC),
                            op=ALU.mult),
                            wait=[pb.at(s_fa), pb.at(s_fe)], inc=[s_fe])
                    OTN = pb.at(s_fe)
                    for mc in range(FC):
                        bas = resbufs[mc % 2]
                        pb.dma("sync", bas[:], base_d[mc * 128:(mc + 1) * 128, bcols],
                               wait=[pb.at(s_fe), pb.at(s_so[mc % 2])],
                               inc=[s_bas[mc % 2]])
                        BAS = pb.at(s_bas[mc % 2])
                        for dc in range(FC):
                            pb.emit("tensor", lambda E, dc=dc, mc=mc: E.matmul(
                                out=op_ps, lhsT=wo_sb[:, dc, mc * 128:(mc + 1) * 128],
                                rhs=otn[:, dc, :], start=(dc == 0), stop=(dc == FC - 1)),
                                wait=([OTN, WO_L, pb.at(s_fe), st_war[0]]
                                      if dc == 0 else []),
                                inc=[s_fl] if dc == FC - 1 else [])
                        OPD = pb.at(s_fl)
                        if o_bias:
                            pb.emit("vector", lambda E, bas=bas, mc=mc: E.scalar_tensor_tensor(
                                out=bas[:], in0=op_ps, scalar=ob_sb[:, mc:mc + 1],
                                in1=bas[:], op0=ALU.add, op1=ALU.add),
                                wait=[OPD, BAS, IN0], inc=[s_fe])
                        else:
                            pb.emit("vector", lambda E, bas=bas: E.tensor_tensor(
                                out=bas[:], in0=bas[:], in1=op_ps, op=ALU.add),
                                wait=[OPD, BAS], inc=[s_fe])
                        pb.dma("sync", xout[mc * 128:(mc + 1) * 128, bcols], bas[:],
                               wait=[pb.at(s_fe)], inc=[s_so[mc % 2]])
                    opd_prev = pb.at(s_fe)
                    st_war[0] = pb.at(s_fe)
                fin_done = pb.at(s_fe)
        ph5.close()
        fin_waits = [pb.at(s_out), pb.at(s_st), pb.at(s_so[0]), pb.at(s_so[1]),
                     pb.at(s_zex)]
        pb.emit("sync", lambda E: E.nop(), wait=fin_waits)
        pb.realize()
    return nc



def prep_attn_inputs(cfg, weights, x_parts):
    H, HD = cfg["H"], cfg["HD"]
    rope = cfg["rope"]
    perm = qk_perm(HD, rope) if rope else np.arange(DM)
    wq = np.asarray(weights["wq"], dtype=np.float32)
    wk = np.asarray(weights["wk"], dtype=np.float32)
    wv = np.asarray(weights["wv"], dtype=np.float32)
    wo = np.asarray(weights["wo"], dtype=np.float32)
    wqkv = np.concatenate([wq[:, perm], wk[:, perm], wv], axis=1)
    wqkv_c = chunked_w(wqkv)
    wo_c = chunked_w(wo)
    in_maps = []
    for core in range(NCORES):
        m = {"xin": x_parts[core], "wqkv": wqkv_c, "wo": wo_c,
             "maskin": make_causal_mask(core)}
        if rope:
            cs1, cs2 = rope_tables(core, HD, rope)
            m["cossin1"] = cs1
            m["cossin2"] = cs2
        if cfg["qk_bias"]:
            bq = np.asarray(weights["bqkv"], dtype=np.float32)
            qkbias = np.concatenate([bq[0:DM][perm], bq[DM:2 * DM][perm]])
            m["qkbias"] = colsplit(qkbias)
            m["vbias"] = np.ascontiguousarray(bq[2 * DM:].reshape(1, DM))
        if cfg["o_bias"]:
            m["obias"] = colsplit(np.asarray(weights["bo"], dtype=np.float32))
        if cfg["two_norms"]:
            m["gain"] = colsplit(np.asarray(weights["gain"], dtype=np.float32))
        in_maps.append(m)
    return in_maps





def build_moe(cfg):
    kind = cfg["kind"]
    DFF = cfg["dff"]
    NE = 8
    CAP = 1280.0
    act_f = AF.Gelu if cfg["act"] == "gelu" else AF.Relu
    biases = cfg["biases"]
    rms = cfg["rms"]
    xres_f = cfg["xres"]
    NFB = DFF // 1024
    FBC = 8

    nc = bass.Bass("TRN2", target_bir_lowering=False)
    xin = nc.dram_tensor("xin", [DM, TOK], F32, kind="ExternalInput").ap()
    w1_d = nc.dram_tensor("w1", [NE, FC, 128, DFF], F32, kind="ExternalInput").ap()
    w2_d = nc.dram_tensor("w2", [NE, DFF // 128, 128, DM], F32,
                          kind="ExternalInput").ap()
    wr_d = nc.dram_tensor("wr", [FC, 128, NE], F32, kind="ExternalInput").ap()
    xout = nc.dram_tensor("xout", [DM, TOK], F32, kind="ExternalOutput").ap()
    if xres_f:
        xres = nc.dram_tensor("xres", [DM, TOK], F32, kind="ExternalInput").ap()
    if biases:
        b1_d = nc.dram_tensor("b1", [NE, 128, DFF // 128], F32,
                              kind="ExternalInput").ap()
        b2_d = nc.dram_tensor("b2", [NE, 128, FC], F32, kind="ExternalInput").ap()
    if rms:
        gain_d = nc.dram_tensor("gain", [128, FC], F32, kind="ExternalInput").ap()
    if kind == "cap2":
        g_loc = nc.dram_tensor("g_loc", [2 * TOK, NE], F32, kind="Internal").ap()
        g_all = nc.dram_tensor("g_all", [NCORES * 2 * TOK, NE], F32,
                               kind="Internal").ap()
    wrow_d = nc.dram_tensor("wrow_d", [NE, TOK], F32, kind="Internal").ap()

    dbg = set(cfg.get("debug", ()))
    dbg_t = {}
    def dbg_mk(name, shape):
        dbg_t[name] = nc.dram_tensor(name, shape, F32, kind="ExternalOutput").ap()
    if "w" in dbg: dbg_mk("dbg_w", [128, 16, NE])
    if "th" in dbg: dbg_mk("dbg_th", [1, NE])

    ctx = ExitStack()
    with ctx:
        pb = PB(nc, ctx)
        pb.init_chains()
        sbt = lambda n, s, d: ctx.enter_context(nc.sbuf_tensor(n, s, d))

        xn_sb = sbt("xn_sb", [128, FC, TOK], BF16)
        y_acc = sbt("y_acc", [128, FC, TOK], F32)
        wt_sb = sbt("wt_sb", [128, 16, NE], F32)
        g2w_sb = sbt("g2w_sb", [128, 16, NE], F32)
        wr_sb = sbt("wr_sb", [128, FC, NE], F32)
        rr_sb = sbt("rr_sb", [1, TOK], F32)
        lg_sb = sbt("lg_sb", [8, TOK], F32)
        ones_f = sbt("ones_f", [128, 1], F32)
        onesr_f = sbt("onesr_f", [1, 128], F32)
        ident_f = sbt("ident_f", [128, 128], F32)
        eps_c = sbt("eps_c", [1, 1], F32)
        grow_sb = sbt("grow_sb", [1, TOK], F32)
        if biases:
            b1_sb = sbt("b1_sb", [128, NE, DFF // 128], F32)
            b2_sb = sbt("b2_sb", [128, NE, FC], F32)
        if rms:
            g_sb = sbt("g_sb", [128, FC], F32)

        s_in = pb.sem("s_in")
        s_gp = pb.sem("s_gp")
        s_ln = pb.sem("s_ln")
        s_lnmm = pb.sem("s_lnmm")
        s_lnact = pb.sem("s_lnact")
        s_xt = [pb.sem("s_xt0"), pb.sem("s_xt1")]
        s_mm = pb.sem("s_mm")
        s_ev = pb.sem("s_ev")
        s_act = pb.sem("s_act")
        s_wu = [pb.sem("s_wu0"), pb.sem("s_wu1")]
        s_wd2 = [pb.sem("s_wd0"), pb.sem("s_wd1")]
        s_ag = pb.sem("s_ag")
        s_gd = pb.sem("s_gd")
        s_wr = pb.sem("s_wr")
        s_bas = [pb.sem("s_bas0"), pb.sem("s_bas1")]
        s_so = [pb.sem("s_so0"), pb.sem("s_so1")]
        s_out = pb.sem("s_out")

        pb.emit("gpsimd", lambda E: E.memset(ones_f[:], 1.0))
        pb.emit("gpsimd", lambda E: E.memset(onesr_f[:], 1.0))
        pb.emit("gpsimd", lambda E: E.memset(
            eps_c[:], 1e-6 if rms else 1e-5))
        pb.emit("gpsimd", lambda E: E.memset(ident_f[:], 1.0))
        pb.emit("gpsimd", lambda E: E.affine_select(
            out=ident_f[:], in_=ident_f[:], pattern=[[-1, 128]],
            compare_op=ALU.is_equal, fill=0.0, base=0, channel_multiplier=1))
        pb.emit("gpsimd", lambda E: E.memset(y_acc[:], 0.0), inc=[s_gp])
        CONST = pb.at(s_gp)
        pb.dma("sync", wr_sb[:], wr_d.rearrange("c p n -> p c n"), inc=[s_in])
        if biases:
            pb.dma("sync", b1_sb[:], b1_d.rearrange("e p n -> p e n"), inc=[s_in])
            pb.dma("sync", b2_sb[:], b2_d.rearrange("e p n -> p e n"), inc=[s_in])
        if rms:
            pb.dma("sync", g_sb[:], gain_d, inc=[s_in])
        IN0 = pb.at(s_in)
        if rms:
            for mc in range(FC):
                pb.emit("vector", lambda E, mc=mc: E.tensor_scalar_mul(
                    wr_sb[:, mc, :], wr_sb[:, mc, :], g_sb[:, mc:mc + 1]),
                    wait=[IN0], inc=[s_ev])
        WRG = pb.at(s_ev)

        with nc.sbuf_tensor("xs0", [128, 512], F32) as xs0, \
             nc.sbuf_tensor("xs1", [128, 512], F32) as xs1, \
             nc.sbuf_tensor("xq0", [128, 512], F32) as xq0, \
             nc.sbuf_tensor("xq1", [128, 512], F32) as xq1, \
             nc.sbuf_tensor("srow", [1, TOK], F32) as srow, \
             nc.sbuf_tensor("mrow", [1, TOK], F32) as mrow, \
             nc.sbuf_tensor("mb", [128, 512], F32) as mb, \
             nc.sbuf_tensor("rb", [128, 512], F32) as rb:
            xss = [xs0, xs1]
            xqs = [xq0, xq1]
            with nc.psum_tensor("mean_ps", [1, TOK], F32) as mean_ps, \
                 nc.psum_tensor("sq_ps", [1, TOK], F32) as sq_ps:
                cons_mm = {}
                cons_act = {}
                for idx in range(32):
                    fc, tc = idx // 4, idx % 4
                    tl = xss[idx % 2]
                    xq = xqs[idx % 2]
                    if idx >= 2:
                        w = [(s_lnmm, cons_mm[idx - 2]),
                             (s_lnact, cons_act[idx - 2])]
                    else:
                        w = []
                    pb.dma("sync", tl[:],
                           xin[fc * 128:(fc + 1) * 128, tc * 512:(tc + 1) * 512],
                           wait=w, inc=[s_xt[idx % 2]])
                    XL = pb.at(s_xt[idx % 2])
                    if not rms:
                        pb.emit("tensor", lambda E, tl=tl, fc=fc, tc=tc: E.matmul(
                            out=mean_ps[:, tc * 512:(tc + 1) * 512],
                            lhsT=ones_f[:], rhs=tl[:],
                            start=(fc == 0), stop=(fc == FC - 1)),
                            wait=[XL, CONST], inc=[s_lnmm])
                    pb.emit("scalar", lambda E, tl=tl, xq=xq: E.activation(
                        out=xq[:], in_=tl[:], func=AF.Square),
                        wait=[XL] + ([(s_lnmm, cons_mm[idx - 2])] if idx >= 2 else []),
                        inc=[s_lnact])
                    pb.emit("tensor", lambda E, xq=xq, fc=fc, tc=tc: E.matmul(
                        out=sq_ps[:, tc * 512:(tc + 1) * 512],
                        lhsT=ones_f[:], rhs=xq[:],
                        start=(fc == 0), stop=(fc == FC - 1)),
                        wait=[pb.at(s_lnact), CONST], inc=[s_lnmm])
                    cons_mm[idx] = s_lnmm.n
                    cons_act[idx] = s_lnact.n
                if not rms:
                    pb.emit("vector", lambda E: E.tensor_scalar_mul(
                        mrow[:], mean_ps[:], 1.0 / DM),
                        wait=[pb.at(s_lnmm)], inc=[s_ln])
                    pb.emit("vector", lambda E: E.tensor_tensor(
                        out=srow[:], in0=mrow[:], in1=mrow[:], op=ALU.mult),
                        wait=[pb.at(s_ln)], inc=[s_ln])
                    pb.emit("vector", lambda E: E.scalar_tensor_tensor(
                        out=srow[:], in0=sq_ps[:], scalar=1.0 / DM, in1=srow[:],
                        op0=ALU.mult, op1=ALU.subtract),
                        wait=[pb.at(s_ln)], inc=[s_ln])
                else:
                    pb.emit("vector", lambda E: E.tensor_scalar_mul(
                        srow[:], sq_ps[:], 1.0 / DM),
                        wait=[pb.at(s_lnmm)], inc=[s_ln])
                pb.emit("scalar", lambda E: E.activation(
                    out=rr_sb[:], in_=srow[:], func=AF.Sqrt, bias=eps_c[:]),
                    wait=[pb.at(s_ln), CONST], inc=[s_lnact])
                pb.emit("scalar", lambda E: E.activation(
                    out=rr_sb[:], in_=rr_sb[:], func=AF.Ln),
                    wait=[pb.at(s_lnact)])
                pb.emit("scalar", lambda E: E.activation(
                    out=rr_sb[:], in_=rr_sb[:], func=AF.Exp, scale=-1.0),
                    inc=[s_lnact])
                pb.emit("vector", lambda E: E.engine_nop(),
                        wait=[pb.at(s_lnact)], inc=[s_ln])
            with nc.psum_tensor("bc1_ps", [128, 512], F32) as bc1_ps, \
                 nc.psum_tensor("bc2_ps", [128, 512], F32) as bc2_ps, \
                 nc.psum_tensor("lg_ps", [8, TOK], F32) as lg_ps:
                RR = pb.at(s_ln)
                rt_mm = {}
                for tc in range(4):
                    sl = slice(tc * 512, (tc + 1) * 512)
                    pb.emit("tensor", lambda E, sl=sl: E.matmul(
                        out=bc1_ps[:], lhsT=onesr_f[:], rhs=rr_sb[:, sl],
                        start=True, stop=True),
                        wait=[RR, pb.at(s_ln), CONST], inc=[s_lnmm])
                    pb.emit("vector", lambda E: E.tensor_copy(
                        out=rb[:], in_=bc1_ps[:]),
                        wait=[pb.at(s_lnmm), pb.at(s_ln)], inc=[s_ln])
                    if not rms:
                        pb.emit("tensor", lambda E, sl=sl: E.matmul(
                            out=bc2_ps[:], lhsT=onesr_f[:], rhs=mrow[:, sl],
                            start=True, stop=True),
                            wait=[pb.at(s_ln)], inc=[s_lnmm])
                        pb.emit("vector", lambda E: E.tensor_copy(
                            out=mb[:], in_=bc2_ps[:]),
                            wait=[pb.at(s_lnmm)], inc=[s_ln])
                    BC = pb.at(s_ln)
                    for fc in range(FC):
                        idx = tc * 8 + fc
                        tl = xss[idx % 2]
                        xq = xqs[idx % 2]
                        rtw = ([(s_lnmm, rt_mm[idx - 2])] if idx >= 2 else [])
                        pb.dma("sync", tl[:], xin[fc * 128:(fc + 1) * 128, sl],
                               wait=[pb.at(s_ln), pb.at(s_xt[idx % 2])] +
                               (rtw if rms else []),
                               inc=[s_xt[idx % 2]])
                        XL = pb.at(s_xt[idx % 2])
                        if not rms:
                            pb.emit("vector", lambda E, tl=tl, xq=xq: E.tensor_tensor(
                                out=xq[:], in0=tl[:], in1=mb[:], op=ALU.subtract),
                                wait=[XL, BC, pb.at(s_ln)] + rtw, inc=[s_ln])
                            srct = xq
                            SR = pb.at(s_ln)
                        else:
                            srct = tl
                            SR = XL
                        pb.emit("tensor", lambda E, st=srct, fc=fc, sl=sl: E.matmul(
                            out=lg_ps[:, sl], lhsT=wr_sb[:, fc, :], rhs=st[:],
                            start=(fc == 0), stop=(fc == FC - 1)),
                            wait=[SR, WRG, IN0], inc=[s_lnmm])
                        rt_mm[idx] = s_lnmm.n
                        if rms:
                            pb.emit("vector", lambda E, st=srct, xq=xq, fc=fc,
                                    sl=sl: E.tensor_tensor(
                                out=xq[:], in0=st[:], in1=rb[:], op=ALU.mult),
                                wait=[pb.at(s_ln), BC, XL], inc=[s_ln])
                            pb.emit("vector", lambda E, xq=xq, fc=fc, sl=sl:
                                    E.tensor_scalar_mul(
                                xn_sb[:, fc, sl], xq[:], g_sb[:, fc:fc + 1]),
                                wait=[pb.at(s_ln), IN0], inc=[s_ln])
                        else:
                            pb.emit("vector", lambda E, st=srct, fc=fc, sl=sl:
                                    E.tensor_tensor(
                                out=xn_sb[:, fc, sl], in0=st[:], in1=rb[:],
                                op=ALU.mult),
                                wait=[pb.at(s_ln), BC, XL], inc=[s_ln])
                    pb.emit("tensor", lambda E, sl=sl: E.matmul(
                        out=bc2_ps[0:8, :], lhsT=onesr_f[:, 0:8], rhs=rr_sb[:, sl],
                        start=True, stop=True),
                        wait=[pb.at(s_lnmm), pb.at(s_ln)], inc=[s_lnmm])
                    pb.emit("vector", lambda E, sl=sl: E.tensor_copy(
                        out=lg_sb[:, sl], in_=lg_ps[:, sl]),
                        wait=[pb.at(s_lnmm)], inc=[s_ln])
                    pb.emit("vector", lambda E, sl=sl: E.tensor_tensor(
                        out=lg_sb[:, sl], in0=lg_sb[:, sl], in1=bc2_ps[0:8, :],
                        op=ALU.mult),
                        wait=[pb.at(s_ln), pb.at(s_lnmm)], inc=[s_ln])
        LN_V = pb.at(s_ln)

        with nc.sbuf_tensor("lgt", [128, 16, NE], F32) as lgt, \
             nc.sbuf_tensor("mx", [128, 16], F32) as mx, \
             nc.sbuf_tensor("sm", [128, 16], F32) as sm, \
             nc.sbuf_tensor("oh", [128, 16, NE], F32) as oh, \
             nc.sbuf_tensor("pm", [128, 16, NE], F32) as pm, \
             nc.psum_tensor("tp_ps", [128, 16 * NE], F32) as tp_ps:
            for g in range(16):
                pb.emit("tensor", lambda E, g=g: E.transpose(
                    out=tp_ps[:, g * NE:(g + 1) * NE],
                    in_=lg_sb[0:NE, g * 128:(g + 1) * 128],
                    identity=ident_f[0:NE, 0:NE]),
                    wait=[LN_V, CONST] if g == 0 else [],
                    inc=[s_mm] if g == 15 else [])
            pb.emit("vector", lambda E: E.tensor_copy(out=lgt[:], in_=tp_ps[:]
                    .rearrange("p (g e) -> p g e", e=NE)),
                    wait=[pb.at(s_mm)], inc=[s_ev])
            pb.emit("vector", lambda E: E.reduce_max(
                out=mx[:], in_=lgt[:], axis=AX.X),
                wait=[pb.at(s_ev)], inc=[s_ev])
            pb.emit("vector", lambda E: E.tensor_tensor(
                out=lgt[:], in0=lgt[:],
                in1=mx[:].unsqueeze(2).to_broadcast([128, 16, NE]),
                op=ALU.subtract),
                wait=[pb.at(s_ev)], inc=[s_ev])
            pb.emit("scalar", lambda E: E.activation(
                out=lgt[:], in_=lgt[:], func=AF.Exp),
                wait=[pb.at(s_ev)], inc=[s_act])
            pb.emit("vector", lambda E: E.reduce_sum(
                out=sm[:], in_=lgt[:], axis=AX.X),
                wait=[pb.at(s_act)], inc=[s_ev])
            pb.emit("scalar", lambda E: E.activation(
                out=sm[:], in_=sm[:], func=AF.Ln),
                wait=[pb.at(s_ev)])
            pb.emit("scalar", lambda E: E.activation(
                out=sm[:], in_=sm[:], func=AF.Exp, scale=-1.0), inc=[s_act])
            pb.emit("vector", lambda E: E.tensor_tensor(
                out=lgt[:], in0=lgt[:],
                in1=sm[:].unsqueeze(2).to_broadcast([128, 16, NE]), op=ALU.mult),
                wait=[pb.at(s_act), pb.at(s_ev)], inc=[s_ev])
            pb.emit("vector", lambda E: E.reduce_max(
                out=mx[:], in_=lgt[:], axis=AX.X),
                wait=[pb.at(s_ev)], inc=[s_ev])
            pb.emit("vector", lambda E: E.tensor_tensor(
                out=oh[:], in0=lgt[:],
                in1=mx[:].unsqueeze(2).to_broadcast([128, 16, NE]),
                op=ALU.is_ge),
                wait=[pb.at(s_ev)], inc=[s_ev])
            pb.emit("vector", lambda E: E.tensor_tensor(
                out=wt_sb[:], in0=oh[:],
                in1=mx[:].unsqueeze(2).to_broadcast([128, 16, NE]), op=ALU.mult),
                wait=[pb.at(s_ev)], inc=[s_ev])
            if kind == "cap2":
                pb.emit("vector", lambda E: E.tensor_scalar(
                    out=pm[:], in0=oh[:], scalar1=-1e9, scalar2=None,
                    op0=ALU.mult),
                    wait=[pb.at(s_ev)], inc=[s_ev])
                pb.emit("vector", lambda E: E.tensor_tensor(
                    out=pm[:], in0=pm[:], in1=lgt[:], op=ALU.add),
                    wait=[pb.at(s_ev)], inc=[s_ev])
                pb.emit("vector", lambda E: E.reduce_max(
                    out=mx[:], in_=pm[:], axis=AX.X),
                    wait=[pb.at(s_ev)], inc=[s_ev])
                pb.emit("vector", lambda E: E.tensor_tensor(
                    out=g2w_sb[:], in0=pm[:],
                    in1=mx[:].unsqueeze(2).to_broadcast([128, 16, NE]),
                    op=ALU.is_ge),
                    wait=[pb.at(s_ev)], inc=[s_ev])
                pb.emit("vector", lambda E: E.tensor_tensor(
                    out=g2w_sb[:], in0=g2w_sb[:],
                    in1=mx[:].unsqueeze(2).to_broadcast([128, 16, NE]),
                    op=ALU.mult),
                    wait=[pb.at(s_ev)], inc=[s_ev])
                pb.dma("sync", g_loc[0:TOK, :]
                       .rearrange("(g p) e -> p g e", p=128), wt_sb[:],
                       wait=[pb.at(s_ev)], inc=[s_gd])
                pb.dma("sync", g_loc[TOK:2 * TOK, :]
                       .rearrange("(g p) e -> p g e", p=128), g2w_sb[:],
                       wait=[pb.at(s_ev)], inc=[s_gd])
                pb.emit("gpsimd", lambda E: E.collective_compute(
                    "AllGather", ALU.bypass, replica_groups=[list(range(NCORES))],
                    ins=[g_loc.opt()], outs=[g_all.opt()]),
                    wait=[pb.at(s_gd)], inc=[s_ag], chain=False)
        if kind == "cap2":
            AGD = pb.at(s_ag)
            with nc.sbuf_tensor("cnt_sb", [128, 256, NE], F32) as cnt_sb, \
                 nc.sbuf_tensor("ind", [128, 256, NE], BF16) as ind, \
                 nc.sbuf_tensor("red", [128, NE], F32) as red, \
                 nc.sbuf_tensor("lo", [1, NE], F32) as lo, \
                 nc.sbuf_tensor("hi", [1, NE], F32) as hi, \
                 nc.sbuf_tensor("mid", [1, NE], F32) as mid, \
                 nc.sbuf_tensor("prd", [1, NE], F32) as prd, \
                 nc.sbuf_tensor("tv", [1, NE], F32) as tv, \
                 nc.sbuf_tensor("thb", [128, NE], F32) as thb, \
                 nc.sbuf_tensor("cnt1", [1, NE], F32) as cnt1, \
                 nc.psum_tensor("thb_ps", [128, NE], F32) as thb_ps, \
                 nc.psum_tensor("cnt_ps", [1, NE], F32) as cnt_ps:
                pb.dma("sync", cnt_sb[:],
                       g_all.rearrange("(p c) e -> p c e", p=128),
                       wait=[AGD], inc=[s_gd])
                GLD = pb.at(s_gd)
                pb.emit("vector", lambda E: E.memset(lo[:], 0.0), inc=[s_ev])
                pb.emit("vector", lambda E: E.memset(hi[:], 1.0), inc=[s_ev])
                pb.emit("vector", lambda E: E.memset(mid[:], 0.5), inc=[s_ev])
                for it in range(36):
                    pb.emit("tensor", lambda E: E.matmul(
                        out=thb_ps[:], lhsT=onesr_f[:], rhs=mid[:],
                        start=True, stop=True),
                        wait=[pb.at(s_ev), CONST], inc=[s_mm])
                    pb.emit("vector", lambda E: E.tensor_copy(
                        out=thb[:], in_=thb_ps[:]),
                        wait=[pb.at(s_mm)], inc=[s_ev])
                    pb.emit("vector", lambda E: E.tensor_tensor(
                        out=ind[:], in0=cnt_sb[:],
                        in1=thb[:].unsqueeze(1).to_broadcast([128, 256, NE]),
                        op=ALU.is_ge),
                        wait=[pb.at(s_ev), GLD], inc=[s_ev])
                    pb.emit("vector", lambda E: E.reduce_sum(
                        out=red[:], in_=ind[:].rearrange("p c e -> p e c"),
                        axis=AX.X),
                        wait=[pb.at(s_ev)], inc=[s_ev])
                    pb.emit("tensor", lambda E: E.matmul(
                        out=cnt_ps[:], lhsT=ones_f[:], rhs=red[:],
                        start=True, stop=True),
                        wait=[pb.at(s_ev), CONST], inc=[s_mm])
                    pb.emit("vector", lambda E: E.tensor_copy(
                        out=cnt1[:], in_=cnt_ps[:]),
                        wait=[pb.at(s_mm)], inc=[s_ev])
                    pb.emit("vector", lambda E: E.tensor_scalar(
                        out=prd[:], in0=cnt1[:], scalar1=CAP + 0.5, scalar2=None,
                        op0=ALU.is_ge),
                        wait=[pb.at(s_ev)], inc=[s_ev])
                    pb.emit("vector", lambda E: E.tensor_tensor(
                        out=tv[:], in0=mid[:], in1=lo[:], op=ALU.subtract),
                        wait=[pb.at(s_ev)], inc=[s_ev])
                    pb.emit("vector", lambda E: E.tensor_tensor(
                        out=tv[:], in0=tv[:], in1=prd[:], op=ALU.mult),
                        wait=[pb.at(s_ev)], inc=[s_ev])
                    pb.emit("vector", lambda E: E.tensor_tensor(
                        out=lo[:], in0=lo[:], in1=tv[:], op=ALU.add),
                        wait=[pb.at(s_ev)], inc=[s_ev])
                    pb.emit("vector", lambda E: E.tensor_tensor(
                        out=tv[:], in0=hi[:], in1=mid[:], op=ALU.subtract),
                        wait=[pb.at(s_ev)], inc=[s_ev])
                    pb.emit("vector", lambda E: E.tensor_tensor(
                        out=tv[:], in0=tv[:], in1=prd[:], op=ALU.mult),
                        wait=[pb.at(s_ev)], inc=[s_ev])
                    pb.emit("vector", lambda E: E.tensor_tensor(
                        out=hi[:], in0=mid[:], in1=tv[:], op=ALU.add),
                        wait=[pb.at(s_ev)], inc=[s_ev])
                    pb.emit("vector", lambda E: E.tensor_tensor(
                        out=mid[:], in0=lo[:], in1=hi[:], op=ALU.add),
                        wait=[pb.at(s_ev)], inc=[s_ev])
                    pb.emit("vector", lambda E: E.tensor_scalar_mul(
                        mid[:], mid[:], 0.5),
                        wait=[pb.at(s_ev)], inc=[s_ev])
                if "dbg_th" in dbg_t:
                    pb.dma("sync", dbg_t["dbg_th"], hi[:],
                           wait=[pb.at(s_ev)], inc=[s_out])
                pb.emit("tensor", lambda E: E.matmul(
                    out=thb_ps[:], lhsT=onesr_f[:], rhs=hi[:],
                    start=True, stop=True),
                    wait=[pb.at(s_ev), CONST], inc=[s_mm])
                pb.emit("vector", lambda E: E.tensor_copy(
                    out=thb[:], in_=thb_ps[:]),
                    wait=[pb.at(s_mm)], inc=[s_ev])
                with nc.sbuf_tensor("selt", [128, 16, NE], F32) as selt:
                    pb.emit("vector", lambda E: E.tensor_tensor(
                        out=selt[:], in0=wt_sb[:],
                        in1=thb[:].unsqueeze(1).to_broadcast([128, 16, NE]),
                        op=ALU.is_ge),
                        wait=[pb.at(s_ev)], inc=[s_ev])
                    pb.emit("vector", lambda E: E.tensor_tensor(
                        out=wt_sb[:], in0=wt_sb[:], in1=selt[:], op=ALU.mult),
                        wait=[pb.at(s_ev)], inc=[s_ev])
                    pb.emit("vector", lambda E: E.tensor_tensor(
                        out=selt[:], in0=g2w_sb[:],
                        in1=thb[:].unsqueeze(1).to_broadcast([128, 16, NE]),
                        op=ALU.is_ge),
                        wait=[pb.at(s_ev)], inc=[s_ev])
                    pb.emit("vector", lambda E: E.tensor_tensor(
                        out=selt[:], in0=selt[:], in1=g2w_sb[:], op=ALU.mult),
                        wait=[pb.at(s_ev)], inc=[s_ev])
                    pb.emit("vector", lambda E: E.tensor_tensor(
                        out=wt_sb[:], in0=wt_sb[:], in1=selt[:], op=ALU.add),
                        wait=[pb.at(s_ev)], inc=[s_ev])
        W_V = pb.at(s_ev)
        if "dbg_w" in dbg_t:
            pb.dma("sync", dbg_t["dbg_w"], wt_sb[:], wait=[W_V], inc=[s_out])

        with nc.sbuf_tensor("wrow_t", [16, 128], F32) as wrow_t, \
             nc.psum_tensor("wr_ps", [16, 128], F32) as wr_ps:
            for e in range(NE):
                pb.emit("tensor", lambda E, e=e: E.transpose(
                    out=wr_ps[:], in_=wt_sb[:, :, e], identity=ident_f[:]),
                    wait=[W_V, CONST, pb.at(s_ev)], inc=[s_mm])
                pb.emit("vector", lambda E: E.tensor_copy(
                    out=wrow_t[:], in_=wr_ps[:]),
                    wait=[pb.at(s_mm), pb.at(s_wr)], inc=[s_ev])
                pb.dma("sync", wrow_d[e:e + 1, :]
                       .rearrange("o (g p) -> (o g) p", p=128), wrow_t[:],
                       wait=[pb.at(s_ev), pb.at(s_wr)], inc=[s_wr])
        GRD = pb.at(s_wr)

        with nc.sbuf_tensor("wu0", [128, FC, 128], BF16) as wu0, \
             nc.sbuf_tensor("wu1", [128, FC, 128], BF16) as wu1, \
             nc.sbuf_tensor("wd0", [128, FBC, 128], BF16) as wd0, \
             nc.sbuf_tensor("wd1", [128, FBC, 128], BF16) as wd1, \
             nc.sbuf_tensor("h_sb", [128, FBC, TOK], BF16) as h_sb, \
             nc.sbuf_tensor("grow", [1, TOK], F32) as grow, \
             nc.sbuf_tensor("gb_sb", [128, TOK], F32) as gb_sb, \
             nc.sbuf_tensor("yt", [128, 512], F32) as yt, \
             nc.psum_tensor("up_ps0", [128, 512], F32) as up_ps0, \
             nc.psum_tensor("up_ps1", [128, 512], F32) as up_ps1, \
             nc.psum_tensor("dn_ps0", [128, 512], F32) as dn_ps0, \
             nc.psum_tensor("dn_ps1", [128, 512], F32) as dn_ps1, \
             nc.psum_tensor("gb_ps", [128, 512], F32) as gb_ps:
            wus = [wu0, wu1]
            wds = [wd0, wd1]
            upss = [up_ps0, up_ps1]
            dnss = [dn_ps0, dn_ps1]
            uidx = 0
            didx = 0
            for e in range(NE):
                pb.dma("sync", grow[:], wrow_d[e:e + 1, :],
                       wait=[GRD, pb.at(s_ev), pb.at(s_mm), pb.at(s_gd)],
                       inc=[s_gd])
                GR1 = pb.at(s_gd)
                for tc in range(4):
                    sl = slice(tc * 512, (tc + 1) * 512)
                    pb.emit("tensor", lambda E, sl=sl: E.matmul(
                        out=gb_ps[:], lhsT=onesr_f[:], rhs=grow[:, sl],
                        start=True, stop=True),
                        wait=[GR1, CONST, pb.at(s_ev)], inc=[s_mm])
                    pb.emit("vector", lambda E, sl=sl: E.tensor_copy(
                        out=gb_sb[:, sl], in_=gb_ps[:]),
                        wait=[pb.at(s_mm)], inc=[s_ev])
                GBD = pb.at(s_ev)
                for fb in range(NFB):
                    for oc in range(FBC):
                        wb = wus[uidx % 2]
                        pb.dma("gpsimd", wb[:],
                               w1_d[e, :, :, (fb * FBC + oc) * 128:
                                    (fb * FBC + oc + 1) * 128]
                               .rearrange("c p n -> p c n"),
                               wait=[pb.at(s_act), pb.at(s_mm)] if uidx >= 2
                               else [LN_V, W_V, GRD, pb.at(s_out)],
                               inc=[s_wu[uidx % 2]])
                        WBU = pb.at(s_wu[uidx % 2])
                        for tc in range(4):
                            sl = slice(tc * 512, (tc + 1) * 512)
                            pp = upss[tc % 2]
                            for mc in range(FC):
                                pb.emit("tensor", lambda E, wb=wb, mc=mc, sl=sl,
                                        pp=pp: E.matmul(
                                    out=pp[:], lhsT=wb[:, mc, :],
                                    rhs=xn_sb[:, mc, sl],
                                    start=(mc == 0), stop=(mc == FC - 1)),
                                    wait=([WBU, LN_V, pb.at(s_act)]
                                          if mc == 0 else []),
                                    inc=[s_mm] if mc == FC - 1 else [])
                            MM = pb.at(s_mm)
                            if biases:
                                pb.emit("scalar", lambda E, pp=pp, oc=oc, fb=fb,
                                        e=e, sl=sl: E.activation(
                                    out=h_sb[:, oc, sl], in_=pp[:], func=act_f,
                                    bias=b1_sb[:, e, fb * FBC + oc:
                                               fb * FBC + oc + 1]),
                                    wait=[MM, IN0], inc=[s_act])
                            else:
                                pb.emit("scalar", lambda E, pp=pp, oc=oc,
                                        sl=sl: E.activation(
                                    out=h_sb[:, oc, sl], in_=pp[:], func=act_f),
                                    wait=[MM], inc=[s_act])
                        uidx += 1
                    HUP = pb.at(s_act)
                    for mc in range(FC):
                        wb = wds[didx % 2]
                        pb.dma("gpsimd", wb[:],
                               w2_d[e, fb * FBC:(fb + 1) * FBC, :,
                                    mc * 128:(mc + 1) * 128]
                               .rearrange("c p n -> p c n"),
                               wait=[pb.at(s_mm), pb.at(s_ev)] if didx >= 2
                               else [LN_V, W_V, GRD, pb.at(s_out)],
                               inc=[s_wd2[didx % 2]])
                        WBD = pb.at(s_wd2[didx % 2])
                        for tc in range(4):
                            sl = slice(tc * 512, (tc + 1) * 512)
                            pp = dnss[tc % 2]
                            for fc2 in range(FBC):
                                pb.emit("tensor", lambda E, wb=wb, fc2=fc2, sl=sl,
                                        pp=pp: E.matmul(
                                    out=pp[:], lhsT=wb[:, fc2, :],
                                    rhs=h_sb[:, fc2, sl],
                                    start=(fc2 == 0), stop=(fc2 == FBC - 1)),
                                    wait=([WBD, HUP, pb.at(s_ev)]
                                          if fc2 == 0 else []),
                                    inc=[s_mm] if fc2 == FBC - 1 else [])
                            MMD = pb.at(s_mm)
                            if biases and fb == 0:
                                pb.emit("vector", lambda E, pp=pp, e=e, mc=mc,
                                        sl=sl: E.scalar_tensor_tensor(
                                    out=yt[:], in0=pp[:],
                                    scalar=b2_sb[:, e, mc:mc + 1],
                                    in1=gb_sb[:, sl], op0=ALU.add, op1=ALU.mult),
                                    wait=[MMD, IN0, GBD, pb.at(s_ev)],
                                    inc=[s_ev])
                            else:
                                pb.emit("vector", lambda E, pp=pp, sl=sl:
                                        E.tensor_tensor(
                                    out=yt[:], in0=pp[:], in1=gb_sb[:, sl],
                                    op=ALU.mult),
                                    wait=[MMD, GBD, pb.at(s_ev)], inc=[s_ev])
                            pb.emit("vector", lambda E, mc=mc, sl=sl:
                                    E.tensor_tensor(
                                out=y_acc[:, mc, sl], in0=y_acc[:, mc, sl],
                                in1=yt[:], op=ALU.add),
                                wait=[pb.at(s_ev), CONST], inc=[s_ev])
                        didx += 1
        YD = pb.at(s_ev)

        with nc.sbuf_tensor("bs0", [128, 512], F32) as bs0, \
             nc.sbuf_tensor("bs1", [128, 512], F32) as bs1, \
             nc.sbuf_tensor("br0", [128, 512], F32) as br0, \
             nc.sbuf_tensor("br1", [128, 512], F32) as br1:
            bss = [bs0, bs1]
            brs = [br0, br1]
            for i, (fc, tc) in enumerate([(f, t) for f in range(FC)
                                          for t in range(4)]):
                sl = slice(tc * 512, (tc + 1) * 512)
                bs = bss[i % 2]
                br = brs[i % 2]
                pb.dma("sync", bs[:], xin[fc * 128:(fc + 1) * 128, sl],
                       wait=[pb.at(s_so[i % 2]), pb.at(s_ev)],
                       inc=[s_bas[i % 2]])
                if xres_f:
                    pb.dma("sync", br[:], xres[fc * 128:(fc + 1) * 128, sl],
                           inc=[s_bas[i % 2]])
                BL = pb.at(s_bas[i % 2])
                pb.emit("vector", lambda E, bs=bs, fc=fc, sl=sl: E.tensor_tensor(
                    out=bs[:], in0=bs[:], in1=y_acc[:, fc, sl], op=ALU.add),
                    wait=[BL, YD, pb.at(s_ev)], inc=[s_ev])
                if xres_f:
                    pb.emit("vector", lambda E, bs=bs, br=br: E.tensor_tensor(
                        out=bs[:], in0=bs[:], in1=br[:], op=ALU.add),
                        wait=[pb.at(s_ev)], inc=[s_ev])
                pb.dma("sync", xout[fc * 128:(fc + 1) * 128, sl], bs[:],
                       wait=[pb.at(s_ev)], inc=[s_so[i % 2]])
        pb.emit("sync", lambda E: E.nop(),
                wait=[pb.at(s_so[0]), pb.at(s_so[1]), pb.at(s_out)])
        pb.realize()
    return nc


def prep_moe_inputs(cfg, weights, x_parts, xres_parts=None):
    NE = 8
    DFF = cfg["dff"]
    w1 = np.ascontiguousarray(np.asarray(weights["w1"], np.float32)
                              .reshape(NE, FC, 128, DFF))
    w2 = np.ascontiguousarray(np.asarray(weights["w2"], np.float32)
                              .reshape(NE, DFF // 128, 128, DM))
    wr = chunked_w(np.asarray(weights["wr"], np.float32))
    in_maps = []
    for core in range(NCORES):
        m = {"xin": x_parts[core], "w1": w1, "w2": w2, "wr": wr}
        if cfg["xres"]:
            m["xres"] = xres_parts[core]
        if cfg["biases"]:
            b1 = np.asarray(weights["b1"], np.float32)
            b2 = np.asarray(weights["b2"], np.float32)
            m["b1"] = np.ascontiguousarray(
                b1.reshape(NE, DFF // 128, 128).transpose(0, 2, 1))
            m["b2"] = np.ascontiguousarray(
                b2.reshape(NE, FC, 128).transpose(0, 2, 1))
        if cfg["rms"]:
            m["gain"] = colsplit(np.asarray(weights["gain"], np.float32))
        in_maps.append(m)
    return in_maps


MOE_CFGS = {
    "L1_ffn": dict(kind="top1", dff=4096, act="gelu", biases=True, rms=False,
                   xres=False),
    "L0_moe": dict(kind="cap2", dff=2048, act="relu", biases=False, rms=True,
                   xres=True),
}


ATTN_CFGS = {
    "L0_attn": dict(H=8, HD=128, scale=1.0 / 128.0, rope="half",
                    qk_bias=False, o_bias=False, two_norms=False),
    "L0_mha": dict(H=16, HD=64, scale=1.0 / 8.0, rope="int",
                   qk_bias=False, o_bias=False, two_norms=True),
    "L1_attn": dict(H=16, HD=64, scale=1.0 / 8.0, rope=None,
                    qk_bias=True, o_bias=True, two_norms=False),
}


def _run(nc, in_maps):
    from concourse.bass_utils import run_bass_kernel_spmd
    res = run_bass_kernel_spmd(nc, in_maps, core_ids=list(range(NCORES)))
    return res


def kernel(**inputs):
    I = {k: np.asarray(v, dtype=np.float32) for k, v in inputs.items()}
    x0_parts = shard_stream(I["L0_attn_x"])

    cfg = dict(ATTN_CFGS["L0_attn"])
    cfg["debug"] = ()
    cfg["ablate"] = ""
    nc = build_attn(cfg)
    w = {"wq": I["L0_attn_wq"].reshape(DM, DM),
         "wk": I["L0_attn_wk"].reshape(DM, DM),
         "wv": I["L0_attn_wv"].reshape(DM, DM),
         "wo": I["L0_attn_wo"].reshape(DM, DM)}
    res = _run(nc, prep_attn_inputs(cfg, w, x0_parts))
    x1_parts = [res.results[c]["xout"] for c in range(NCORES)]

    cfg = dict(ATTN_CFGS["L0_mha"])
    cfg["debug"] = ()
    cfg["ablate"] = ""
    nc = build_attn(cfg)
    wqkv = I["L0_ffn_w_qkv"].reshape(DM, 3 * DM)
    w = {"wq": wqkv[:, 0:DM], "wk": wqkv[:, DM:2 * DM],
         "wv": wqkv[:, 2 * DM:3 * DM],
         "wo": I["L0_ffn_w_o"].reshape(DM, DM),
         "gain": I["L0_ffn_g1"].reshape(DM)}
    res = _run(nc, prep_attn_inputs(cfg, w, x1_parts))
    z1_parts = [res.results[c]["xout"] for c in range(NCORES)]

    cfg = dict(MOE_CFGS["L0_moe"])
    cfg["debug"] = ()
    nc = build_moe(cfg)
    w = {"wr": I["L0_ffn_w_router"].reshape(DM, 8),
         "w1": I["L0_ffn_w1"], "w2": I["L0_ffn_w2"],
         "gain": I["L0_ffn_g2"].reshape(DM)}
    res = _run(nc, prep_moe_inputs(cfg, w, z1_parts, x1_parts))
    x2_parts = [res.results[c]["xout"] for c in range(NCORES)]

    cfg = dict(ATTN_CFGS["L1_attn"])
    cfg["debug"] = ()
    cfg["ablate"] = ""
    nc = build_attn(cfg)
    wqkv = I["L1_attn_Wqkv"].reshape(DM, 3 * DM)
    w = {"wq": wqkv[:, 0:DM], "wk": wqkv[:, DM:2 * DM],
         "wv": wqkv[:, 2 * DM:3 * DM],
         "wo": I["L1_attn_Wproj"].reshape(DM, DM),
         "bqkv": I["L1_attn_bqkv"].reshape(3 * DM),
         "bo": I["L1_attn_bproj"].reshape(DM)}
    res = _run(nc, prep_attn_inputs(cfg, w, x2_parts))
    x3_parts = [res.results[c]["xout"] for c in range(NCORES)]

    cfg = dict(MOE_CFGS["L1_ffn"])
    cfg["debug"] = ()
    nc = build_moe(cfg)
    w = {"wr": I["L1_ffn_gate_w"].reshape(DM, 8),
         "w1": I["L1_ffn_w1"], "b1": I["L1_ffn_b1"],
         "w2": I["L1_ffn_w2"], "b2": I["L1_ffn_b2"]}
    res = _run(nc, prep_moe_inputs(cfg, w, x3_parts))
    x4_parts = [res.results[c]["xout"] for c in range(NCORES)]

    return unshard_stream(x4_parts).astype(np.float32)


# version 04
# speedup vs baseline: 1.5088x; 1.5088x over previous
import numpy as np
from contextlib import ExitStack

import concourse.bass as bass
import concourse.mybir as mybir
from concourse.bass_utils import run_bass_kernel_spmd

F32 = mybir.dt.float32
BF16 = mybir.dt.bfloat16
AF = mybir.ActivationFunctionType
ALU = mybir.AluOpType
AX = mybir.AxisListType

NCORES = 8
B = 2
T = 8192
DM = 1024
TOK = 2048
NSUB = 8
FC = 8
KVROWS = 2056
ENGINES = ("sync", "vector", "scalar", "tensor", "gpsimd")



_POS_CACHE = {}


def pos_arrays(core):
    if core not in _POS_CACHE:
        bidx, pidx = [], []
        for b in range(B):
            for jj in range(NSUB):
                s = 8 * jj + core
                for r in range(128):
                    bidx.append(b)
                    pidx.append(s * 128 + r)
        _POS_CACHE[core] = (np.array(bidx), np.array(pidx))
    return _POS_CACHE[core]


def shard_stream(x):
    out = []
    for core in range(NCORES):
        bidx, pidx = pos_arrays(core)
        out.append(np.ascontiguousarray(x[bidx, pidx, :].T.astype(np.float32)))
    return out


def unshard_stream(parts):
    x = np.empty((B, T, DM), dtype=np.float32)
    for core in range(NCORES):
        bidx, pidx = pos_arrays(core)
        x[bidx, pidx, :] = parts[core].T
    return x


def rope_tables(core, HD, kind):
    base = 10000.0
    half = HD // 2
    if kind == "half":
        ang = np.power(base, -(np.arange(half).astype(np.float64)) / half)
    else:
        ang = base ** (-2.0 * (np.arange(half).astype(np.float64) / HD))
    ang = ang.astype(np.float32)
    _, pidx = pos_arrays(core)
    pos = pidx.astype(np.float32)
    rad = (pos[None, :] * ang[:, None]).astype(np.float32)
    cos = np.cos(rad).astype(np.float32)
    sin = np.sin(rad).astype(np.float32)
    reps = 64 // half
    cos64 = np.concatenate([cos] * reps, axis=0)
    sin64 = np.concatenate([sin] * reps, axis=0)
    cs1 = np.concatenate([cos64, sin64], axis=0)
    cs2 = np.concatenate([sin64, -cos64], axis=0)
    return np.ascontiguousarray(cs1), np.ascontiguousarray(cs2)


def qk_perm(HD, kind):
    perm = []
    hpc = 128 // HD
    half = HD // 2
    for c in range(DM // 128):
        hs = [c * hpc + k for k in range(hpc)]
        ev_cols, od_cols = [], []
        for h in hs:
            if kind == "int":
                ev = [h * HD + 2 * i for i in range(half)]
                od = [h * HD + 2 * i + 1 for i in range(half)]
            else:
                ev = [h * HD + i for i in range(half)]
                od = [h * HD + half + i for i in range(half)]
            ev_cols += ev
            od_cols += od
        perm += ev_cols + od_cols
    return np.array(perm)


def make_causal_mask(core):
    import ml_dtypes
    p = np.arange(128)[:, None, None]
    u = np.arange(8)[None, :, None]
    q = np.arange(128)[None, None, :]
    m = np.where((u * 128 + p) <= (core * 128 + q), 0.0, -1e30).astype(np.float32)
    return np.ascontiguousarray(m.astype(ml_dtypes.bfloat16))


def chunked_w(w):
    return np.ascontiguousarray(np.asarray(w, dtype=np.float32).reshape(FC, 128, -1))


def colsplit(v):
    n = np.asarray(v).shape[0] // 128
    return np.ascontiguousarray(np.asarray(v, dtype=np.float32).reshape(n, 128).T)



class Sem:
    __slots__ = ("h", "n", "name")


CHAINED = ("vector", "scalar", "gpsimd")


class PB:
    def __init__(self, nc, ctx):
        self.nc = nc
        self.ctx = ctx
        self.q = {e: [] for e in ENGINES}
        self.chain = {}
        self.cache = {}
        self.pending = {}
        self.fence_t = None

    def init_chains(self):
        for e in CHAINED:
            self.chain[e] = self.sem("chain_" + e)
        self.fence_t = self.ctx.enter_context(
            self.nc.sbuf_tensor("fence_t", [1, 1], F32))

    def sem(self, name):
        if name in self.cache:
            return self.cache[name]
        s = Sem()
        s.h = self.ctx.enter_context(self.nc.semaphore(name))
        s.n = 0
        s.name = name
        self.cache[name] = s
        return s

    def fence(self):
        toks = [(s, s.n) for s in self.cache.values() if s.n > 0]
        self.emit("gpsimd", lambda E: E.memset(self.fence_t[:], 0.0), wait=toks)
        cg = self.chain["gpsimd"]
        tok = (cg, cg.n)
        for e in ENGINES:
            self.pending[e] = [tok]

    def emit(self, eng, fn, wait=(), inc=(), inc_amt=1, chain=True):
        wait = list(wait)
        pend = self.pending.pop(eng, None)
        if pend:
            wait = list(pend) + wait
        inc = list(inc)
        if chain and eng in self.chain:
            cs = self.chain[eng]
            wait.append((cs, cs.n))
            cs.n += 1
            for s in inc:
                s.h = cs.h
                s.n = cs.n
            real_incs = [(cs.h, 1)]
        else:
            real_incs = []
            for s in inc:
                s.n += inc_amt
                real_incs.append((s.h, inc_amt))
        waitv = [(s.h, v) for (s, v) in wait if v > 0]

        def run(E, fn=fn, waitv=waitv, incs=real_incs):
            for (h, v) in waitv:
                E.wait_ge(h, v)
            ins = fn(E)
            for (h, a) in incs:
                ins = ins.then_inc(h, a)
        self.q[eng].append(run)

    def dma(self, eng, out, in_, wait=(), inc=(), **kw):
        self.emit(eng, lambda E: E.dma_start(out=out, in_=in_, **kw),
                  wait=wait, inc=inc, inc_amt=16, chain=False)

    def at(self, sem):
        return (sem, sem.n)

    def realize(self):
        blk = self.ctx.enter_context(self.nc.Block())
        for e in ENGINES:
            steps = self.q[e]
            getattr(blk, e)(lambda E, steps=steps: [f(E) for f in steps])


class NCP:
    def __init__(self, nc, pfx):
        self._nc = nc
        self._pfx = pfx

    def sbuf_tensor(self, name, *a, **k):
        return self._nc.sbuf_tensor(self._pfx + name, *a, **k)

    def psum_tensor(self, name, *a, **k):
        return self._nc.psum_tensor(self._pfx + name, *a, **k)

    def dram_tensor(self, name, *a, **k):
        return self._nc.dram_tensor(self._pfx + name, *a, **k)

    def __getattr__(self, n):
        return getattr(self._nc, n)



def build_attn(cfg, fuse=None):
    H, HD = cfg["H"], cfg["HD"]
    HPC = 128 // HD
    scale = float(cfg["scale"])
    rope = cfg["rope"]
    two = cfg["two_norms"]
    qk_bias = cfg["qk_bias"]
    o_bias = cfg["o_bias"]
    dbg = set(cfg.get("debug", ()))

    if fuse is None:
        nc = NCP(bass.Bass("TRN2", target_bir_lowering=False), "")
    else:
        nc = NCP(fuse["nc"], fuse["pfx"])
    if fuse is None:
        xin = nc.dram_tensor("xin", [DM, TOK], F32, kind="ExternalInput").ap()
        xout = nc.dram_tensor("xout", [DM, TOK], F32, kind="ExternalOutput").ap()
        zout = nc.dram_tensor("zout", [DM, TOK], F32,
                              kind="ExternalOutput").ap() if two else None
    else:
        xin = fuse["xin"]
        xout = fuse["xout"]
        zout = fuse.get("zout") if two else None
    wqkv = nc.dram_tensor("wqkv", [FC, 128, 3 * DM], F32, kind="ExternalInput").ap()
    wo_d = nc.dram_tensor("wo", [FC, 128, DM], F32, kind="ExternalInput").ap()
    maskd = nc.dram_tensor("maskin", [128, 8, 128], BF16, kind="ExternalInput").ap()
    if rope:
        cs1_d = nc.dram_tensor("cossin1", [128, TOK], F32, kind="ExternalInput").ap()
        cs2_d = nc.dram_tensor("cossin2", [128, TOK], F32, kind="ExternalInput").ap()
    if qk_bias:
        qkb_d = nc.dram_tensor("qkbias", [128, 16], F32, kind="ExternalInput").ap()
        vb_d = nc.dram_tensor("vbias", [1, DM], F32, kind="ExternalInput").ap()
    if o_bias:
        ob_d = nc.dram_tensor("obias", [128, FC], F32, kind="ExternalInput").ap()
    if two:
        gain_d = nc.dram_tensor("gain", [128, FC], F32, kind="ExternalInput").ap()

    kv_in = nc.dram_tensor("kv_in", [KVROWS, TOK], BF16, kind="Internal").ap()
    kv_all = nc.dram_tensor("kv_all", [NCORES * KVROWS, TOK], BF16, kind="Internal").ap()
    kv_all_r = kv_all.rearrange("(r k) t -> r k t", r=NCORES)

    dbg_t = {}
    def dbg_mk(name, shape):
        dbg_t[name] = nc.dram_tensor(name, shape, F32, kind="ExternalOutput").ap()
    if "xn" in dbg: dbg_mk("dbg_xn", [DM, TOK])
    if "q" in dbg: dbg_mk("dbg_q", [DM, TOK])
    if "kv" in dbg: dbg_mk("dbg_kv", [KVROWS, TOK])
    if "c" in dbg: dbg_mk("dbg_c", [128, 16, 16])
    if "o" in dbg: dbg_mk("dbg_o", [128, B * NSUB, DM])
    if "l" in dbg: dbg_mk("dbg_l", [128, B * NSUB, 16])

    ctx = ExitStack()
    with ctx:
        if fuse is None:
            pb = PB(nc, ctx)
            pb.init_chains()
        else:
            pb = fuse["pb"]
        sbt = lambda n, s, d: ctx.enter_context(nc.sbuf_tensor(n, s, d))

        q_sb = sbt("q_sb", [128, FC, TOK], BF16)
        c_all = sbt("c_all", [128, 16, 16], F32)
        qmn = sbt("qmn", [1, 256], F32)
        kmaxrow = sbt("kmaxrow", [1, 32], F32)
        kn_sb = sbt("kn_sb", [1, 32], F32)
        ones_f = sbt("ones_f", [128, 1], F32)
        ones_bf = sbt("ones_bf", [128, 1], BF16)
        onesr_f = sbt("onesr_f", [1, 128], F32)
        nonesr_f = sbt("nonesr_f", [1, 128], F32)
        ident = sbt("ident", [128, 128], BF16)
        eps_c = sbt("eps_c", [1, 1], F32)
        eps2_c = sbt("eps2_c", [1, 1], F32)
        if qk_bias:
            qkb_sb = sbt("qkb_sb", [128, 16], F32)
        if o_bias:
            ob_sb = sbt("ob_sb", [128, FC], F32)
        if two:
            g_sb = sbt("g_sb", [128, FC], F32)

        s_in = pb.sem("s_in")
        s_gp = pb.sem("s_gp")
        s_x = pb.sem("s_x")
        s_ln = pb.sem("s_ln")
        s_lnmm = pb.sem("s_lnmm")
        s_lnact = pb.sem("s_lnact")
        s_w = pb.sem("s_w")
        s_mm = pb.sem("s_mm")
        s_ev = pb.sem("s_ev")
        s_act = pb.sem("s_act")
        s_st = pb.sem("s_st")
        s_ag = pb.sem("s_ag")
        s_kt = pb.sem("s_kt")
        s_fl = pb.sem("s_fl")
        s_fa = pb.sem("s_fa")
        s_fe = pb.sem("s_fe")
        s_out = pb.sem("s_out")
        s_gpd = pb.sem("s_gpd")
        s_xt = [pb.sem("s_xt0"), pb.sem("s_xt1")]
        s_zex = pb.sem("s_zex")
        s_cs = pb.sem("s_cs")
        s_mk = pb.sem("s_mk")
        s_ktp = [pb.sem("s_ktp0"), pb.sem("s_ktp1")]
        s_wv = [pb.sem("s_wv0"), pb.sem("s_wv1")]
        s_bas = [pb.sem("s_bas0"), pb.sem("s_bas1")]
        s_so = [pb.sem("s_so0"), pb.sem("s_so1")]
        s_wb = [pb.sem("s_wb0"), pb.sem("s_wb1")]
        s_kst = pb.sem("s_kst")
        s_vst = pb.sem("s_vst")

        pb.emit("gpsimd", lambda E: E.memset(ones_f[:], 1.0))
        pb.emit("gpsimd", lambda E: E.memset(ones_bf[:], 1.0))
        pb.emit("gpsimd", lambda E: E.memset(onesr_f[:], 1.0))
        pb.emit("gpsimd", lambda E: E.memset(nonesr_f[:], -1.0))
        pb.emit("gpsimd", lambda E: E.memset(eps_c[:], 1e-5))
        pb.emit("gpsimd", lambda E: E.memset(eps2_c[:], 1e-6))
        pb.emit("gpsimd", lambda E: E.memset(kn_sb[:], 0.0))
        pb.emit("gpsimd", lambda E: E.memset(qmn[:], 0.0))
        pb.emit("gpsimd", lambda E: E.memset(kmaxrow[:], 0.0))
        pb.emit("gpsimd", lambda E: E.memset(c_all[:], 0.0))
        pb.emit("gpsimd", lambda E: E.memset(ident[:], 1.0))
        pb.emit("gpsimd", lambda E: E.affine_select(
            out=ident[:], in_=ident[:], pattern=[[-1, 128]],
            compare_op=ALU.is_equal, fill=0.0, base=0, channel_multiplier=1),
            inc=[s_gp])
        CONST = pb.at(s_gp)
        if qk_bias:
            pb.dma("sync", qkb_sb[:], qkb_d, inc=[s_in])
        if o_bias:
            pb.dma("sync", ob_sb[:], ob_d, inc=[s_in])
        if two:
            pb.dma("sync", g_sb[:], gain_d, inc=[s_in])
        IN0 = pb.at(s_in)

        phx = ExitStack()
        xn_sb = phx.enter_context(nc.sbuf_tensor("xn_sb", [128, FC, TOK], BF16))
        def ln_stats(src_ap, rms_only, tt):
            (xss, xqs, zsl, srow, mrow, rrow, mb, rb) = tt[:8]
            mean_ps, sq_ps = tt[8], tt[9]
            cons_mm = {}
            cons_act = {}
            for idx in range(32):
                fc, tc = idx // 4, idx % 4
                tl = xss[idx % 2]
                xq = xqs[idx % 2]
                if idx >= 2:
                    w = [(s_lnmm, cons_mm[idx - 2]), (s_lnact, cons_act[idx - 2])]
                else:
                    w = [pb.at(s_xt[idx % 2]), pb.at(s_ln), pb.at(s_zex)]
                pb.dma("sync", tl[:],
                       src_ap[fc * 128:(fc + 1) * 128, tc * 512:(tc + 1) * 512],
                       wait=w, inc=[s_xt[idx % 2]])
                XL = pb.at(s_xt[idx % 2])
                if not rms_only:
                    pb.emit("tensor", lambda E, tl=tl, fc=fc, tc=tc: E.matmul(
                        out=mean_ps[:, tc * 512:(tc + 1) * 512],
                        lhsT=ones_f[:], rhs=tl[:],
                        start=(fc == 0), stop=(fc == FC - 1)),
                        wait=[XL, CONST], inc=[s_lnmm])
                pb.emit("scalar", lambda E, tl=tl, xq=xq: E.activation(
                    out=xq[:], in_=tl[:], func=AF.Square),
                    wait=[XL] + ([(s_lnmm, cons_mm[idx - 2])] if idx >= 2 else []),
                    inc=[s_lnact])
                pb.emit("tensor", lambda E, xq=xq, fc=fc, tc=tc: E.matmul(
                    out=sq_ps[:, tc * 512:(tc + 1) * 512],
                    lhsT=ones_f[:], rhs=xq[:],
                    start=(fc == 0), stop=(fc == FC - 1)),
                    wait=[pb.at(s_lnact), CONST], inc=[s_lnmm])
                cons_mm[idx] = s_lnmm.n
                cons_act[idx] = s_lnact.n
            if not rms_only:
                pb.emit("vector", lambda E: E.tensor_scalar_mul(
                    mrow[:], mean_ps[:], 1.0 / DM),
                    wait=[pb.at(s_lnmm)], inc=[s_ln])
                pb.emit("vector", lambda E: E.tensor_tensor(
                    out=srow[:], in0=mrow[:], in1=mrow[:], op=ALU.mult),
                    wait=[pb.at(s_ln)], inc=[s_ln])
                pb.emit("vector", lambda E: E.scalar_tensor_tensor(
                    out=srow[:], in0=sq_ps[:], scalar=1.0 / DM, in1=srow[:],
                    op0=ALU.mult, op1=ALU.subtract),
                    wait=[pb.at(s_ln)], inc=[s_ln])
            else:
                pb.emit("vector", lambda E: E.tensor_scalar_mul(
                    srow[:], sq_ps[:], 1.0 / DM),
                    wait=[pb.at(s_lnmm)], inc=[s_ln])
            eps_use = eps2_c if rms_only else eps_c
            pb.emit("scalar", lambda E, e=eps_use: E.activation(
                out=rrow[:], in_=srow[:], func=AF.Sqrt, bias=e[:]),
                wait=[pb.at(s_ln), CONST], inc=[s_lnact])
            pb.emit("vector", lambda E: E.reciprocal(out=rrow[:], in_=rrow[:]),
                    wait=[pb.at(s_lnact)], inc=[s_ln])

        def ln_apply(src_ap, rms_only, out_bf, export_dram, tt):
            (xss, xqs, zsl, srow, mrow, rrow, mb, rb) = tt[:8]
            bc1_ps, bc2_ps = tt[10], tt[11]
            RR = pb.at(s_ln)
            for tc in range(4):
                sl = slice(tc * 512, (tc + 1) * 512)
                pb.emit("tensor", lambda E, sl=sl: E.matmul(
                    out=bc1_ps[:], lhsT=onesr_f[:], rhs=rrow[:, sl],
                    start=True, stop=True),
                    wait=[RR, pb.at(s_ln), CONST], inc=[s_lnmm])
                pb.emit("vector", lambda E: E.tensor_copy(out=rb[:], in_=bc1_ps[:]),
                        wait=[pb.at(s_lnmm), pb.at(s_ln)], inc=[s_ln])
                if not rms_only:
                    pb.emit("tensor", lambda E, sl=sl: E.matmul(
                        out=bc2_ps[:], lhsT=onesr_f[:], rhs=mrow[:, sl],
                        start=True, stop=True),
                        wait=[pb.at(s_ln)], inc=[s_lnmm])
                    pb.emit("vector", lambda E: E.tensor_copy(out=mb[:], in_=bc2_ps[:]),
                            wait=[pb.at(s_lnmm)], inc=[s_ln])
                BC = pb.at(s_ln)
                for fc in range(FC):
                    idx = tc * 8 + fc
                    tl = xss[idx % 2]
                    xq = xqs[idx % 2]
                    pb.dma("sync", tl[:], src_ap[fc * 128:(fc + 1) * 128, sl],
                           wait=[pb.at(s_ln), pb.at(s_xt[idx % 2]), pb.at(s_zex)],
                           inc=[s_xt[idx % 2]])
                    XL = pb.at(s_xt[idx % 2])
                    if not rms_only:
                        pb.emit("vector", lambda E, tl=tl, xq=xq: E.tensor_tensor(
                            out=xq[:], in0=tl[:], in1=mb[:], op=ALU.subtract),
                            wait=[XL, BC, pb.at(s_ln)], inc=[s_ln])
                        srct = xq
                    else:
                        srct = tl
                    if export_dram is not None:
                        pb.emit("vector", lambda E, st=srct: E.tensor_tensor(
                            out=zsl[:], in0=st[:], in1=rb[:], op=ALU.mult),
                            wait=[pb.at(s_ln), BC, XL, pb.at(s_zex)], inc=[s_ln])
                        pb.dma("sync", export_dram[fc * 128:(fc + 1) * 128, sl],
                               zsl[:], wait=[pb.at(s_ln), pb.at(s_zex)],
                               inc=[s_zex])
                    if out_bf is not None:
                        if rms_only and two:
                            pb.emit("vector", lambda E, st=srct: E.tensor_tensor(
                                out=zsl[:], in0=st[:], in1=rb[:], op=ALU.mult),
                                wait=[pb.at(s_ln), BC, XL, pb.at(s_zex)], inc=[s_ln])
                            pb.emit("vector", lambda E, fc=fc, sl=sl: E.tensor_scalar_mul(
                                out_bf[:, fc, sl], zsl[:], g_sb[:, fc:fc + 1]),
                                wait=[pb.at(s_ln), IN0], inc=[s_ln])
                        else:
                            pb.emit("vector", lambda E, st=srct, fc=fc, sl=sl: E.tensor_tensor(
                                out=out_bf[:, fc, sl], in0=st[:], in1=rb[:],
                                op=ALU.mult),
                                wait=[pb.at(s_ln), BC, XL], inc=[s_ln])

        with nc.sbuf_tensor("xs0", [128, 512], F32) as xs0, \
             nc.sbuf_tensor("xs1", [128, 512], F32) as xs1, \
             nc.sbuf_tensor("xq0", [128, 512], F32) as xq0, \
             nc.sbuf_tensor("xq1", [128, 512], F32) as xq1, \
             nc.sbuf_tensor("zsl", [128, 512], F32) as zsl, \
             nc.sbuf_tensor("srow", [1, TOK], F32) as srow, \
             nc.sbuf_tensor("mrow", [1, TOK], F32) as mrow, \
             nc.sbuf_tensor("rrow", [1, TOK], F32) as rrow, \
             nc.sbuf_tensor("mb", [128, 512], F32) as mb, \
             nc.sbuf_tensor("rb", [128, 512], F32) as rb:
            tt = ([xs0, xs1], [xq0, xq1], zsl, srow, mrow, rrow, mb, rb,
                  None, None, None, None)
            tt = list(tt)
            passes = [(xin, False, xn_sb, None)] if not two else \
                     [(xin, False, None, zout), (zout, True, xn_sb, None)]
            for pi, (srcp, rmsp, outp, expp) in enumerate(passes):
                with nc.psum_tensor("mean_ps%d" % pi, [1, TOK], F32) as mean_ps, \
                     nc.psum_tensor("sq_ps%d" % pi, [1, TOK], F32) as sq_ps:
                    tt[8], tt[9] = mean_ps, sq_ps
                    ln_stats(srcp, rmsp, tt)
                with nc.psum_tensor("bc1_ps%d" % pi, [128, 512], F32) as bc1_ps, \
                     nc.psum_tensor("bc2_ps%d" % pi, [128, 512], F32) as bc2_ps:
                    tt[10], tt[11] = bc1_ps, bc2_ps
                    ln_apply(srcp, rmsp, outp, expp, tt)

        if "dbg_xn" in dbg_t:
            for fc in range(FC):
                pb.dma("gpsimd", dbg_t["dbg_xn"][fc * 128:(fc + 1) * 128, :],
                       xn_sb[:, fc, :], wait=[pb.at(s_ln)], inc=[s_out])
        LN_V = pb.at(s_ln)

        with nc.sbuf_tensor("cs1_sb", [128, TOK], F32) as cs1_sb, \
             nc.sbuf_tensor("cs2_sb", [128, TOK], F32) as cs2_sb, \
             nc.sbuf_tensor("wbuf0", [128, FC, 128], BF16) as wbuf0, \
             nc.sbuf_tensor("wbuf1", [128, FC, 128], BF16) as wbuf1, \
             nc.sbuf_tensor("kstage", [128, TOK], BF16) as kstage, \
             nc.sbuf_tensor("ksq", [128, TOK], BF16) as ksq, \
             nc.sbuf_tensor("statpad", [KVROWS - 2048, TOK], BF16) as statpad, \
             nc.sbuf_tensor("r1", [64, 512], F32) as r1, \
             nc.sbuf_tensor("r2", [64, 512], F32) as r2, \
             nc.sbuf_tensor("r3", [64, 512], F32) as r3, \
             nc.sbuf_tensor("r4", [64, 512], F32) as r4, \
             nc.psum_tensor("qkv_ps0", [128, 512], F32) as qkv_ps0, \
             nc.psum_tensor("qkv_ps1", [128, 512], F32) as qkv_ps1, \
             nc.psum_tensor("kn_ps", [1, TOK], F32) as kn_ps:
            wbufs = [wbuf0, wbuf1]
            pss = [qkv_ps0, qkv_ps1]
            P1D = [LN_V, pb.at(s_zex), pb.at(s_lnmm), pb.at(s_lnact)]
            if rope:
                pb.dma("sync", cs1_sb[:], cs1_d, wait=P1D, inc=[s_cs])
                pb.dma("sync", cs2_sb[:], cs2_d, inc=[s_cs])
            CS0 = pb.at(s_cs)
            for oc in range(16):
                wb = wbufs[oc % 2]
                w = [pb.at(s_mm)] if oc >= 2 else P1D
                pb.dma("gpsimd", wb[:], wqkv[:, :, oc * 128:(oc + 1) * 128]
                       .rearrange("c p n -> p c n"), wait=w, inc=[s_wb[oc % 2]])
                WB = pb.at(s_wb[oc % 2])
                dest_q = oc < 8
                for tc in range(4):
                    sl = slice(tc * 512, (tc + 1) * 512)
                    pp = pss[tc % 2]
                    for mc in range(FC):
                        pb.emit("tensor", lambda E, wb=wb, mc=mc, sl=sl, pp=pp: E.matmul(
                            out=pp[:], lhsT=wb[:, mc, :], rhs=xn_sb[:, mc, sl],
                            start=(mc == 0), stop=(mc == FC - 1)),
                            wait=([WB, LN_V, pb.at(s_ev), pb.at(s_act)]
                                  if mc == 0 else []),
                            inc=[s_mm] if mc == FC - 1 else [])
                    MM = pb.at(s_mm)
                    dst = q_sb[:, oc, sl] if dest_q else kstage[:, sl]
                    if rope:
                        pb.emit("vector", lambda E, pp=pp, sl=sl: E.tensor_tensor(
                            out=r1[:], in0=pp[0:64, :], in1=cs1_sb[0:64, sl],
                            op=ALU.mult),
                            wait=[MM, CS0, pb.at(s_ev), pb.at(s_kst)], inc=[s_ev])
                        pb.emit("vector", lambda E, pp=pp, sl=sl: E.tensor_tensor(
                            out=r2[:], in0=pp[64:128, :], in1=cs1_sb[64:128, sl],
                            op=ALU.mult), inc=[s_ev])
                        pb.emit("vector", lambda E, pp=pp, sl=sl: E.tensor_tensor(
                            out=r3[:], in0=pp[0:64, :], in1=cs2_sb[0:64, sl],
                            op=ALU.mult), inc=[s_ev])
                        pb.emit("vector", lambda E, pp=pp, sl=sl: E.tensor_tensor(
                            out=r4[:], in0=pp[64:128, :], in1=cs2_sb[64:128, sl],
                            op=ALU.mult), inc=[s_ev])
                        EVR = pb.at(s_ev)
                        pb.emit("vector", lambda E, dst=dst: E.tensor_tensor(
                            out=dst[0:64, :], in0=r1[:], in1=r2[:],
                            op=ALU.subtract), wait=[EVR], inc=[s_ev])
                        pb.emit("vector", lambda E, dst=dst: E.tensor_tensor(
                            out=dst[64:128, :], in0=r3[:], in1=r4[:],
                            op=ALU.subtract), inc=[s_ev])
                    elif qk_bias:
                        pb.emit("vector", lambda E, pp=pp, dst=dst, oc=oc: E.tensor_scalar_add(
                            dst, pp[:], qkb_sb[:, oc:oc + 1]),
                            wait=[MM, IN0, pb.at(s_kst), pb.at(s_ev)], inc=[s_ev])
                    else:
                        pb.emit("vector", lambda E, pp=pp, dst=dst: E.tensor_copy(
                            out=dst, in_=pp[:]),
                            wait=[MM, pb.at(s_kst), pb.at(s_ev)], inc=[s_ev])
                if not dest_q:
                    kc = oc - 8
                    EVK = [pb.at(s_ev), pb.at(s_act)]
                    pb.dma("sync", kv_in[kc * 128:(kc + 1) * 128, :], kstage[:],
                           wait=EVK + [pb.at(s_kst)], inc=[s_kst])
                    pb.emit("scalar", lambda E: E.activation(
                        out=ksq[:], in_=kstage[:], func=AF.Square),
                        wait=EVK + [pb.at(s_mm)], inc=[s_act])
                    KSQ = pb.at(s_act)
                    for hh in range(HPC):
                        hsl = slice(hh * HD, (hh + 1) * HD)
                        head = kc * HPC + hh
                        for t2 in range(4):
                            sl2 = slice(t2 * 512, (t2 + 1) * 512)
                            pb.emit("tensor", lambda E, hsl=hsl, sl2=sl2: E.matmul(
                                out=kn_ps[:, sl2], lhsT=ones_bf[hsl, :],
                                rhs=ksq[hsl, sl2], start=True, stop=True),
                                wait=([KSQ, pb.at(s_ev), CONST] if t2 == 0 else []),
                                inc=[s_mm] if t2 == 3 else [])
                        pb.emit("vector", lambda E, head=head: E.reduce_max(
                            out=kn_sb[:, 2 * head:2 * head + 2],
                            in_=kn_ps[:].rearrange("p (b t) -> p b t", b=2),
                            axis=AX.X),
                            wait=[pb.at(s_mm)], inc=[s_ev])
            pb.emit("gpsimd", lambda E: E.memset(statpad[:], 0.0), inc=[s_gp])
            pb.emit("vector", lambda E: E.tensor_copy(
                out=statpad[0:1, 0:32], in_=kn_sb[:]),
                wait=[pb.at(s_ev), pb.at(s_gp)], inc=[s_ev])
            pb.dma("gpsimd", kv_in[2048:KVROWS, :], statpad[:],
                   wait=[pb.at(s_ev)], inc=[s_st])
        Q_V = pb.at(s_ev)
        Q_A = pb.at(s_act)

        with nc.sbuf_tensor("vbr_sb", [1, DM], F32) as vbr_sb, \
             nc.sbuf_tensor("vbb_sb", [128, DM], F32) as vbb_sb, \
             nc.sbuf_tensor("wv0", [128, FC, 512], BF16) as wv0, \
             nc.sbuf_tensor("wv1", [128, FC, 512], BF16) as wv1, \
             nc.sbuf_tensor("vstage", [128, 512], BF16) as vstage, \
             nc.psum_tensor("v_ps0", [128, 512], F32) as v_ps0, \
             nc.psum_tensor("v_ps1", [128, 512], F32) as v_ps1, \
             nc.psum_tensor("vb_ps", [128, 512], F32) as vb_ps:
            if qk_bias:
                pb.dma("sync", vbr_sb[:], vb_d, inc=[s_in])
                VBR = pb.at(s_in)
                for vc in range(2):
                    sl = slice(vc * 512, (vc + 1) * 512)
                    pb.emit("tensor", lambda E, sl=sl: E.matmul(
                        out=vb_ps[:], lhsT=onesr_f[:], rhs=vbr_sb[:, sl],
                        start=True, stop=True),
                        wait=[VBR, pb.at(s_ev), CONST], inc=[s_mm])
                    pb.emit("vector", lambda E, sl=sl: E.tensor_copy(
                        out=vbb_sb[:, sl], in_=vb_ps[:]),
                        wait=[pb.at(s_mm)], inc=[s_ev])
            wvs = [wv0, wv1]
            vpss = [v_ps0, v_ps1]
            for vc in range(2):
                wb = wvs[vc]
                pb.dma("gpsimd", wb[:],
                       wqkv[:, :, 2 * DM + vc * 512: 2 * DM + (vc + 1) * 512]
                       .rearrange("c p n -> p c n"),
                       wait=[Q_V, Q_A, pb.at(s_mm), pb.at(s_st), pb.at(s_kst)],
                       inc=[s_wv[vc]])
                WB = pb.at(s_wv[vc])
                for tt_ in range(16):
                    pp = vpss[tt_ % 2]
                    for mc in range(FC):
                        pb.emit("tensor", lambda E, wb=wb, mc=mc, tt_=tt_, pp=pp: E.matmul(
                            out=pp[:], lhsT=xn_sb[:, mc, tt_ * 128:(tt_ + 1) * 128],
                            rhs=wb[:, mc, :], start=(mc == 0), stop=(mc == FC - 1)),
                            wait=([WB, LN_V, pb.at(s_ev)] if mc == 0 else []),
                            inc=[s_mm] if mc == FC - 1 else [])
                    MM = pb.at(s_mm)
                    if qk_bias:
                        pb.emit("vector", lambda E, pp=pp, vc=vc: E.tensor_tensor(
                            out=vstage[:], in0=pp[:],
                            in1=vbb_sb[:, vc * 512:(vc + 1) * 512], op=ALU.add),
                            wait=[MM, pb.at(s_vst), pb.at(s_ev)], inc=[s_ev])
                    else:
                        pb.emit("vector", lambda E, pp=pp: E.tensor_copy(
                            out=vstage[:], in_=pp[:]),
                            wait=[MM, pb.at(s_vst), pb.at(s_ev)], inc=[s_ev])
                    dstv = kv_in[1024 + tt_ * 64: 1024 + (tt_ + 1) * 64, :] \
                        .rearrange("r (s d) -> (r s) d", s=2)[:, vc * 512:(vc + 1) * 512]
                    pb.dma("sync", dstv, vstage[:],
                           wait=[pb.at(s_ev), pb.at(s_vst)], inc=[s_vst])

        with nc.sbuf_tensor("qs_sb", [128, 128], BF16) as qs_sb, \
             nc.psum_tensor("qn_ps", [1, TOK], F32) as qn_ps:
            for bjj in range(16):
                b, jj = bjj // 8, bjj % 8
                cols = slice(b * 1024 + jj * 128, b * 1024 + (jj + 1) * 128)
                RED = pb.at(s_ev)
                for oc in range(FC):
                    pb.emit("scalar", lambda E, oc=oc, cols=cols: E.activation(
                        out=qs_sb[:], in_=q_sb[:, oc, cols], func=AF.Square),
                        wait=[Q_V, pb.at(s_mm), RED], inc=[s_act])
                    QSQ = pb.at(s_act)
                    for hh in range(HPC):
                        hsl = slice(hh * HD, (hh + 1) * HD)
                        head = oc * HPC + hh
                        pb.emit("tensor", lambda E, hsl=hsl, head=head: E.matmul(
                            out=qn_ps[:, head * 128:(head + 1) * 128],
                            lhsT=ones_bf[hsl, :], rhs=qs_sb[hsl, :],
                            start=True, stop=True),
                            wait=[QSQ, RED, CONST], inc=[s_mm])
                pb.emit("vector", lambda E, bjj=bjj: E.reduce_max(
                    out=qmn[:, bjj * 16: bjj * 16 + H],
                    in_=qn_ps[:, 0:H * 128].rearrange("p (h t) -> p h t", h=H),
                    axis=AX.X),
                    wait=[pb.at(s_mm)], inc=[s_ev])
            pb.emit("scalar", lambda E: E.activation(
                out=qmn[:], in_=qmn[:], func=AF.Sqrt),
                wait=[pb.at(s_ev)], inc=[s_act])
        QN_A = pb.at(s_act)
        phx.close()

        pb.emit("gpsimd", lambda E: E.collective_compute(
            "AllGather", ALU.bypass, replica_groups=[list(range(NCORES))],
            ins=[kv_in.opt()], outs=[kv_all.opt()]),
            wait=[pb.at(s_st), pb.at(s_kst), pb.at(s_vst), pb.at(s_gp)],
            inc=[s_ag], chain=False)
        AGD = pb.at(s_ag)

        if "dbg_kv" in dbg_t:
            with nc.sbuf_tensor("dkv", [128, TOK], F32) as dkv:
                for rr in range(16):
                    pb.dma("gpsimd", dkv[:], kv_in[rr * 128:(rr + 1) * 128, :],
                           wait=[AGD, pb.at(s_out)], inc=[s_gpd])
                    pb.dma("sync", dbg_t["dbg_kv"][rr * 128:(rr + 1) * 128, :],
                           dkv[:], wait=[pb.at(s_gpd)], inc=[s_out])
                pb.dma("gpsimd", dkv[0:1, 0:32], kv_in[2048:2049, 0:32],
                       wait=[pb.at(s_out)], inc=[s_gpd])
                pb.dma("sync", dbg_t["dbg_kv"][2048:2049, 0:32], dkv[0:1, 0:32],
                       wait=[pb.at(s_gpd)], inc=[s_out])
        if "dbg_q" in dbg_t:
            for fc in range(FC):
                pb.dma("gpsimd", dbg_t["dbg_q"][fc * 128:(fc + 1) * 128, :],
                       q_sb[:, fc, :], wait=[Q_V, Q_A], inc=[s_out])

        with nc.sbuf_tensor("kst", [1, NCORES, 32], F32) as kst, \
             nc.sbuf_tensor("crow", [1, 16], F32) as crow, \
             nc.psum_tensor("c_ps", [128, 16], F32) as c_ps:
            pb.dma("gpsimd", kst[:],
                   kv_all_r[:, 2048:2049, 0:32].rearrange("r k s -> k r s"),
                   wait=[AGD], inc=[s_gpd])
            pb.emit("vector", lambda E: E.reduce_max(
                out=kmaxrow[:], in_=kst[:].rearrange("p r s -> p s r"),
                axis=AX.X),
                wait=[pb.at(s_gpd)], inc=[s_ev])
            pb.emit("scalar", lambda E: E.activation(
                out=kmaxrow[:], in_=kmaxrow[:], func=AF.Sqrt),
                wait=[pb.at(s_ev)], inc=[s_act])
            KMX = pb.at(s_act)
            kmx_r = kmaxrow[:].rearrange("p (h b) -> p b h", b=2)
            for bjj in range(16):
                b = bjj // 8
                pb.emit("vector", lambda E, bjj=bjj, b=b: E.tensor_tensor(
                    out=crow[:, 0:H], in0=qmn[:, bjj * 16:bjj * 16 + H],
                    in1=kmx_r[:, b, 0:H], op=ALU.mult),
                    wait=[KMX, QN_A, pb.at(s_mm)], inc=[s_ev])
                pb.emit("vector", lambda E: E.tensor_scalar_mul(
                    crow[:, 0:H], crow[:, 0:H], scale),
                    wait=[pb.at(s_ev)], inc=[s_ev])
                CRW = pb.at(s_ev)
                pb.emit("tensor", lambda E: E.matmul(
                    out=c_ps[:, 0:H], lhsT=nonesr_f[:], rhs=crow[:, 0:H],
                    start=True, stop=True), wait=[CRW, CONST], inc=[s_mm])
                pb.emit("vector", lambda E, bjj=bjj: E.tensor_copy(
                    out=c_all[:, bjj, 0:H], in_=c_ps[:, 0:H]),
                    wait=[pb.at(s_mm)], inc=[s_ev])
        C_V = pb.at(s_ev)
        if "dbg_c" in dbg_t:
            pb.dma("sync", dbg_t["dbg_c"], c_all[:], wait=[C_V], inc=[s_out])

        base_d = zout if two else xin
        ph5 = ExitStack()
        maskt_sb = ph5.enter_context(nc.sbuf_tensor("maskt_sb", [128, 8, 128], BF16))
        wo_sb = ph5.enter_context(nc.sbuf_tensor("wo_sb", [128, FC, DM], BF16))
        rl_sb = ph5.enter_context(nc.sbuf_tensor("rl_sb", [128, 1024], F32))
        otn = ph5.enter_context(nc.sbuf_tensor("otn", [128, FC, 512], BF16))
        bas0 = ph5.enter_context(nc.sbuf_tensor("bas0", [128, 512], F32))
        bas1 = ph5.enter_context(nc.sbuf_tensor("bas1", [128, 512], F32))
        ktile0 = ph5.enter_context(nc.sbuf_tensor("ktile0", [128, FC, 1024], BF16))
        ktile1 = ph5.enter_context(nc.sbuf_tensor("ktile1", [128, FC, 1024], BF16))
        vtile0 = ph5.enter_context(nc.sbuf_tensor("vtile0", [128, 8, 1024], BF16))
        vtile1 = ph5.enter_context(nc.sbuf_tensor("vtile1", [128, 8, 1024], BF16))
        pexp = [ph5.enter_context(nc.sbuf_tensor("pexp%d" % i, [128, 1024], BF16))
                for i in range(4)]
        allones = ph5.enter_context(nc.sbuf_tensor("allones", [128, 128], BF16))
        mhalf1 = ph5.enter_context(nc.sbuf_tensor("mhalf1", [128, 128], BF16))
        mhalf2 = ph5.enter_context(nc.sbuf_tensor("mhalf2", [128, 128], BF16))
        o_sb = ph5.enter_context(nc.sbuf_tensor("o_sb", [128, NSUB, 2048], F32))
        st_ps0 = ph5.enter_context(nc.psum_tensor("st_ps0", [128, 1024], F32))
        st_ps1 = ph5.enter_context(nc.psum_tensor("st_ps1", [128, 1024], F32))
        ot_ps = ph5.enter_context(nc.psum_tensor("ot_ps", [128, 2048], F32))
        if True:
            stpss = [st_ps0, st_ps1]
            op_ps = st_ps0[:, 0:512]
            P2D = [AGD, QN_A, pb.at(s_mm), pb.at(s_ev)]
            pb.dma("sync", maskt_sb[:], maskd, wait=P2D, inc=[s_mk])
            pb.dma("gpsimd", wo_sb[:], wo_d.rearrange("c p n -> p c n"),
                   wait=P2D, inc=[s_w])
            pb.emit("gpsimd", lambda E: E.memset(allones[:], 1.0))
            pb.emit("gpsimd", lambda E: E.memset(mhalf1[:], 1.0))
            pb.emit("gpsimd", lambda E: E.affine_select(
                out=mhalf1[:], in_=mhalf1[:], pattern=[[-1, 128]],
                compare_op=ALU.is_ge, fill=0.0, base=63, channel_multiplier=0))
            pb.emit("gpsimd", lambda E: E.memset(mhalf2[:], 1.0))
            pb.emit("gpsimd", lambda E: E.affine_select(
                out=mhalf2[:], in_=mhalf2[:], pattern=[[1, 128]],
                compare_op=ALU.is_ge, fill=0.0, base=-64, channel_multiplier=0),
                inc=[s_gp])
            IN5 = pb.at(s_mk)
            WO_L = pb.at(s_w)
            CONST5 = pb.at(s_gp)
            ktiles = [ktile0, ktile1]
            vtiles = [vtile0, vtile1]
            FL_tile_done = {}
            exp_war = [CONST5] * 4
            st_war = [CONST5, CONST5]
            ablate = cfg.get("ablate", "")
            merge_prev = pb.at(s_fe)
            fin_done = pb.at(s_fe)
            opd_prev = pb.at(s_fe)
            if ablate == "exp":
                for i in range(4):
                    pb.emit("gpsimd", lambda E, i=i: E.memset(pexp[i][:], 0.001),
                            inc=[s_gp] if i == 3 else [])
            for b in range(B):
                pb.emit("gpsimd", lambda E: E.memset(o_sb[:], 0.0),
                        wait=[fin_done] if b else [AGD], inc=[s_gp])
                ZRO = pb.at(s_gp)
                for j in range(NSUB):
                    par = j % 2
                    kt = ktiles[par]
                    vt = vtiles[par]
                    key = (b, j - 2) if j >= 2 else ((b - 1, j + NSUB - 2) if b else None)
                    wkt = [FL_tile_done[key]] if key in FL_tile_done else [AGD]
                    bc0 = b * 1024 + j * 128
                    for fc in range(FC):
                        pb.dma("sync",
                               kt[:, fc, :].rearrange("p (r t) -> p r t", r=NCORES),
                               kv_all_r[:, fc * 128:(fc + 1) * 128, bc0:bc0 + 128]
                               .rearrange("r k t -> k r t"),
                               wait=wkt if fc == 0 else [], inc=[s_ktp[par]])
                    vrow = 1024 + bc0 // 2
                    for u in range(NCORES):
                        pb.dma("sync", vt[:, u, :],
                               kv_all_r[u, vrow:vrow + 64, :]
                               .rearrange("k (s d) -> (k s) d", s=2),
                               wait=wkt if u == 0 else [], inc=[s_ktp[par]])
                    KT = pb.at(s_ktp[par])
                    for jj in (() if ablate == "flash" else range(j, NSUB)):
                        bjj = b * 8 + jj
                        qcols = slice(b * 1024 + jj * 128, b * 1024 + (jj + 1) * 128)
                        diag = (jj == j) and ablate != "mask"
                        first_ot = True
                        for dc in range(FC):
                            for hh in range(HPC):
                                h = dc * HPC + hh
                                hsl = slice(hh * HD, (hh + 1) * HD)
                                sp = stpss[h % 2]
                                px = pexp[h % 4]
                                for u in range(8):
                                    last = (u == 7) and not diag
                                    pb.emit("tensor", lambda E, sp=sp, kt=kt, u=u,
                                            oc=dc, hsl=hsl, qcols=qcols,
                                            diag=diag: E.matmul(
                                        out=sp[:, u * 128:(u + 1) * 128],
                                        lhsT=kt[hsl, oc, u * 128:(u + 1) * 128],
                                        rhs=q_sb[hsl, oc, qcols],
                                        start=True, stop=not diag),
                                        wait=([KT, Q_V, Q_A, st_war[h % 2]]
                                              if u == 0 else []),
                                        inc=[s_fl] if last else [])
                                    if diag:
                                        pb.emit("tensor", lambda E, sp=sp, u=u: E.matmul(
                                            out=sp[:, u * 128:(u + 1) * 128],
                                            lhsT=ident[:],
                                            rhs=maskt_sb[:, u, :],
                                            start=False, stop=True),
                                            wait=[IN5, CONST] if u == 0 else [],
                                            inc=[s_fl] if u == 7 else [])
                                SMM = pb.at(s_fl)
                                if ablate == "exp":
                                    pb.emit("scalar", lambda E, px=px: E.activation(
                                        out=px[:, 0:1], in_=c_all[:, 0, 0:1],
                                        func=AF.Exp),
                                        wait=[SMM, C_V, exp_war[h % 4]], inc=[s_fa])
                                else:
                                    pb.emit("scalar", lambda E, sp=sp, px=px, bjj=bjj,
                                            h=h: E.activation(
                                        out=px[:], in_=sp[:], func=AF.Exp, scale=scale,
                                        bias=c_all[:, bjj, h:h + 1]),
                                        wait=[SMM, C_V, exp_war[h % 4]],
                                        inc=[s_fa])
                                st_war[h % 2] = pb.at(s_fa)
                            EXPD = pb.at(s_fa)
                            for hh in (() if ablate == "ot" else range(HPC)):
                                h = dc * HPC + hh
                                px = pexp[h % 4]
                                osl = slice(hh * HD, (hh + 1) * HD) if HPC == 2 \
                                    else slice(0, 128)
                                for u in range(8):
                                    pb.emit("tensor", lambda E, vt=vt, u=u, dc=dc,
                                            px=px, osl=osl, hh=hh: E.matmul(
                                        out=ot_ps[osl, dc * 128:(dc + 1) * 128],
                                        lhsT=vt[:, u, dc * 128 + osl.start:
                                                dc * 128 + osl.stop],
                                        rhs=px[:, u * 128:(u + 1) * 128],
                                        start=(u == 0), stop=(u == 7)),
                                        wait=([EXPD, KT] +
                                              ([merge_prev] if first_ot else []))
                                        if u == 0 else [])
                                    first_ot = False
                            for hh in (() if ablate == "l" else range(HPC)):
                                h = dc * HPC + hh
                                px = pexp[h % 4]
                                lsel = allones if HPC == 1 else \
                                    (mhalf1 if hh == 0 else mhalf2)
                                for u in range(8):
                                    last = (hh == HPC - 1 and u == 7)
                                    pb.emit("tensor", lambda E, px=px, u=u, dc=dc,
                                            lsel=lsel, hh=hh: E.matmul(
                                        out=ot_ps[:, (FC + dc) * 128:
                                                  (FC + dc + 1) * 128],
                                        lhsT=lsel[:], rhs=px[:, u * 128:(u + 1) * 128],
                                        start=(hh == 0 and u == 0),
                                        stop=(hh == HPC - 1 and u == 7)),
                                        wait=[CONST5] if (hh == 0 and u == 0) else [],
                                        inc=[s_fl] if last else [])
                            exp_war[(dc * HPC) % 4] = pb.at(s_fl)
                            if HPC == 2:
                                exp_war[(dc * HPC + 1) % 4] = pb.at(s_fl)
                        OTD = pb.at(s_fl)
                        pb.emit("vector", lambda E, jj=jj: E.tensor_tensor(
                            out=o_sb[:, jj, :], in0=o_sb[:, jj, :], in1=ot_ps[:],
                            op=ALU.add),
                            wait=[OTD, ZRO, pb.at(s_fe)], inc=[s_fe])
                        merge_prev = pb.at(s_fe)
                    FL_tile_done[(b, j)] = pb.at(s_fe)
                FLB = pb.at(s_fe)
                if "dbg_o" in dbg_t:
                    pb.dma("sync", dbg_t["dbg_o"][:, b * 8:(b + 1) * 8, :],
                           o_sb[:, :, 0:1024], wait=[FLB], inc=[s_out])
                if "dbg_l" in dbg_t:
                    pb.dma("sync", dbg_t["dbg_l"][:, b * 8:(b + 1) * 8, :],
                           o_sb[:, :, 1024:1024 + 16].rearrange("p s t -> p s t")
                           if False else o_sb[:, :, 1024:1040], wait=[FLB],
                           inc=[s_out])
                resbufs = [bas0, bas1]
                for g in range(2):
                    bcols = slice(b * 1024 + g * 512, b * 1024 + (g + 1) * 512)
                    for jji in range(4):
                        jj = g * 4 + jji
                        if ablate in ("l", "rl"):
                            pb.emit("gpsimd", lambda E: E.memset(rl_sb[:], 1.0),
                                    wait=[FLB], inc=[s_fa])
                        else:
                            pb.emit("scalar", lambda E, jj=jj: E.activation(
                                out=rl_sb[:], in_=o_sb[:, jj, 1024:2048],
                                func=AF.Ln),
                                wait=[FLB, pb.at(s_fa), pb.at(s_fe), opd_prev])
                            pb.emit("scalar", lambda E: E.activation(
                                out=rl_sb[:], in_=rl_sb[:], func=AF.Exp,
                                scale=-1.0), inc=[s_fa])
                        pb.emit("vector", lambda E, jj=jj, jji=jji: E.tensor_tensor(
                            out=otn[:, :, jji * 128:(jji + 1) * 128],
                            in0=o_sb[:, jj, 0:1024]
                            .rearrange("p (c n) -> p c n", c=FC),
                            in1=rl_sb[:].rearrange("p (c n) -> p c n", c=FC),
                            op=ALU.mult),
                            wait=[pb.at(s_fa), pb.at(s_fe)], inc=[s_fe])
                    OTN = pb.at(s_fe)
                    for mc in range(FC):
                        bas = resbufs[mc % 2]
                        pb.dma("sync", bas[:], base_d[mc * 128:(mc + 1) * 128, bcols],
                               wait=[pb.at(s_fe), pb.at(s_so[mc % 2])],
                               inc=[s_bas[mc % 2]])
                        BAS = pb.at(s_bas[mc % 2])
                        for dc in range(FC):
                            pb.emit("tensor", lambda E, dc=dc, mc=mc: E.matmul(
                                out=op_ps, lhsT=wo_sb[:, dc, mc * 128:(mc + 1) * 128],
                                rhs=otn[:, dc, :], start=(dc == 0), stop=(dc == FC - 1)),
                                wait=([OTN, WO_L, pb.at(s_fe), st_war[0]]
                                      if dc == 0 else []),
                                inc=[s_fl] if dc == FC - 1 else [])
                        OPD = pb.at(s_fl)
                        if o_bias:
                            pb.emit("vector", lambda E, bas=bas, mc=mc: E.scalar_tensor_tensor(
                                out=bas[:], in0=op_ps, scalar=ob_sb[:, mc:mc + 1],
                                in1=bas[:], op0=ALU.add, op1=ALU.add),
                                wait=[OPD, BAS, IN0], inc=[s_fe])
                        else:
                            pb.emit("vector", lambda E, bas=bas: E.tensor_tensor(
                                out=bas[:], in0=bas[:], in1=op_ps, op=ALU.add),
                                wait=[OPD, BAS], inc=[s_fe])
                        pb.dma("sync", xout[mc * 128:(mc + 1) * 128, bcols], bas[:],
                               wait=[pb.at(s_fe)], inc=[s_so[mc % 2]])
                    opd_prev = pb.at(s_fe)
                    st_war[0] = pb.at(s_fe)
                fin_done = pb.at(s_fe)
        ph5.close()
        fin_waits = [pb.at(s_out), pb.at(s_st), pb.at(s_so[0]), pb.at(s_so[1]),
                     pb.at(s_zex)]
        pb.emit("sync", lambda E: E.nop(), wait=fin_waits)
        if fuse is None:
            pb.realize()
    return nc._nc



def prep_attn_inputs(cfg, weights, x_parts, pfx=None):
    H, HD = cfg["H"], cfg["HD"]
    rope = cfg["rope"]
    perm = qk_perm(HD, rope) if rope else np.arange(DM)
    wq = np.asarray(weights["wq"], dtype=np.float32)
    wk = np.asarray(weights["wk"], dtype=np.float32)
    wv = np.asarray(weights["wv"], dtype=np.float32)
    wo = np.asarray(weights["wo"], dtype=np.float32)
    wqkv = np.concatenate([wq[:, perm], wk[:, perm], wv], axis=1)
    wqkv_c = chunked_w(wqkv)
    wo_c = chunked_w(wo)
    in_maps = []
    for core in range(NCORES):
        m = {"wqkv": wqkv_c, "wo": wo_c, "maskin": make_causal_mask(core)}
        if pfx is None:
            m["xin"] = x_parts[core]
        if rope:
            cs1, cs2 = rope_tables(core, HD, rope)
            m["cossin1"] = cs1
            m["cossin2"] = cs2
        if cfg["qk_bias"]:
            bq = np.asarray(weights["bqkv"], dtype=np.float32)
            qkbias = np.concatenate([bq[0:DM][perm], bq[DM:2 * DM][perm]])
            m["qkbias"] = colsplit(qkbias)
            m["vbias"] = np.ascontiguousarray(bq[2 * DM:].reshape(1, DM))
        if cfg["o_bias"]:
            m["obias"] = colsplit(np.asarray(weights["bo"], dtype=np.float32))
        if cfg["two_norms"]:
            m["gain"] = colsplit(np.asarray(weights["gain"], dtype=np.float32))
        if pfx:
            m = {pfx + k: v for k, v in m.items()}
        in_maps.append(m)
    return in_maps





def build_moe(cfg, fuse=None):
    kind = cfg["kind"]
    DFF = cfg["dff"]
    NE = 8
    CAP = 1280.0
    SPC = cfg.get("sparse")
    BIGI = 5.0e6
    act_f = AF.Gelu if cfg["act"] == "gelu" else AF.Relu
    biases = cfg["biases"]
    rms = cfg["rms"]
    xres_f = cfg["xres"]
    NFB = DFF // 1024
    FBC = 8

    if fuse is None:
        nc = NCP(bass.Bass("TRN2", target_bir_lowering=False), "")
        xin = nc.dram_tensor("xin", [DM, TOK], F32, kind="ExternalInput").ap()
    else:
        nc = NCP(fuse["nc"], fuse["pfx"])
        xin = fuse["xin"]
    w1_d = nc.dram_tensor("w1", [NE, FC, 128, DFF], F32, kind="ExternalInput").ap()
    w2_d = nc.dram_tensor("w2", [NE, DFF // 128, 128, DM], F32,
                          kind="ExternalInput").ap()
    wr_d = nc.dram_tensor("wr", [FC, 128, NE], F32, kind="ExternalInput").ap()
    if fuse is None:
        xout = nc.dram_tensor("xout", [DM, TOK], F32, kind="ExternalOutput").ap()
    else:
        xout = fuse["xout"]
    if xres_f:
        if fuse is None:
            xres = nc.dram_tensor("xres", [DM, TOK], F32,
                                  kind="ExternalInput").ap()
        else:
            xres = fuse["xres"]
    if biases:
        b1_d = nc.dram_tensor("b1", [NE, 128, DFF // 128], F32,
                              kind="ExternalInput").ap()
        b2_d = nc.dram_tensor("b2", [NE, 128, FC], F32, kind="ExternalInput").ap()
    if rms:
        gain_d = nc.dram_tensor("gain", [128, FC], F32, kind="ExternalInput").ap()
    if kind == "cap2":
        g_loc = nc.dram_tensor("g_loc", [2 * TOK, NE], F32, kind="Internal").ap()
        g_all = nc.dram_tensor("g_all", [NCORES * 2 * TOK, NE], F32,
                               kind="Internal").ap()
    wrow_d = nc.dram_tensor("wrow_d", [NE, TOK], F32, kind="Internal").ap()
    if SPC:
        iota_d = nc.dram_tensor("iota", [128, 1], F32, kind="ExternalInput").ap()
        xn_tok_d = nc.dram_tensor("xn_tok", [TOK, DM], BF16, kind="Internal").ap()
        idx_d = nc.dram_tensor("idx_d", [NE * SPC, 2], F32, kind="Internal").ap()
        y_tok_d = nc.dram_tensor("y_tok", [TOK, DM], F32, kind="Internal").ap()

    dbg = set(cfg.get("debug", ()))
    dbg_t = {}
    def dbg_mk(name, shape):
        dbg_t[name] = nc.dram_tensor(name, shape, F32, kind="ExternalOutput").ap()
    if "w" in dbg: dbg_mk("dbg_w", [128, 16, NE])
    if "th" in dbg: dbg_mk("dbg_th", [1, NE])

    ctx = ExitStack()
    with ctx:
        if fuse is None:
            pb = PB(nc, ctx)
            pb.init_chains()
        else:
            pb = fuse["pb"]
        sbt = lambda n, s, d: ctx.enter_context(nc.sbuf_tensor(n, s, d))

        xn_sb = sbt("xn_sb", [128, FC, TOK], BF16)
        y_acc = sbt("y_acc", [128, FC, TOK], F32) if not SPC else None
        wt_sb = sbt("wt_sb", [128, 16, NE], F32)
        g2w_sb = sbt("g2w_sb", [128, 16, NE], F32)
        wr_sb = sbt("wr_sb", [128, FC, NE], F32)
        rr_sb = sbt("rr_sb", [1, TOK], F32)
        lg_sb = sbt("lg_sb", [8, TOK], F32)
        ones_f = sbt("ones_f", [128, 1], F32)
        onesr_f = sbt("onesr_f", [1, 128], F32)
        ident_f = sbt("ident_f", [128, 128], F32)
        eps_c = sbt("eps_c", [1, 1], F32)
        grow_sb = sbt("grow_sb", [1, TOK], F32)
        if SPC:
            ident_b = sbt("ident_b", [128, 128], BF16)
            ltri = sbt("ltri", [128, 128], F32)
            eoff_sb = sbt("eoff_sb", [128, NE], F32)
            iota_sb = sbt("iota_sb", [128, 1], F32)
        if biases:
            b1_sb = sbt("b1_sb", [128, NE, DFF // 128], F32)
            b2_sb = sbt("b2_sb", [128, NE, FC], F32)
        if rms:
            g_sb = sbt("g_sb", [128, FC], F32)

        s_in = pb.sem("s_in")
        s_gp = pb.sem("s_gp")
        s_ln = pb.sem("s_ln")
        s_lnmm = pb.sem("s_lnmm")
        s_lnact = pb.sem("s_lnact")
        s_xt = [pb.sem("s_xt0"), pb.sem("s_xt1")]
        s_mm = pb.sem("s_mm")
        s_ev = pb.sem("s_ev")
        s_act = pb.sem("s_act")
        s_wu = [pb.sem("s_wu0"), pb.sem("s_wu1")]
        s_wd2 = [pb.sem("s_wd0"), pb.sem("s_wd1")]
        s_ag = pb.sem("s_ag")
        s_gd = pb.sem("s_gd")
        s_wr = pb.sem("s_wr")
        s_bas = [pb.sem("s_bas0"), pb.sem("s_bas1")]
        s_so = [pb.sem("s_so0"), pb.sem("s_so1")]
        s_out = pb.sem("s_out")

        pb.emit("gpsimd", lambda E: E.memset(ones_f[:], 1.0))
        pb.emit("gpsimd", lambda E: E.memset(onesr_f[:], 1.0))
        pb.emit("gpsimd", lambda E: E.memset(
            eps_c[:], 1e-6 if rms else 1e-5))
        pb.emit("gpsimd", lambda E: E.memset(ident_f[:], 1.0))
        pb.emit("gpsimd", lambda E: E.affine_select(
            out=ident_f[:], in_=ident_f[:], pattern=[[-1, 128]],
            compare_op=ALU.is_equal, fill=0.0, base=0, channel_multiplier=1))
        if not SPC:
            pb.emit("gpsimd", lambda E: E.memset(y_acc[:], 0.0))
        if SPC:
            pb.emit("gpsimd", lambda E: E.memset(ident_b[:], 1.0))
            pb.emit("gpsimd", lambda E: E.affine_select(
                out=ident_b[:], in_=ident_b[:], pattern=[[-1, 128]],
                compare_op=ALU.is_equal, fill=0.0, base=0, channel_multiplier=1))
            pb.emit("gpsimd", lambda E: E.memset(ltri[:], 1.0))
            pb.emit("gpsimd", lambda E: E.affine_select(
                out=ltri[:], in_=ltri[:], pattern=[[1, 128]],
                compare_op=ALU.is_ge, fill=0.0, base=-1, channel_multiplier=-1))
            for e in range(NE):
                pb.emit("gpsimd", lambda E, e=e: E.memset(
                    eoff_sb[:, e:e + 1], float(e * SPC)))
        pb.emit("gpsimd", lambda E: E.memset(
            eps_c[:], 1e-6 if rms else 1e-5), inc=[s_gp])
        CONST = pb.at(s_gp)
        if SPC:
            pb.dma("sync", iota_sb[:], iota_d, inc=[s_in])
        pb.dma("sync", wr_sb[:], wr_d.rearrange("c p n -> p c n"), inc=[s_in])
        if biases:
            pb.dma("sync", b1_sb[:], b1_d.rearrange("e p n -> p e n"), inc=[s_in])
            pb.dma("sync", b2_sb[:], b2_d.rearrange("e p n -> p e n"), inc=[s_in])
        if rms:
            pb.dma("sync", g_sb[:], gain_d, inc=[s_in])
        IN0 = pb.at(s_in)
        if rms:
            for mc in range(FC):
                pb.emit("vector", lambda E, mc=mc: E.tensor_scalar_mul(
                    wr_sb[:, mc, :], wr_sb[:, mc, :], g_sb[:, mc:mc + 1]),
                    wait=[IN0], inc=[s_ev])
        WRG = pb.at(s_ev)

        with nc.sbuf_tensor("xs0", [128, 512], F32) as xs0, \
             nc.sbuf_tensor("xs1", [128, 512], F32) as xs1, \
             nc.sbuf_tensor("xq0", [128, 512], F32) as xq0, \
             nc.sbuf_tensor("xq1", [128, 512], F32) as xq1, \
             nc.sbuf_tensor("srow", [1, TOK], F32) as srow, \
             nc.sbuf_tensor("mrow", [1, TOK], F32) as mrow, \
             nc.sbuf_tensor("mb", [128, 512], F32) as mb, \
             nc.sbuf_tensor("rb", [128, 512], F32) as rb:
            xss = [xs0, xs1]
            xqs = [xq0, xq1]
            with nc.psum_tensor("mean_ps", [1, TOK], F32) as mean_ps, \
                 nc.psum_tensor("sq_ps", [1, TOK], F32) as sq_ps:
                cons_mm = {}
                cons_act = {}
                for idx in range(32):
                    fc, tc = idx // 4, idx % 4
                    tl = xss[idx % 2]
                    xq = xqs[idx % 2]
                    if idx >= 2:
                        w = [(s_lnmm, cons_mm[idx - 2]),
                             (s_lnact, cons_act[idx - 2])]
                    else:
                        w = []
                    pb.dma("sync", tl[:],
                           xin[fc * 128:(fc + 1) * 128, tc * 512:(tc + 1) * 512],
                           wait=w, inc=[s_xt[idx % 2]])
                    XL = pb.at(s_xt[idx % 2])
                    if not rms:
                        pb.emit("tensor", lambda E, tl=tl, fc=fc, tc=tc: E.matmul(
                            out=mean_ps[:, tc * 512:(tc + 1) * 512],
                            lhsT=ones_f[:], rhs=tl[:],
                            start=(fc == 0), stop=(fc == FC - 1)),
                            wait=[XL, CONST], inc=[s_lnmm])
                    pb.emit("scalar", lambda E, tl=tl, xq=xq: E.activation(
                        out=xq[:], in_=tl[:], func=AF.Square),
                        wait=[XL] + ([(s_lnmm, cons_mm[idx - 2])] if idx >= 2 else []),
                        inc=[s_lnact])
                    pb.emit("tensor", lambda E, xq=xq, fc=fc, tc=tc: E.matmul(
                        out=sq_ps[:, tc * 512:(tc + 1) * 512],
                        lhsT=ones_f[:], rhs=xq[:],
                        start=(fc == 0), stop=(fc == FC - 1)),
                        wait=[pb.at(s_lnact), CONST], inc=[s_lnmm])
                    cons_mm[idx] = s_lnmm.n
                    cons_act[idx] = s_lnact.n
                if not rms:
                    pb.emit("vector", lambda E: E.tensor_scalar_mul(
                        mrow[:], mean_ps[:], 1.0 / DM),
                        wait=[pb.at(s_lnmm)], inc=[s_ln])
                    pb.emit("vector", lambda E: E.tensor_tensor(
                        out=srow[:], in0=mrow[:], in1=mrow[:], op=ALU.mult),
                        wait=[pb.at(s_ln)], inc=[s_ln])
                    pb.emit("vector", lambda E: E.scalar_tensor_tensor(
                        out=srow[:], in0=sq_ps[:], scalar=1.0 / DM, in1=srow[:],
                        op0=ALU.mult, op1=ALU.subtract),
                        wait=[pb.at(s_ln)], inc=[s_ln])
                else:
                    pb.emit("vector", lambda E: E.tensor_scalar_mul(
                        srow[:], sq_ps[:], 1.0 / DM),
                        wait=[pb.at(s_lnmm)], inc=[s_ln])
                pb.emit("scalar", lambda E: E.activation(
                    out=rr_sb[:], in_=srow[:], func=AF.Sqrt, bias=eps_c[:]),
                    wait=[pb.at(s_ln), CONST], inc=[s_lnact])
                pb.emit("scalar", lambda E: E.activation(
                    out=rr_sb[:], in_=rr_sb[:], func=AF.Ln),
                    wait=[pb.at(s_lnact)])
                pb.emit("scalar", lambda E: E.activation(
                    out=rr_sb[:], in_=rr_sb[:], func=AF.Exp, scale=-1.0),
                    inc=[s_lnact])
                pb.emit("vector", lambda E: E.engine_nop(),
                        wait=[pb.at(s_lnact)], inc=[s_ln])
            with nc.psum_tensor("bc1_ps", [128, 512], F32) as bc1_ps, \
                 nc.psum_tensor("bc2_ps", [128, 512], F32) as bc2_ps, \
                 nc.psum_tensor("lg_ps", [8, TOK], F32) as lg_ps:
                RR = pb.at(s_ln)
                rt_mm = {}
                for tc in range(4):
                    sl = slice(tc * 512, (tc + 1) * 512)
                    pb.emit("tensor", lambda E, sl=sl: E.matmul(
                        out=bc1_ps[:], lhsT=onesr_f[:], rhs=rr_sb[:, sl],
                        start=True, stop=True),
                        wait=[RR, pb.at(s_ln), CONST], inc=[s_lnmm])
                    pb.emit("vector", lambda E: E.tensor_copy(
                        out=rb[:], in_=bc1_ps[:]),
                        wait=[pb.at(s_lnmm), pb.at(s_ln)], inc=[s_ln])
                    if not rms:
                        pb.emit("tensor", lambda E, sl=sl: E.matmul(
                            out=bc2_ps[:], lhsT=onesr_f[:], rhs=mrow[:, sl],
                            start=True, stop=True),
                            wait=[pb.at(s_ln)], inc=[s_lnmm])
                        pb.emit("vector", lambda E: E.tensor_copy(
                            out=mb[:], in_=bc2_ps[:]),
                            wait=[pb.at(s_lnmm)], inc=[s_ln])
                    BC = pb.at(s_ln)
                    for fc in range(FC):
                        idx = tc * 8 + fc
                        tl = xss[idx % 2]
                        xq = xqs[idx % 2]
                        rtw = ([(s_lnmm, rt_mm[idx - 2])] if idx >= 2 else [])
                        pb.dma("sync", tl[:], xin[fc * 128:(fc + 1) * 128, sl],
                               wait=[pb.at(s_ln), pb.at(s_xt[idx % 2])] +
                               (rtw if rms else []),
                               inc=[s_xt[idx % 2]])
                        XL = pb.at(s_xt[idx % 2])
                        if not rms:
                            pb.emit("vector", lambda E, tl=tl, xq=xq: E.tensor_tensor(
                                out=xq[:], in0=tl[:], in1=mb[:], op=ALU.subtract),
                                wait=[XL, BC, pb.at(s_ln)] + rtw, inc=[s_ln])
                            srct = xq
                            SR = pb.at(s_ln)
                        else:
                            srct = tl
                            SR = XL
                        pb.emit("tensor", lambda E, st=srct, fc=fc, sl=sl: E.matmul(
                            out=lg_ps[:, sl], lhsT=wr_sb[:, fc, :], rhs=st[:],
                            start=(fc == 0), stop=(fc == FC - 1)),
                            wait=[SR, WRG, IN0], inc=[s_lnmm])
                        rt_mm[idx] = s_lnmm.n
                        if rms:
                            pb.emit("vector", lambda E, st=srct, xq=xq, fc=fc,
                                    sl=sl: E.tensor_tensor(
                                out=xq[:], in0=st[:], in1=rb[:], op=ALU.mult),
                                wait=[pb.at(s_ln), BC, XL], inc=[s_ln])
                            pb.emit("vector", lambda E, xq=xq, fc=fc, sl=sl:
                                    E.tensor_scalar_mul(
                                xn_sb[:, fc, sl], xq[:], g_sb[:, fc:fc + 1]),
                                wait=[pb.at(s_ln), IN0], inc=[s_ln])
                        else:
                            pb.emit("vector", lambda E, st=srct, fc=fc, sl=sl:
                                    E.tensor_tensor(
                                out=xn_sb[:, fc, sl], in0=st[:], in1=rb[:],
                                op=ALU.mult),
                                wait=[pb.at(s_ln), BC, XL], inc=[s_ln])
                    pb.emit("tensor", lambda E, sl=sl: E.matmul(
                        out=bc2_ps[0:8, :], lhsT=onesr_f[:, 0:8], rhs=rr_sb[:, sl],
                        start=True, stop=True),
                        wait=[pb.at(s_lnmm), pb.at(s_ln)], inc=[s_lnmm])
                    pb.emit("vector", lambda E, sl=sl: E.tensor_copy(
                        out=lg_sb[:, sl], in_=lg_ps[:, sl]),
                        wait=[pb.at(s_lnmm)], inc=[s_ln])
                    pb.emit("vector", lambda E, sl=sl: E.tensor_tensor(
                        out=lg_sb[:, sl], in0=lg_sb[:, sl], in1=bc2_ps[0:8, :],
                        op=ALU.mult),
                        wait=[pb.at(s_ln), pb.at(s_lnmm)], inc=[s_ln])
        LN_V = pb.at(s_ln)

        with nc.sbuf_tensor("lgt", [128, 16, NE], F32) as lgt, \
             nc.sbuf_tensor("mx", [128, 16], F32) as mx, \
             nc.sbuf_tensor("sm", [128, 16], F32) as sm, \
             nc.sbuf_tensor("oh", [128, 16, NE], F32) as oh, \
             nc.sbuf_tensor("pm", [128, 16, NE], F32) as pm, \
             nc.psum_tensor("tp_ps", [128, 16 * NE], F32) as tp_ps:
            for g in range(16):
                pb.emit("tensor", lambda E, g=g: E.transpose(
                    out=tp_ps[:, g * NE:(g + 1) * NE],
                    in_=lg_sb[0:NE, g * 128:(g + 1) * 128],
                    identity=ident_f[0:NE, 0:NE]),
                    wait=[LN_V, CONST] if g == 0 else [],
                    inc=[s_mm] if g == 15 else [])
            pb.emit("vector", lambda E: E.tensor_copy(out=lgt[:], in_=tp_ps[:]
                    .rearrange("p (g e) -> p g e", e=NE)),
                    wait=[pb.at(s_mm)], inc=[s_ev])
            pb.emit("vector", lambda E: E.reduce_max(
                out=mx[:], in_=lgt[:], axis=AX.X),
                wait=[pb.at(s_ev)], inc=[s_ev])
            pb.emit("vector", lambda E: E.tensor_tensor(
                out=lgt[:], in0=lgt[:],
                in1=mx[:].unsqueeze(2).to_broadcast([128, 16, NE]),
                op=ALU.subtract),
                wait=[pb.at(s_ev)], inc=[s_ev])
            pb.emit("scalar", lambda E: E.activation(
                out=lgt[:], in_=lgt[:], func=AF.Exp),
                wait=[pb.at(s_ev)], inc=[s_act])
            pb.emit("vector", lambda E: E.reduce_sum(
                out=sm[:], in_=lgt[:], axis=AX.X),
                wait=[pb.at(s_act)], inc=[s_ev])
            pb.emit("scalar", lambda E: E.activation(
                out=sm[:], in_=sm[:], func=AF.Ln),
                wait=[pb.at(s_ev)])
            pb.emit("scalar", lambda E: E.activation(
                out=sm[:], in_=sm[:], func=AF.Exp, scale=-1.0), inc=[s_act])
            pb.emit("vector", lambda E: E.tensor_tensor(
                out=lgt[:], in0=lgt[:],
                in1=sm[:].unsqueeze(2).to_broadcast([128, 16, NE]), op=ALU.mult),
                wait=[pb.at(s_act), pb.at(s_ev)], inc=[s_ev])
            pb.emit("vector", lambda E: E.reduce_max(
                out=mx[:], in_=lgt[:], axis=AX.X),
                wait=[pb.at(s_ev)], inc=[s_ev])
            pb.emit("vector", lambda E: E.tensor_tensor(
                out=oh[:], in0=lgt[:],
                in1=mx[:].unsqueeze(2).to_broadcast([128, 16, NE]),
                op=ALU.is_ge),
                wait=[pb.at(s_ev)], inc=[s_ev])
            pb.emit("vector", lambda E: E.tensor_tensor(
                out=wt_sb[:], in0=oh[:],
                in1=mx[:].unsqueeze(2).to_broadcast([128, 16, NE]), op=ALU.mult),
                wait=[pb.at(s_ev)], inc=[s_ev])
            if kind == "cap2":
                pb.emit("vector", lambda E: E.tensor_scalar(
                    out=pm[:], in0=oh[:], scalar1=-1e9, scalar2=None,
                    op0=ALU.mult),
                    wait=[pb.at(s_ev)], inc=[s_ev])
                pb.emit("vector", lambda E: E.tensor_tensor(
                    out=pm[:], in0=pm[:], in1=lgt[:], op=ALU.add),
                    wait=[pb.at(s_ev)], inc=[s_ev])
                pb.emit("vector", lambda E: E.reduce_max(
                    out=mx[:], in_=pm[:], axis=AX.X),
                    wait=[pb.at(s_ev)], inc=[s_ev])
                pb.emit("vector", lambda E: E.tensor_tensor(
                    out=g2w_sb[:], in0=pm[:],
                    in1=mx[:].unsqueeze(2).to_broadcast([128, 16, NE]),
                    op=ALU.is_ge),
                    wait=[pb.at(s_ev)], inc=[s_ev])
                pb.emit("vector", lambda E: E.tensor_tensor(
                    out=g2w_sb[:], in0=g2w_sb[:],
                    in1=mx[:].unsqueeze(2).to_broadcast([128, 16, NE]),
                    op=ALU.mult),
                    wait=[pb.at(s_ev)], inc=[s_ev])
                pb.dma("sync", g_loc[0:TOK, :]
                       .rearrange("(g p) e -> p g e", p=128), wt_sb[:],
                       wait=[pb.at(s_ev)], inc=[s_gd])
                pb.dma("sync", g_loc[TOK:2 * TOK, :]
                       .rearrange("(g p) e -> p g e", p=128), g2w_sb[:],
                       wait=[pb.at(s_ev)], inc=[s_gd])
                pb.emit("gpsimd", lambda E: E.collective_compute(
                    "AllGather", ALU.bypass, replica_groups=[list(range(NCORES))],
                    ins=[g_loc.opt()], outs=[g_all.opt()]),
                    wait=[pb.at(s_gd)], inc=[s_ag], chain=False)
        if kind == "cap2":
            AGD = pb.at(s_ag)
            with nc.sbuf_tensor("cnt_sb", [128, 256, NE], F32) as cnt_sb, \
                 nc.sbuf_tensor("ind", [128, 256, NE], BF16) as ind, \
                 nc.sbuf_tensor("red", [128, NE], F32) as red, \
                 nc.sbuf_tensor("lo", [1, NE], F32) as lo, \
                 nc.sbuf_tensor("hi", [1, NE], F32) as hi, \
                 nc.sbuf_tensor("mid", [1, NE], F32) as mid, \
                 nc.sbuf_tensor("prd", [1, NE], F32) as prd, \
                 nc.sbuf_tensor("tv", [1, NE], F32) as tv, \
                 nc.sbuf_tensor("thb", [128, NE], F32) as thb, \
                 nc.sbuf_tensor("cnt1", [1, NE], F32) as cnt1, \
                 nc.psum_tensor("thb_ps", [128, NE], F32) as thb_ps, \
                 nc.psum_tensor("cnt_ps", [1, NE], F32) as cnt_ps:
                pb.dma("sync", cnt_sb[:],
                       g_all.rearrange("(p c) e -> p c e", p=128),
                       wait=[AGD], inc=[s_gd])
                GLD = pb.at(s_gd)
                pb.emit("vector", lambda E: E.memset(lo[:], 0.0), inc=[s_ev])
                pb.emit("vector", lambda E: E.memset(hi[:], 1.0), inc=[s_ev])
                pb.emit("vector", lambda E: E.memset(mid[:], 0.5), inc=[s_ev])
                for it in range(36):
                    pb.emit("tensor", lambda E: E.matmul(
                        out=thb_ps[:], lhsT=onesr_f[:], rhs=mid[:],
                        start=True, stop=True),
                        wait=[pb.at(s_ev), CONST], inc=[s_mm])
                    pb.emit("vector", lambda E: E.tensor_copy(
                        out=thb[:], in_=thb_ps[:]),
                        wait=[pb.at(s_mm)], inc=[s_ev])
                    pb.emit("vector", lambda E: E.tensor_tensor(
                        out=ind[:], in0=cnt_sb[:],
                        in1=thb[:].unsqueeze(1).to_broadcast([128, 256, NE]),
                        op=ALU.is_ge),
                        wait=[pb.at(s_ev), GLD], inc=[s_ev])
                    pb.emit("vector", lambda E: E.reduce_sum(
                        out=red[:], in_=ind[:].rearrange("p c e -> p e c"),
                        axis=AX.X),
                        wait=[pb.at(s_ev)], inc=[s_ev])
                    pb.emit("tensor", lambda E: E.matmul(
                        out=cnt_ps[:], lhsT=ones_f[:], rhs=red[:],
                        start=True, stop=True),
                        wait=[pb.at(s_ev), CONST], inc=[s_mm])
                    pb.emit("vector", lambda E: E.tensor_copy(
                        out=cnt1[:], in_=cnt_ps[:]),
                        wait=[pb.at(s_mm)], inc=[s_ev])
                    pb.emit("vector", lambda E: E.tensor_scalar(
                        out=prd[:], in0=cnt1[:], scalar1=CAP + 0.5, scalar2=None,
                        op0=ALU.is_ge),
                        wait=[pb.at(s_ev)], inc=[s_ev])
                    pb.emit("vector", lambda E: E.tensor_tensor(
                        out=tv[:], in0=mid[:], in1=lo[:], op=ALU.subtract),
                        wait=[pb.at(s_ev)], inc=[s_ev])
                    pb.emit("vector", lambda E: E.tensor_tensor(
                        out=tv[:], in0=tv[:], in1=prd[:], op=ALU.mult),
                        wait=[pb.at(s_ev)], inc=[s_ev])
                    pb.emit("vector", lambda E: E.tensor_tensor(
                        out=lo[:], in0=lo[:], in1=tv[:], op=ALU.add),
                        wait=[pb.at(s_ev)], inc=[s_ev])
                    pb.emit("vector", lambda E: E.tensor_tensor(
                        out=tv[:], in0=hi[:], in1=mid[:], op=ALU.subtract),
                        wait=[pb.at(s_ev)], inc=[s_ev])
                    pb.emit("vector", lambda E: E.tensor_tensor(
                        out=tv[:], in0=tv[:], in1=prd[:], op=ALU.mult),
                        wait=[pb.at(s_ev)], inc=[s_ev])
                    pb.emit("vector", lambda E: E.tensor_tensor(
                        out=hi[:], in0=mid[:], in1=tv[:], op=ALU.add),
                        wait=[pb.at(s_ev)], inc=[s_ev])
                    pb.emit("vector", lambda E: E.tensor_tensor(
                        out=mid[:], in0=lo[:], in1=hi[:], op=ALU.add),
                        wait=[pb.at(s_ev)], inc=[s_ev])
                    pb.emit("vector", lambda E: E.tensor_scalar_mul(
                        mid[:], mid[:], 0.5),
                        wait=[pb.at(s_ev)], inc=[s_ev])
                if "dbg_th" in dbg_t:
                    pb.dma("sync", dbg_t["dbg_th"], hi[:],
                           wait=[pb.at(s_ev)], inc=[s_out])
                pb.emit("tensor", lambda E: E.matmul(
                    out=thb_ps[:], lhsT=onesr_f[:], rhs=hi[:],
                    start=True, stop=True),
                    wait=[pb.at(s_ev), CONST], inc=[s_mm])
                pb.emit("vector", lambda E: E.tensor_copy(
                    out=thb[:], in_=thb_ps[:]),
                    wait=[pb.at(s_mm)], inc=[s_ev])
                with nc.sbuf_tensor("selt", [128, 16, NE], F32) as selt:
                    pb.emit("vector", lambda E: E.tensor_tensor(
                        out=selt[:], in0=wt_sb[:],
                        in1=thb[:].unsqueeze(1).to_broadcast([128, 16, NE]),
                        op=ALU.is_ge),
                        wait=[pb.at(s_ev)], inc=[s_ev])
                    pb.emit("vector", lambda E: E.tensor_tensor(
                        out=wt_sb[:], in0=wt_sb[:], in1=selt[:], op=ALU.mult),
                        wait=[pb.at(s_ev)], inc=[s_ev])
                    pb.emit("vector", lambda E: E.tensor_tensor(
                        out=selt[:], in0=g2w_sb[:],
                        in1=thb[:].unsqueeze(1).to_broadcast([128, 16, NE]),
                        op=ALU.is_ge),
                        wait=[pb.at(s_ev)], inc=[s_ev])
                    pb.emit("vector", lambda E: E.tensor_tensor(
                        out=selt[:], in0=selt[:], in1=g2w_sb[:], op=ALU.mult),
                        wait=[pb.at(s_ev)], inc=[s_ev])
                    pb.emit("vector", lambda E: E.tensor_tensor(
                        out=wt_sb[:], in0=wt_sb[:], in1=selt[:], op=ALU.add),
                        wait=[pb.at(s_ev)], inc=[s_ev])
        W_V = pb.at(s_ev)
        if "dbg_w" in dbg_t:
            pb.dma("sync", dbg_t["dbg_w"], wt_sb[:], wait=[W_V], inc=[s_out])

        if SPC:
            C = SPC
            NSG = C // 128
            IOA = bass.IndirectOffsetOnAxis
            U32 = mybir.dt.uint32
            bregs = {}

            def _mkregs(E, bregs=bregs):
                bregs["idx"] = E.to_reg(NE * SPC - 1)
                bregs["tok"] = E.to_reg(TOK - 1)
            pb.emit("gpsimd", _mkregs, chain=False)
            s_xnt = pb.sem("s_xnt")
            s_ind = pb.sem("s_ind")
            s_pf = pb.sem("s_pf")
            s_id = pb.sem("s_id")
            s_yz = pb.sem("s_yz")
            s_yl = pb.sem("s_yl")
            with nc.sbuf_tensor("xtt", [128, DM], BF16) as xtt, \
                 nc.psum_tensor("xt_ps", [128, 512], BF16) as xt_ps:
                for g in range(16):
                    for half in range(2):
                        for q in range(4):
                            fc = half * 4 + q
                            pb.emit("tensor", lambda E, g=g, q=q, fc=fc: E.transpose(
                                out=xt_ps[:, q * 128:(q + 1) * 128],
                                in_=xn_sb[:, fc, g * 128:(g + 1) * 128],
                                identity=ident_b[:]),
                                wait=([LN_V, CONST, pb.at(s_ev)]
                                      if (g == 0 and fc == 0) else
                                      ([pb.at(s_ev)] if q == 0 else [])),
                                inc=[s_mm] if q == 3 else [])
                        pb.emit("vector", lambda E, half=half: E.tensor_copy(
                            out=xtt[:, half * 512:(half + 1) * 512], in_=xt_ps[:]),
                            wait=[pb.at(s_mm), pb.at(s_xnt)], inc=[s_ev])
                    pb.dma("sync", xn_tok_d[g * 128:(g + 1) * 128, :], xtt[:],
                           wait=[pb.at(s_ev), pb.at(s_xnt)], inc=[s_xnt])
            XNT = pb.at(s_xnt)
            with nc.sbuf_tensor("m_sb", [128, 16, NE], F32) as m_sb, \
                 nc.sbuf_tensor("sA", [128, 16, NE], F32) as sA, \
                 nc.sbuf_tensor("sB", [128, 16, NE], F32) as sB, \
                 nc.sbuf_tensor("r_sb", [128, NE], F32) as r_sb, \
                 nc.sbuf_tensor("c_sb", [128, NE], F32) as c_sb, \
                 nc.sbuf_tensor("sidx", [128, 16, NE], F32) as sidx, \
                 nc.sbuf_tensor("ovf", [128, 16, NE], F32) as ovf, \
                 nc.sbuf_tensor("padt", [128, 32, 2], F32) as padt, \
                 nc.sbuf_tensor("srct", [128, 2], F32) as srct, \
                 nc.sbuf_tensor("sidx_u", [128, 1], U32) as sidx_u, \
                 nc.psum_tensor("c_ps", [128, NE], F32) as c_ps:
                pb.emit("vector", lambda E: E.memset(padt[:], BIGI),
                        wait=[W_V, XNT], inc=[s_ev])
                pb.emit("vector", lambda E: E.memset(padt[:, :, 1], 0.0),
                        wait=[pb.at(s_ev)], inc=[s_ev])
                pb.dma("sync", idx_d.rearrange("(g p) c -> p g c", p=128), padt[:],
                       wait=[pb.at(s_ev)], inc=[s_pf])
                pb.emit("vector", lambda E: E.tensor_scalar(
                    out=m_sb[:], in0=wt_sb[:], scalar1=1e-30, scalar2=None,
                    op0=ALU.is_ge), wait=[W_V, pb.at(s_ev)], inc=[s_ev])
                pb.emit("vector", lambda E: E.reduce_sum(
                    out=r_sb[:], in_=m_sb[:].rearrange("p g e -> p e g"),
                    axis=AX.X), wait=[pb.at(s_ev)], inc=[s_ev])
                pb.emit("tensor", lambda E: E.matmul(
                    out=c_ps[:], lhsT=ltri[:], rhs=r_sb[:], start=True, stop=True),
                    wait=[pb.at(s_ev), CONST], inc=[s_mm])
                pb.emit("vector", lambda E: E.tensor_copy(out=c_sb[:], in_=c_ps[:]),
                        wait=[pb.at(s_mm)], inc=[s_ev])
                pb.emit("vector", lambda E: E.tensor_copy(out=sA[:], in_=m_sb[:]),
                        wait=[pb.at(s_ev)], inc=[s_ev])
                cur, nxt = sA, sB
                for k in (1, 2, 4, 8):
                    pb.emit("vector", lambda E, cur=cur, nxt=nxt, k=k: E.tensor_copy(
                        out=nxt[:, 0:k, :], in_=cur[:, 0:k, :]),
                        wait=[pb.at(s_ev)], inc=[s_ev])
                    pb.emit("vector", lambda E, cur=cur, nxt=nxt, k=k: E.tensor_tensor(
                        out=nxt[:, k:16, :], in0=cur[:, k:16, :],
                        in1=cur[:, 0:16 - k, :], op=ALU.add),
                        wait=[pb.at(s_ev)], inc=[s_ev])
                    cur, nxt = nxt, cur
                pb.emit("vector", lambda E, cur=cur: E.tensor_tensor(
                    out=sidx[:], in0=cur[:], in1=m_sb[:], op=ALU.subtract),
                    wait=[pb.at(s_ev)], inc=[s_ev])
                pb.emit("vector", lambda E: E.tensor_tensor(
                    out=sidx[:], in0=sidx[:],
                    in1=c_sb[:].unsqueeze(1).to_broadcast([128, 16, NE]),
                    op=ALU.add), wait=[pb.at(s_ev)], inc=[s_ev])
                pb.emit("vector", lambda E: E.tensor_scalar(
                    out=ovf[:], in0=sidx[:], scalar1=float(C) - 0.5, scalar2=None,
                    op0=ALU.is_ge), wait=[pb.at(s_ev)], inc=[s_ev])
                pb.emit("vector", lambda E: E.tensor_tensor(
                    out=sidx[:], in0=sidx[:],
                    in1=eoff_sb[:].unsqueeze(1).to_broadcast([128, 16, NE]),
                    op=ALU.add), wait=[pb.at(s_ev), CONST], inc=[s_ev])
                pb.emit("vector", lambda E: E.scalar_tensor_tensor(
                    out=sidx[:], in0=ovf[:], scalar=BIGI, in1=sidx[:],
                    op0=ALU.mult, op1=ALU.add), wait=[pb.at(s_ev)], inc=[s_ev])
                pb.emit("vector", lambda E: E.scalar_tensor_tensor(
                    out=sidx[:], in0=m_sb[:], scalar=-BIGI, in1=sidx[:],
                    op0=ALU.mult, op1=ALU.add), wait=[pb.at(s_ev)], inc=[s_ev])
                pb.emit("vector", lambda E: E.tensor_scalar(
                    out=sidx[:], in0=sidx[:], scalar1=BIGI, scalar2=None,
                    op0=ALU.add), wait=[pb.at(s_ev)], inc=[s_ev])
                for g in range(16):
                    pb.emit("vector", lambda E, g=g: E.tensor_scalar(
                        out=srct[:, 0:1], in0=iota_sb[:], scalar1=float(g * 128),
                        scalar2=None, op0=ALU.add),
                        wait=[pb.at(s_ev), IN0, pb.at(s_pf), pb.at(s_ind)],
                        inc=[s_ev])
                    for e in range(NE):
                        pb.emit("vector", lambda E, g=g, e=e: E.tensor_copy(
                            out=srct[:, 1:2], in_=wt_sb[:, g, e:e + 1]),
                            wait=[pb.at(s_ev), pb.at(s_ind)], inc=[s_ev])
                        pb.emit("vector", lambda E, g=g, e=e: E.tensor_copy(
                            out=sidx_u[:], in_=sidx[:, g, e:e + 1]),
                            wait=[pb.at(s_ev)], inc=[s_ev])
                        pb.emit("gpsimd", lambda E: E.indirect_dma_start(
                            out=idx_d,
                            out_offset=IOA(ap=sidx_u[:], axis=0),
                            in_=srct[:], in_offset=None,
                            bounds_check=bregs["idx"], oob_is_err=False,
                            compute_op=ALU.bypass),
                            wait=[pb.at(s_ev)], inc=[s_ind], inc_amt=16,
                            chain=False)
            IDXD = pb.at(s_ind)
            p4c = ExitStack()
            with p4c:
                sb4 = lambda n, s, d: p4c.enter_context(nc.sbuf_tensor(n, s, d))
                ps4 = lambda n: p4c.enter_context(
                    nc.psum_tensor(n, [128, 512], F32))
                wu0 = sb4("wu0", [128, FC, 128], BF16)
                wu1 = sb4("wu1", [128, FC, 128], BF16)
                wd0 = sb4("wd0", [128, FBC, 128], BF16)
                wd1 = sb4("wd1", [128, FBC, 128], BF16)
                h_sb = sb4("h_sb", [128, FBC, C], BF16)
                ye_acc = sb4("ye_acc", [128, FC, C], F32)
                xg = sb4("xg", [128, DM], BF16)
                xe = sb4("xe", [128, FC, C], BF16)
                idro = sb4("idro", [128, 4, 2], F32)
                idxu = sb4("idxu", [128, 4], U32)
                ysc = sb4("ysc", [128, DM], F32)
                zt = sb4("zt", [128, DM], F32)
                up_ps0 = ps4("up_ps0")
                up_ps1 = ps4("up_ps1")
                dn_ps0 = ps4("dn_ps0")
                dn_ps1 = ps4("dn_ps1")
                tr_ps = p4c.enter_context(
                    nc.psum_tensor("tr_ps", [128, 512], BF16))
                trf_ps = ps4("trf_ps")
                wus = [wu0, wu1]
                wds = [wd0, wd1]
                upss = [up_ps0, up_ps1]
                dnss = [dn_ps0, dn_ps1]
                pb.emit("vector", lambda E: E.memset(zt[:], 0.0),
                        wait=[IDXD, XNT], inc=[s_ev])
                pb.emit("vector", lambda E: E.memset(xg[:], 0.0), inc=[s_ev])
                ZT = pb.at(s_ev)
                for g in range(16):
                    pb.dma("sync", y_tok_d[g * 128:(g + 1) * 128, :], zt[:],
                           wait=[ZT], inc=[s_yz])
                YZ0 = pb.at(s_yz)
                uidx = 0
                didx = 0
                for e in range(NE):
                    pb.dma("sync", idro[:, 0:NSG, :],
                           idx_d[e * SPC:(e + 1) * SPC, :]
                           .rearrange("(s p) c -> p s c", p=128),
                           wait=[IDXD, pb.at(s_ev), pb.at(s_id)], inc=[s_id])
                    IDL = pb.at(s_id)
                    pb.emit("vector", lambda E: E.tensor_copy(
                        out=idxu[:, 0:NSG], in_=idro[:, 0:NSG, 0]),
                        wait=[IDL, pb.at(s_ind)], inc=[s_ev])
                    IDU = pb.at(s_ev)
                    for sg in range(NSG):
                        pb.emit("gpsimd", lambda E, sg=sg: E.indirect_dma_start(
                            out=xg[:], out_offset=None,
                            in_=xn_tok_d,
                            in_offset=IOA(ap=idxu[:, sg:sg + 1], axis=0),
                            bounds_check=bregs["tok"], oob_is_err=False,
                            compute_op=ALU.bypass),
                            wait=[IDU, XNT, pb.at(s_mm), pb.at(s_ind), ZT],
                            inc=[s_ind], inc_amt=16, chain=False)
                        XG = pb.at(s_ind)
                        for fc in range(FC):
                            pb.emit("tensor", lambda E, fc=fc: E.transpose(
                                out=tr_ps[:, (fc % 4) * 128:(fc % 4 + 1) * 128],
                                in_=xg[:, fc * 128:(fc + 1) * 128],
                                identity=ident_b[:]),
                                wait=([XG, CONST, pb.at(s_ev)] if fc == 0 else
                                      ([pb.at(s_ev)] if fc == 4 else [])),
                                inc=[s_mm] if fc % 4 == 3 else [])
                            if fc % 4 == 3:
                                pb.emit("vector", lambda E, sg=sg, fc=fc:
                                        E.tensor_copy(
                                    out=xe[:, fc - 3:fc + 1,
                                           sg * 128:(sg + 1) * 128],
                                    in_=tr_ps[:].rearrange(
                                        "p (c n) -> p c n", n=128)),
                                    wait=[pb.at(s_mm)], inc=[s_ev])
                    XE = pb.at(s_ev)
                    for fb in range(NFB):
                        for oc in range(FBC):
                            wb = wus[uidx % 2]
                            pb.dma("gpsimd", wb[:],
                                   w1_d[e, :, :, (fb * FBC + oc) * 128:
                                        (fb * FBC + oc + 1) * 128]
                                   .rearrange("c p n -> p c n"),
                                   wait=[pb.at(s_act), pb.at(s_mm)] if uidx >= 2
                                   else [LN_V, W_V, IDXD, XNT, pb.at(s_out)],
                                   inc=[s_wu[uidx % 2]])
                            WBU = pb.at(s_wu[uidx % 2])
                            pp = upss[oc % 2]
                            for mc in range(FC):
                                pb.emit("tensor", lambda E, wb=wb, mc=mc,
                                        pp=pp: E.matmul(
                                    out=pp[:], lhsT=wb[:, mc, :],
                                    rhs=xe[:, mc, :],
                                    start=(mc == 0), stop=(mc == FC - 1)),
                                    wait=([WBU, XE, pb.at(s_act)]
                                          if mc == 0 else []),
                                    inc=[s_mm] if mc == FC - 1 else [])
                            MM = pb.at(s_mm)
                            if biases:
                                pb.emit("scalar", lambda E, pp=pp, oc=oc, fb=fb,
                                        e=e: E.activation(
                                    out=h_sb[:, oc, :], in_=pp[:], func=act_f,
                                    bias=b1_sb[:, e, fb * FBC + oc:
                                               fb * FBC + oc + 1]),
                                    wait=[MM, IN0], inc=[s_act])
                            else:
                                pb.emit("scalar", lambda E, pp=pp,
                                        oc=oc: E.activation(
                                    out=h_sb[:, oc, :], in_=pp[:], func=act_f),
                                    wait=[MM], inc=[s_act])
                            uidx += 1
                        HUP = pb.at(s_act)
                        for mc in range(FC):
                            wb = wds[didx % 2]
                            pb.dma("gpsimd", wb[:],
                                   w2_d[e, fb * FBC:(fb + 1) * FBC, :,
                                        mc * 128:(mc + 1) * 128]
                                   .rearrange("c p n -> p c n"),
                                   wait=[pb.at(s_mm), pb.at(s_ev)] if didx >= 2
                                   else [LN_V, W_V, IDXD, XNT, pb.at(s_out)],
                                   inc=[s_wd2[didx % 2]])
                            WBD = pb.at(s_wd2[didx % 2])
                            pp = dnss[mc % 2]
                            for fc2 in range(FBC):
                                pb.emit("tensor", lambda E, wb=wb, fc2=fc2,
                                        pp=pp: E.matmul(
                                    out=pp[:], lhsT=wb[:, fc2, :],
                                    rhs=h_sb[:, fc2, :],
                                    start=(fc2 == 0), stop=(fc2 == FBC - 1)),
                                    wait=([WBD, HUP, pb.at(s_ev)]
                                          if fc2 == 0 else []),
                                    inc=[s_mm] if fc2 == FBC - 1 else [])
                            MMD = pb.at(s_mm)
                            if fb == 0 and biases:
                                pb.emit("vector", lambda E, pp=pp, e=e,
                                        mc=mc: E.tensor_scalar(
                                    out=ye_acc[:, mc, :], in0=pp[:],
                                    scalar1=b2_sb[:, e, mc:mc + 1],
                                    scalar2=None, op0=ALU.add),
                                    wait=[MMD, IN0, pb.at(s_ev)], inc=[s_ev])
                            elif fb == 0:
                                pb.emit("vector", lambda E, pp=pp,
                                        mc=mc: E.tensor_copy(
                                    out=ye_acc[:, mc, :], in_=pp[:]),
                                    wait=[MMD, pb.at(s_ev)], inc=[s_ev])
                            else:
                                pb.emit("vector", lambda E, pp=pp, mc=mc:
                                        E.tensor_tensor(
                                    out=ye_acc[:, mc, :],
                                    in0=ye_acc[:, mc, :], in1=pp[:],
                                    op=ALU.add),
                                    wait=[MMD, pb.at(s_ev)], inc=[s_ev])
                            didx += 1
                    YE = pb.at(s_ev)
                    for sg in range(NSG):
                        for fc in range(FC):
                            pb.emit("tensor", lambda E, sg=sg, fc=fc: E.transpose(
                                out=trf_ps[:, (fc % 4) * 128:(fc % 4 + 1) * 128],
                                in_=ye_acc[:, fc, sg * 128:(sg + 1) * 128],
                                identity=ident_f[:]),
                                wait=([YE, CONST, pb.at(s_ev), pb.at(s_ind)]
                                      if fc == 0 else
                                      ([pb.at(s_ev)] if fc == 4 else [])),
                                inc=[s_mm] if fc % 4 == 3 else [])
                            if fc % 4 == 3:
                                pb.emit("vector", lambda E, fc=fc: E.tensor_copy(
                                    out=ysc[:, (fc - 3) * 128:(fc + 1) * 128],
                                    in_=trf_ps[:]),
                                    wait=[pb.at(s_mm), pb.at(s_ind)], inc=[s_ev])
                        pb.emit("vector", lambda E, sg=sg: E.tensor_scalar_mul(
                            ysc[:], ysc[:], idro[:, sg, 1:2]),
                            wait=[pb.at(s_ev), IDL], inc=[s_ev])
                        pb.emit("gpsimd", lambda E, sg=sg: E.indirect_dma_start(
                            out=y_tok_d,
                            out_offset=IOA(ap=idxu[:, sg:sg + 1], axis=0),
                            in_=ysc[:], in_offset=None,
                            bounds_check=bregs["tok"], oob_is_err=False,
                            compute_op=ALU.add),
                            wait=[pb.at(s_ev), YZ0, pb.at(s_ind)],
                            inc=[s_ind], inc_amt=16, chain=False)
            YSC = pb.at(s_ind)
            with nc.sbuf_tensor("yr", [128, 16, DM], F32) as yr, \
                 nc.sbuf_tensor("bs0", [128, 512], F32) as bs0, \
                 nc.sbuf_tensor("bs1", [128, 512], F32) as bs1, \
                 nc.sbuf_tensor("br0", [128, 512], F32) as br0, \
                 nc.sbuf_tensor("br1", [128, 512], F32) as br1, \
                 nc.psum_tensor("res_ps", [128, 512], F32) as res_ps:
                bss = [bs0, bs1]
                brs = [br0, br1]
                for g in range(16):
                    pb.dma("sync", yr[:, g, :], y_tok_d[g * 128:(g + 1) * 128, :],
                           wait=[YSC, pb.at(s_yz)] if g == 0 else [],
                           inc=[s_yl])
                YRL = pb.at(s_yl)
                for fc in range(FC):
                    for gq in range(4):
                        for g4 in range(4):
                            g = gq * 4 + g4
                            pb.emit("tensor", lambda E, g=g, g4=g4,
                                    fc=fc: E.transpose(
                                out=res_ps[:, g4 * 128:(g4 + 1) * 128],
                                in_=yr[:, g, fc * 128:(fc + 1) * 128],
                                identity=ident_f[:]),
                                wait=([YRL, CONST, pb.at(s_ev)]
                                      if g4 == 0 else []),
                                inc=[s_mm] if g4 == 3 else [])
                        sl = slice(gq * 512, (gq + 1) * 512)
                        i = fc * 4 + gq
                        bs = bss[i % 2]
                        br = brs[i % 2]
                        pb.dma("sync", bs[:], xin[fc * 128:(fc + 1) * 128, sl],
                               wait=[pb.at(s_so[i % 2]), pb.at(s_ev)],
                               inc=[s_bas[i % 2]])
                        if xres_f:
                            pb.dma("sync", br[:],
                                   xres[fc * 128:(fc + 1) * 128, sl],
                                   inc=[s_bas[i % 2]])
                        BL = pb.at(s_bas[i % 2])
                        pb.emit("vector", lambda E, bs=bs: E.tensor_tensor(
                            out=bs[:], in0=bs[:], in1=res_ps[:], op=ALU.add),
                            wait=[BL, pb.at(s_mm), pb.at(s_ev)], inc=[s_ev])
                        if xres_f:
                            pb.emit("vector", lambda E, bs=bs, br=br:
                                    E.tensor_tensor(
                                out=bs[:], in0=bs[:], in1=br[:], op=ALU.add),
                                wait=[pb.at(s_ev)], inc=[s_ev])
                        pb.dma("sync", xout[fc * 128:(fc + 1) * 128, sl], bs[:],
                               wait=[pb.at(s_ev)], inc=[s_so[i % 2]])
        else:
            with nc.sbuf_tensor("wrow_t", [16, 128], F32) as wrow_t, \
                 nc.psum_tensor("wr_ps", [16, 128], F32) as wr_ps:
                for e in range(NE):
                    pb.emit("tensor", lambda E, e=e: E.transpose(
                        out=wr_ps[:], in_=wt_sb[:, :, e], identity=ident_f[:]),
                        wait=[W_V, CONST, pb.at(s_ev)], inc=[s_mm])
                    pb.emit("vector", lambda E: E.tensor_copy(
                        out=wrow_t[:], in_=wr_ps[:]),
                        wait=[pb.at(s_mm), pb.at(s_wr)], inc=[s_ev])
                    pb.dma("sync", wrow_d[e:e + 1, :]
                           .rearrange("o (g p) -> (o g) p", p=128), wrow_t[:],
                           wait=[pb.at(s_ev), pb.at(s_wr)], inc=[s_wr])
            GRD = pb.at(s_wr)

            with nc.sbuf_tensor("wu0", [128, FC, 128], BF16) as wu0, \
                 nc.sbuf_tensor("wu1", [128, FC, 128], BF16) as wu1, \
                 nc.sbuf_tensor("wd0", [128, FBC, 128], BF16) as wd0, \
                 nc.sbuf_tensor("wd1", [128, FBC, 128], BF16) as wd1, \
                 nc.sbuf_tensor("h_sb", [128, FBC, TOK], BF16) as h_sb, \
                 nc.sbuf_tensor("grow", [1, TOK], F32) as grow, \
                 nc.sbuf_tensor("gb_sb", [128, TOK], F32) as gb_sb, \
                 nc.sbuf_tensor("yt", [128, 512], F32) as yt, \
                 nc.psum_tensor("up_ps0", [128, 512], F32) as up_ps0, \
                 nc.psum_tensor("up_ps1", [128, 512], F32) as up_ps1, \
                 nc.psum_tensor("dn_ps0", [128, 512], F32) as dn_ps0, \
                 nc.psum_tensor("dn_ps1", [128, 512], F32) as dn_ps1, \
                 nc.psum_tensor("gb_ps", [128, 512], F32) as gb_ps:
                wus = [wu0, wu1]
                wds = [wd0, wd1]
                upss = [up_ps0, up_ps1]
                dnss = [dn_ps0, dn_ps1]
                uidx = 0
                didx = 0
                for e in range(NE):
                    pb.dma("sync", grow[:], wrow_d[e:e + 1, :],
                           wait=[GRD, pb.at(s_ev), pb.at(s_mm), pb.at(s_gd)],
                           inc=[s_gd])
                    GR1 = pb.at(s_gd)
                    for tc in range(4):
                        sl = slice(tc * 512, (tc + 1) * 512)
                        pb.emit("tensor", lambda E, sl=sl: E.matmul(
                            out=gb_ps[:], lhsT=onesr_f[:], rhs=grow[:, sl],
                            start=True, stop=True),
                            wait=[GR1, CONST, pb.at(s_ev)], inc=[s_mm])
                        pb.emit("vector", lambda E, sl=sl: E.tensor_copy(
                            out=gb_sb[:, sl], in_=gb_ps[:]),
                            wait=[pb.at(s_mm)], inc=[s_ev])
                    GBD = pb.at(s_ev)
                    for fb in range(NFB):
                        for oc in range(FBC):
                            wb = wus[uidx % 2]
                            pb.dma("gpsimd", wb[:],
                                   w1_d[e, :, :, (fb * FBC + oc) * 128:
                                        (fb * FBC + oc + 1) * 128]
                                   .rearrange("c p n -> p c n"),
                                   wait=[pb.at(s_act), pb.at(s_mm)] if uidx >= 2
                                   else [LN_V, W_V, GRD, pb.at(s_out)],
                                   inc=[s_wu[uidx % 2]])
                            WBU = pb.at(s_wu[uidx % 2])
                            for tc in range(4):
                                sl = slice(tc * 512, (tc + 1) * 512)
                                pp = upss[tc % 2]
                                for mc in range(FC):
                                    pb.emit("tensor", lambda E, wb=wb, mc=mc, sl=sl,
                                            pp=pp: E.matmul(
                                        out=pp[:], lhsT=wb[:, mc, :],
                                        rhs=xn_sb[:, mc, sl],
                                        start=(mc == 0), stop=(mc == FC - 1)),
                                        wait=([WBU, LN_V, pb.at(s_act)]
                                              if mc == 0 else []),
                                        inc=[s_mm] if mc == FC - 1 else [])
                                MM = pb.at(s_mm)
                                if biases:
                                    pb.emit("scalar", lambda E, pp=pp, oc=oc, fb=fb,
                                            e=e, sl=sl: E.activation(
                                        out=h_sb[:, oc, sl], in_=pp[:], func=act_f,
                                        bias=b1_sb[:, e, fb * FBC + oc:
                                                   fb * FBC + oc + 1]),
                                        wait=[MM, IN0], inc=[s_act])
                                else:
                                    pb.emit("scalar", lambda E, pp=pp, oc=oc,
                                            sl=sl: E.activation(
                                        out=h_sb[:, oc, sl], in_=pp[:], func=act_f),
                                        wait=[MM], inc=[s_act])
                            uidx += 1
                        HUP = pb.at(s_act)
                        for mc in range(FC):
                            wb = wds[didx % 2]
                            pb.dma("gpsimd", wb[:],
                                   w2_d[e, fb * FBC:(fb + 1) * FBC, :,
                                        mc * 128:(mc + 1) * 128]
                                   .rearrange("c p n -> p c n"),
                                   wait=[pb.at(s_mm), pb.at(s_ev)] if didx >= 2
                                   else [LN_V, W_V, GRD, pb.at(s_out)],
                                   inc=[s_wd2[didx % 2]])
                            WBD = pb.at(s_wd2[didx % 2])
                            for tc in range(4):
                                sl = slice(tc * 512, (tc + 1) * 512)
                                pp = dnss[tc % 2]
                                for fc2 in range(FBC):
                                    pb.emit("tensor", lambda E, wb=wb, fc2=fc2, sl=sl,
                                            pp=pp: E.matmul(
                                        out=pp[:], lhsT=wb[:, fc2, :],
                                        rhs=h_sb[:, fc2, sl],
                                        start=(fc2 == 0), stop=(fc2 == FBC - 1)),
                                        wait=([WBD, HUP, pb.at(s_ev)]
                                              if fc2 == 0 else []),
                                        inc=[s_mm] if fc2 == FBC - 1 else [])
                                MMD = pb.at(s_mm)
                                if biases and fb == 0:
                                    pb.emit("vector", lambda E, pp=pp, e=e, mc=mc,
                                            sl=sl: E.scalar_tensor_tensor(
                                        out=yt[:], in0=pp[:],
                                        scalar=b2_sb[:, e, mc:mc + 1],
                                        in1=gb_sb[:, sl], op0=ALU.add, op1=ALU.mult),
                                        wait=[MMD, IN0, GBD, pb.at(s_ev)],
                                        inc=[s_ev])
                                else:
                                    pb.emit("vector", lambda E, pp=pp, sl=sl:
                                            E.tensor_tensor(
                                        out=yt[:], in0=pp[:], in1=gb_sb[:, sl],
                                        op=ALU.mult),
                                        wait=[MMD, GBD, pb.at(s_ev)], inc=[s_ev])
                                pb.emit("vector", lambda E, mc=mc, sl=sl:
                                        E.tensor_tensor(
                                    out=y_acc[:, mc, sl], in0=y_acc[:, mc, sl],
                                    in1=yt[:], op=ALU.add),
                                    wait=[pb.at(s_ev), CONST], inc=[s_ev])
                            didx += 1
            YD = pb.at(s_ev)

            with nc.sbuf_tensor("bs0", [128, 512], F32) as bs0, \
                 nc.sbuf_tensor("bs1", [128, 512], F32) as bs1, \
                 nc.sbuf_tensor("br0", [128, 512], F32) as br0, \
                 nc.sbuf_tensor("br1", [128, 512], F32) as br1:
                bss = [bs0, bs1]
                brs = [br0, br1]
                for i, (fc, tc) in enumerate([(f, t) for f in range(FC)
                                              for t in range(4)]):
                    sl = slice(tc * 512, (tc + 1) * 512)
                    bs = bss[i % 2]
                    br = brs[i % 2]
                    pb.dma("sync", bs[:], xin[fc * 128:(fc + 1) * 128, sl],
                           wait=[pb.at(s_so[i % 2]), pb.at(s_ev)],
                           inc=[s_bas[i % 2]])
                    if xres_f:
                        pb.dma("sync", br[:], xres[fc * 128:(fc + 1) * 128, sl],
                               inc=[s_bas[i % 2]])
                    BL = pb.at(s_bas[i % 2])
                    pb.emit("vector", lambda E, bs=bs, fc=fc, sl=sl: E.tensor_tensor(
                        out=bs[:], in0=bs[:], in1=y_acc[:, fc, sl], op=ALU.add),
                        wait=[BL, YD, pb.at(s_ev)], inc=[s_ev])
                    if xres_f:
                        pb.emit("vector", lambda E, bs=bs, br=br: E.tensor_tensor(
                            out=bs[:], in0=bs[:], in1=br[:], op=ALU.add),
                            wait=[pb.at(s_ev)], inc=[s_ev])
                    pb.dma("sync", xout[fc * 128:(fc + 1) * 128, sl], bs[:],
                           wait=[pb.at(s_ev)], inc=[s_so[i % 2]])
        pb.emit("sync", lambda E: E.nop(),
                wait=[pb.at(s_so[0]), pb.at(s_so[1]), pb.at(s_out)])
        if fuse is None:
            pb.realize()
    return nc._nc


def prep_moe_inputs(cfg, weights, x_parts, xres_parts=None, pfx=None):
    NE = 8
    DFF = cfg["dff"]
    w1 = np.ascontiguousarray(np.asarray(weights["w1"], np.float32)
                              .reshape(NE, FC, 128, DFF))
    w2 = np.ascontiguousarray(np.asarray(weights["w2"], np.float32)
                              .reshape(NE, DFF // 128, 128, DM))
    wr = chunked_w(np.asarray(weights["wr"], np.float32))
    in_maps = []
    for core in range(NCORES):
        m = {"w1": w1, "w2": w2, "wr": wr}
        if pfx is None:
            m["xin"] = x_parts[core]
            if cfg["xres"]:
                m["xres"] = xres_parts[core]
        if cfg["biases"]:
            b1 = np.asarray(weights["b1"], np.float32)
            b2 = np.asarray(weights["b2"], np.float32)
            m["b1"] = np.ascontiguousarray(
                b1.reshape(NE, DFF // 128, 128).transpose(0, 2, 1))
            m["b2"] = np.ascontiguousarray(
                b2.reshape(NE, FC, 128).transpose(0, 2, 1))
        if cfg["rms"]:
            m["gain"] = colsplit(np.asarray(weights["gain"], np.float32))
        if cfg.get("sparse"):
            m["iota"] = np.arange(128, dtype=np.float32).reshape(128, 1)
        if pfx:
            m = {pfx + k: v for k, v in m.items()}
        in_maps.append(m)
    return in_maps


MOE_CFGS = {
    "L1_ffn": dict(kind="top1", dff=4096, act="gelu", biases=True, rms=False,
                   xres=False, sparse=512),
    "L0_moe": dict(kind="cap2", dff=2048, act="relu", biases=False, rms=True,
                   xres=True, sparse=512),
}


ATTN_CFGS = {
    "L0_attn": dict(H=8, HD=128, scale=1.0 / 128.0, rope="half",
                    qk_bias=False, o_bias=False, two_norms=False),
    "L0_mha": dict(H=16, HD=64, scale=1.0 / 8.0, rope="int",
                   qk_bias=False, o_bias=False, two_norms=True),
    "L1_attn": dict(H=16, HD=64, scale=1.0 / 8.0, rope=None,
                    qk_bias=True, o_bias=True, two_norms=False),
}


def _run(nc, in_maps):
    from concourse.bass_utils import run_bass_kernel_spmd
    res = run_bass_kernel_spmd(nc, in_maps, core_ids=list(range(NCORES)))
    return res


BLOCK_ORDER = [
    ("b0_", "attn", "L0_attn"),
    ("b1_", "attn", "L0_mha"),
    ("b2_", "moe", "L0_moe"),
    ("b3_", "attn", "L1_attn"),
    ("b4_", "moe", "L1_ffn"),
]


def build_fused():
    nc = bass.Bass("TRN2", target_bir_lowering=False)
    xin = nc.dram_tensor("xin", [DM, TOK], F32, kind="ExternalInput").ap()
    xout = nc.dram_tensor("xout", [DM, TOK], F32, kind="ExternalOutput").ap()
    d = {n: nc.dram_tensor(n, [DM, TOK], F32, kind="Internal").ap()
         for n in ("d_x1", "d_z1", "d_zn", "d_x2", "d_x3")}
    io = {
        "b0_": dict(xin=xin, xout=d["d_x1"]),
        "b1_": dict(xin=d["d_x1"], xout=d["d_z1"], zout=d["d_zn"]),
        "b2_": dict(xin=d["d_z1"], xres=d["d_x1"], xout=d["d_x2"]),
        "b3_": dict(xin=d["d_x2"], xout=d["d_x3"]),
        "b4_": dict(xin=d["d_x3"], xout=xout),
    }
    ctx = ExitStack()
    with ctx:
        pb = PB(nc, ctx)
        pb.init_chains()
        for i, (pfx, kind, name) in enumerate(BLOCK_ORDER):
            cfg = dict(ATTN_CFGS[name] if kind == "attn" else MOE_CFGS[name])
            cfg["debug"] = ()
            cfg["ablate"] = ""
            fuse = dict(nc=nc, pb=pb, pfx=pfx, **io[pfx])
            if kind == "attn":
                build_attn(cfg, fuse)
            else:
                build_moe(cfg, fuse)
            if i < len(BLOCK_ORDER) - 1:
                pb.fence()
        pb.realize()
    return nc


def prep_fused_inputs(I, x0_parts):
    in_maps = [{"xin": x0_parts[c]} for c in range(NCORES)]

    def merge(parts):
        for c in range(NCORES):
            in_maps[c].update(parts[c])

    cfg = dict(ATTN_CFGS["L0_attn"])
    w = {"wq": I["L0_attn_wq"].reshape(DM, DM),
         "wk": I["L0_attn_wk"].reshape(DM, DM),
         "wv": I["L0_attn_wv"].reshape(DM, DM),
         "wo": I["L0_attn_wo"].reshape(DM, DM)}
    merge(prep_attn_inputs(cfg, w, None, pfx="b0_"))

    cfg = dict(ATTN_CFGS["L0_mha"])
    wqkv = I["L0_ffn_w_qkv"].reshape(DM, 3 * DM)
    w = {"wq": wqkv[:, 0:DM], "wk": wqkv[:, DM:2 * DM],
         "wv": wqkv[:, 2 * DM:3 * DM],
         "wo": I["L0_ffn_w_o"].reshape(DM, DM),
         "gain": I["L0_ffn_g1"].reshape(DM)}
    merge(prep_attn_inputs(cfg, w, None, pfx="b1_"))

    cfg = dict(MOE_CFGS["L0_moe"])
    w = {"wr": I["L0_ffn_w_router"].reshape(DM, 8),
         "w1": I["L0_ffn_w1"], "w2": I["L0_ffn_w2"],
         "gain": I["L0_ffn_g2"].reshape(DM)}
    merge(prep_moe_inputs(cfg, w, None, pfx="b2_"))

    cfg = dict(ATTN_CFGS["L1_attn"])
    wqkv = I["L1_attn_Wqkv"].reshape(DM, 3 * DM)
    w = {"wq": wqkv[:, 0:DM], "wk": wqkv[:, DM:2 * DM],
         "wv": wqkv[:, 2 * DM:3 * DM],
         "wo": I["L1_attn_Wproj"].reshape(DM, DM),
         "bqkv": I["L1_attn_bqkv"].reshape(3 * DM),
         "bo": I["L1_attn_bproj"].reshape(DM)}
    merge(prep_attn_inputs(cfg, w, None, pfx="b3_"))

    cfg = dict(MOE_CFGS["L1_ffn"])
    w = {"wr": I["L1_ffn_gate_w"].reshape(DM, 8),
         "w1": I["L1_ffn_w1"], "b1": I["L1_ffn_b1"],
         "w2": I["L1_ffn_w2"], "b2": I["L1_ffn_b2"]}
    merge(prep_moe_inputs(cfg, w, None, pfx="b4_"))
    return in_maps


def kernel(**inputs):
    I = {k: np.asarray(v, dtype=np.float32) for k, v in inputs.items()}
    nc = build_fused()
    in_maps = prep_fused_inputs(I, shard_stream(I["L0_attn_x"]))
    res = _run(nc, in_maps)
    return unshard_stream([res.results[c]["xout"]
                           for c in range(NCORES)]).astype(np.float32)


def kernel_multi(**inputs):
    I = {k: np.asarray(v, dtype=np.float32) for k, v in inputs.items()}
    x0_parts = shard_stream(I["L0_attn_x"])

    cfg = dict(ATTN_CFGS["L0_attn"])
    cfg["debug"] = ()
    cfg["ablate"] = ""
    nc = build_attn(cfg)
    w = {"wq": I["L0_attn_wq"].reshape(DM, DM),
         "wk": I["L0_attn_wk"].reshape(DM, DM),
         "wv": I["L0_attn_wv"].reshape(DM, DM),
         "wo": I["L0_attn_wo"].reshape(DM, DM)}
    res = _run(nc, prep_attn_inputs(cfg, w, x0_parts))
    x1_parts = [res.results[c]["xout"] for c in range(NCORES)]

    cfg = dict(ATTN_CFGS["L0_mha"])
    cfg["debug"] = ()
    cfg["ablate"] = ""
    nc = build_attn(cfg)
    wqkv = I["L0_ffn_w_qkv"].reshape(DM, 3 * DM)
    w = {"wq": wqkv[:, 0:DM], "wk": wqkv[:, DM:2 * DM],
         "wv": wqkv[:, 2 * DM:3 * DM],
         "wo": I["L0_ffn_w_o"].reshape(DM, DM),
         "gain": I["L0_ffn_g1"].reshape(DM)}
    res = _run(nc, prep_attn_inputs(cfg, w, x1_parts))
    z1_parts = [res.results[c]["xout"] for c in range(NCORES)]

    cfg = dict(MOE_CFGS["L0_moe"])
    cfg["debug"] = ()
    nc = build_moe(cfg)
    w = {"wr": I["L0_ffn_w_router"].reshape(DM, 8),
         "w1": I["L0_ffn_w1"], "w2": I["L0_ffn_w2"],
         "gain": I["L0_ffn_g2"].reshape(DM)}
    res = _run(nc, prep_moe_inputs(cfg, w, z1_parts, x1_parts))
    x2_parts = [res.results[c]["xout"] for c in range(NCORES)]

    cfg = dict(ATTN_CFGS["L1_attn"])
    cfg["debug"] = ()
    cfg["ablate"] = ""
    nc = build_attn(cfg)
    wqkv = I["L1_attn_Wqkv"].reshape(DM, 3 * DM)
    w = {"wq": wqkv[:, 0:DM], "wk": wqkv[:, DM:2 * DM],
         "wv": wqkv[:, 2 * DM:3 * DM],
         "wo": I["L1_attn_Wproj"].reshape(DM, DM),
         "bqkv": I["L1_attn_bqkv"].reshape(3 * DM),
         "bo": I["L1_attn_bproj"].reshape(DM)}
    res = _run(nc, prep_attn_inputs(cfg, w, x2_parts))
    x3_parts = [res.results[c]["xout"] for c in range(NCORES)]

    cfg = dict(MOE_CFGS["L1_ffn"])
    cfg["debug"] = ()
    nc = build_moe(cfg)
    w = {"wr": I["L1_ffn_gate_w"].reshape(DM, 8),
         "w1": I["L1_ffn_w1"], "b1": I["L1_ffn_b1"],
         "w2": I["L1_ffn_w2"], "b2": I["L1_ffn_b2"]}
    res = _run(nc, prep_moe_inputs(cfg, w, x3_parts))
    x4_parts = [res.results[c]["xout"] for c in range(NCORES)]

    return unshard_stream(x4_parts).astype(np.float32)


# version 05
# speedup vs baseline: 1.5331x; 1.0161x over previous
import numpy as np
from contextlib import ExitStack

import concourse.bass as bass
import concourse.mybir as mybir
from concourse.bass_utils import run_bass_kernel_spmd

F32 = mybir.dt.float32
BF16 = mybir.dt.bfloat16
AF = mybir.ActivationFunctionType
ALU = mybir.AluOpType
AX = mybir.AxisListType

NCORES = 8
B = 2
T = 8192
DM = 1024
TOK = 2048
NSUB = 8
FC = 8
KVROWS = 2056
ENGINES = ("sync", "vector", "scalar", "tensor", "gpsimd")



_POS_CACHE = {}


def pos_arrays(core):
    if core not in _POS_CACHE:
        bidx, pidx = [], []
        for b in range(B):
            for jj in range(NSUB):
                s = 8 * jj + core
                for r in range(128):
                    bidx.append(b)
                    pidx.append(s * 128 + r)
        _POS_CACHE[core] = (np.array(bidx), np.array(pidx))
    return _POS_CACHE[core]


def shard_stream(x):
    out = []
    for core in range(NCORES):
        bidx, pidx = pos_arrays(core)
        out.append(np.ascontiguousarray(x[bidx, pidx, :].T.astype(np.float32)))
    return out


def unshard_stream(parts):
    x = np.empty((B, T, DM), dtype=np.float32)
    for core in range(NCORES):
        bidx, pidx = pos_arrays(core)
        x[bidx, pidx, :] = parts[core].T
    return x


def rope_tables(core, HD, kind):
    base = 10000.0
    half = HD // 2
    if kind == "half":
        ang = np.power(base, -(np.arange(half).astype(np.float64)) / half)
    else:
        ang = base ** (-2.0 * (np.arange(half).astype(np.float64) / HD))
    ang = ang.astype(np.float32)
    _, pidx = pos_arrays(core)
    pos = pidx.astype(np.float32)
    rad = (pos[None, :] * ang[:, None]).astype(np.float32)
    cos = np.cos(rad).astype(np.float32)
    sin = np.sin(rad).astype(np.float32)
    reps = 64 // half
    cos64 = np.concatenate([cos] * reps, axis=0)
    sin64 = np.concatenate([sin] * reps, axis=0)
    cs1 = np.concatenate([cos64, sin64], axis=0)
    cs2 = np.concatenate([sin64, -cos64], axis=0)
    return np.ascontiguousarray(cs1), np.ascontiguousarray(cs2)


def qk_perm(HD, kind):
    perm = []
    hpc = 128 // HD
    half = HD // 2
    for c in range(DM // 128):
        hs = [c * hpc + k for k in range(hpc)]
        ev_cols, od_cols = [], []
        for h in hs:
            if kind == "int":
                ev = [h * HD + 2 * i for i in range(half)]
                od = [h * HD + 2 * i + 1 for i in range(half)]
            else:
                ev = [h * HD + i for i in range(half)]
                od = [h * HD + half + i for i in range(half)]
            ev_cols += ev
            od_cols += od
        perm += ev_cols + od_cols
    return np.array(perm)


def make_causal_mask(core):
    import ml_dtypes
    p = np.arange(128)[:, None, None]
    u = np.arange(8)[None, :, None]
    q = np.arange(128)[None, None, :]
    m = np.where((u * 128 + p) <= (core * 128 + q), 0.0, -1e30).astype(np.float32)
    return np.ascontiguousarray(m.astype(ml_dtypes.bfloat16))


def chunked_w(w):
    return np.ascontiguousarray(np.asarray(w, dtype=np.float32).reshape(FC, 128, -1))


def colsplit(v):
    n = np.asarray(v).shape[0] // 128
    return np.ascontiguousarray(np.asarray(v, dtype=np.float32).reshape(n, 128).T)



class Sem:
    __slots__ = ("h", "n", "name")


CHAINED = ("vector", "scalar", "gpsimd")


class PB:
    def __init__(self, nc, ctx):
        self.nc = nc
        self.ctx = ctx
        self.q = {e: [] for e in ENGINES}
        self.chain = {}
        self.cache = {}
        self.pending = {}
        self.fence_t = None

    def init_chains(self):
        for e in CHAINED:
            self.chain[e] = self.sem("chain_" + e)
        self.fence_t = self.ctx.enter_context(
            self.nc.sbuf_tensor("fence_t", [1, 1], F32))

    def sem(self, name):
        if name in self.cache:
            return self.cache[name]
        s = Sem()
        s.h = self.ctx.enter_context(self.nc.semaphore(name))
        s.n = 0
        s.name = name
        self.cache[name] = s
        return s

    def fence(self):
        toks = [(s, s.n) for s in self.cache.values() if s.n > 0]
        self.emit("gpsimd", lambda E: E.memset(self.fence_t[:], 0.0), wait=toks)
        cg = self.chain["gpsimd"]
        tok = (cg, cg.n)
        for e in ENGINES:
            self.pending[e] = [tok]

    def emit(self, eng, fn, wait=(), inc=(), inc_amt=1, chain=True):
        wait = list(wait)
        pend = self.pending.pop(eng, None)
        if pend:
            wait = list(pend) + wait
        inc = list(inc)
        if chain and eng in self.chain:
            cs = self.chain[eng]
            wait.append((cs, cs.n))
            cs.n += 1
            for s in inc:
                s.h = cs.h
                s.n = cs.n
            real_incs = [(cs.h, 1)]
        else:
            real_incs = []
            for s in inc:
                s.n += inc_amt
                real_incs.append((s.h, inc_amt))
        waitv = [(s.h, v) for (s, v) in wait if v > 0]

        def run(E, fn=fn, waitv=waitv, incs=real_incs):
            for (h, v) in waitv:
                E.wait_ge(h, v)
            ins = fn(E)
            for (h, a) in incs:
                ins = ins.then_inc(h, a)
        self.q[eng].append(run)

    def dma(self, eng, out, in_, wait=(), inc=(), **kw):
        self.emit(eng, lambda E: E.dma_start(out=out, in_=in_, **kw),
                  wait=wait, inc=inc, inc_amt=16, chain=False)

    def at(self, sem):
        return (sem, sem.n)

    def realize(self):
        blk = self.ctx.enter_context(self.nc.Block())
        for e in ENGINES:
            steps = self.q[e]
            getattr(blk, e)(lambda E, steps=steps: [f(E) for f in steps])


class NCP:
    def __init__(self, nc, pfx):
        self._nc = nc
        self._pfx = pfx

    def sbuf_tensor(self, name, *a, **k):
        return self._nc.sbuf_tensor(self._pfx + name, *a, **k)

    def psum_tensor(self, name, *a, **k):
        return self._nc.psum_tensor(self._pfx + name, *a, **k)

    def dram_tensor(self, name, *a, **k):
        return self._nc.dram_tensor(self._pfx + name, *a, **k)

    def __getattr__(self, n):
        return getattr(self._nc, n)



def build_attn(cfg, fuse=None):
    H, HD = cfg["H"], cfg["HD"]
    HPC = 128 // HD
    scale = float(cfg["scale"])
    rope = cfg["rope"]
    two = cfg["two_norms"]
    qk_bias = cfg["qk_bias"]
    o_bias = cfg["o_bias"]
    dbg = set(cfg.get("debug", ()))

    if fuse is None:
        nc = NCP(bass.Bass("TRN2", target_bir_lowering=False), "")
    else:
        nc = NCP(fuse["nc"], fuse["pfx"])
    if fuse is None:
        xin = nc.dram_tensor("xin", [DM, TOK], F32, kind="ExternalInput").ap()
        xout = nc.dram_tensor("xout", [DM, TOK], F32, kind="ExternalOutput").ap()
        zout = nc.dram_tensor("zout", [DM, TOK], F32,
                              kind="ExternalOutput").ap() if two else None
    else:
        xin = fuse["xin"]
        xout = fuse["xout"]
        zout = fuse.get("zout") if two else None
    wqkv = nc.dram_tensor("wqkv", [FC, 128, 3 * DM], F32, kind="ExternalInput").ap()
    wo_d = nc.dram_tensor("wo", [FC, 128, DM], F32, kind="ExternalInput").ap()
    maskd = nc.dram_tensor("maskin", [128, 8, 128], BF16, kind="ExternalInput").ap()
    if rope:
        cs1_d = nc.dram_tensor("cossin1", [128, TOK], F32, kind="ExternalInput").ap()
        cs2_d = nc.dram_tensor("cossin2", [128, TOK], F32, kind="ExternalInput").ap()
    if qk_bias:
        qkb_d = nc.dram_tensor("qkbias", [128, 16], F32, kind="ExternalInput").ap()
        vb_d = nc.dram_tensor("vbias", [1, DM], F32, kind="ExternalInput").ap()
    if o_bias:
        ob_d = nc.dram_tensor("obias", [128, FC], F32, kind="ExternalInput").ap()
    if two:
        gain_d = nc.dram_tensor("gain", [128, FC], F32, kind="ExternalInput").ap()

    kv_in = nc.dram_tensor("kv_in", [KVROWS, TOK], BF16, kind="Internal").ap()
    kv_all = nc.dram_tensor("kv_all", [NCORES * KVROWS, TOK], BF16, kind="Internal").ap()
    kv_all_r = kv_all.rearrange("(r k) t -> r k t", r=NCORES)

    dbg_t = {}
    def dbg_mk(name, shape):
        dbg_t[name] = nc.dram_tensor(name, shape, F32, kind="ExternalOutput").ap()
    if "xn" in dbg: dbg_mk("dbg_xn", [DM, TOK])
    if "q" in dbg: dbg_mk("dbg_q", [DM, TOK])
    if "kv" in dbg: dbg_mk("dbg_kv", [KVROWS, TOK])
    if "c" in dbg: dbg_mk("dbg_c", [128, 16, 16])
    if "o" in dbg: dbg_mk("dbg_o", [128, B * NSUB, DM])
    if "l" in dbg: dbg_mk("dbg_l", [128, B * NSUB, 16])

    ctx = ExitStack()
    with ctx:
        if fuse is None:
            pb = PB(nc, ctx)
            pb.init_chains()
        else:
            pb = fuse["pb"]
        sbt = lambda n, s, d: ctx.enter_context(nc.sbuf_tensor(n, s, d))

        q_sb = sbt("q_sb", [128, FC, TOK], BF16)
        c_all = sbt("c_all", [128, 16, 16], F32)
        qmn = sbt("qmn", [1, 256], F32)
        kmaxrow = sbt("kmaxrow", [1, 32], F32)
        kn_sb = sbt("kn_sb", [1, 32], F32)
        ones_f = sbt("ones_f", [128, 1], F32)
        ones_bf = sbt("ones_bf", [128, 1], BF16)
        onesr_f = sbt("onesr_f", [1, 128], F32)
        nonesr_f = sbt("nonesr_f", [1, 128], F32)
        ident = sbt("ident", [128, 128], BF16)
        eps_c = sbt("eps_c", [1, 1], F32)
        eps2_c = sbt("eps2_c", [1, 1], F32)
        if qk_bias:
            qkb_sb = sbt("qkb_sb", [128, 16], F32)
        if o_bias:
            ob_sb = sbt("ob_sb", [128, FC], F32)
        if two:
            g_sb = sbt("g_sb", [128, FC], F32)

        s_in = pb.sem("s_in")
        s_gp = pb.sem("s_gp")
        s_x = pb.sem("s_x")
        s_ln = pb.sem("s_ln")
        s_lnmm = pb.sem("s_lnmm")
        s_lnact = pb.sem("s_lnact")
        s_w = pb.sem("s_w")
        s_mm = pb.sem("s_mm")
        s_ev = pb.sem("s_ev")
        s_act = pb.sem("s_act")
        s_st = pb.sem("s_st")
        s_ag = pb.sem("s_ag")
        s_kt = pb.sem("s_kt")
        s_fl = pb.sem("s_fl")
        s_fa = pb.sem("s_fa")
        s_fe = pb.sem("s_fe")
        s_out = pb.sem("s_out")
        s_gpd = pb.sem("s_gpd")
        s_xt = [pb.sem("s_xt0"), pb.sem("s_xt1")]
        s_zex = pb.sem("s_zex")
        s_cs = pb.sem("s_cs")
        s_mk = pb.sem("s_mk")
        s_ktp = [pb.sem("s_ktp0"), pb.sem("s_ktp1")]
        s_wv = [pb.sem("s_wv0"), pb.sem("s_wv1")]
        s_bas = [pb.sem("s_bas0"), pb.sem("s_bas1")]
        s_so = [pb.sem("s_so0"), pb.sem("s_so1")]
        s_wb = [pb.sem("s_wb0"), pb.sem("s_wb1")]
        s_kst = pb.sem("s_kst")
        s_vst = pb.sem("s_vst")

        pb.emit("gpsimd", lambda E: E.memset(ones_f[:], 1.0))
        pb.emit("gpsimd", lambda E: E.memset(ones_bf[:], 1.0))
        pb.emit("gpsimd", lambda E: E.memset(onesr_f[:], 1.0))
        pb.emit("gpsimd", lambda E: E.memset(nonesr_f[:], -1.0))
        pb.emit("gpsimd", lambda E: E.memset(eps_c[:], 1e-5))
        pb.emit("gpsimd", lambda E: E.memset(eps2_c[:], 1e-6))
        pb.emit("gpsimd", lambda E: E.memset(kn_sb[:], 0.0))
        pb.emit("gpsimd", lambda E: E.memset(qmn[:], 0.0))
        pb.emit("gpsimd", lambda E: E.memset(kmaxrow[:], 0.0))
        pb.emit("gpsimd", lambda E: E.memset(c_all[:], 0.0))
        pb.emit("gpsimd", lambda E: E.memset(ident[:], 1.0))
        pb.emit("gpsimd", lambda E: E.affine_select(
            out=ident[:], in_=ident[:], pattern=[[-1, 128]],
            compare_op=ALU.is_equal, fill=0.0, base=0, channel_multiplier=1),
            inc=[s_gp])
        CONST = pb.at(s_gp)
        if qk_bias:
            pb.dma("sync", qkb_sb[:], qkb_d, inc=[s_in])
        if o_bias:
            pb.dma("sync", ob_sb[:], ob_d, inc=[s_in])
        if two:
            pb.dma("sync", g_sb[:], gain_d, inc=[s_in])
        IN0 = pb.at(s_in)

        phx = ExitStack()
        xn_sb = phx.enter_context(nc.sbuf_tensor("xn_sb", [128, FC, TOK], BF16))
        def ln_stats(src_ap, rms_only, tt):
            (xss, xqs, zsl, srow, mrow, rrow, mb, rb) = tt[:8]
            mean_ps, sq_ps = tt[8], tt[9]
            cons_mm = {}
            cons_act = {}
            for idx in range(32):
                fc, tc = idx // 4, idx % 4
                tl = xss[idx % 2]
                xq = xqs[idx % 2]
                if idx >= 2:
                    w = [(s_lnmm, cons_mm[idx - 2]), (s_lnact, cons_act[idx - 2])]
                else:
                    w = [pb.at(s_xt[idx % 2]), pb.at(s_ln), pb.at(s_zex)]
                pb.dma("sync", tl[:],
                       src_ap[fc * 128:(fc + 1) * 128, tc * 512:(tc + 1) * 512],
                       wait=w, inc=[s_xt[idx % 2]])
                XL = pb.at(s_xt[idx % 2])
                if not rms_only:
                    pb.emit("tensor", lambda E, tl=tl, fc=fc, tc=tc: E.matmul(
                        out=mean_ps[:, tc * 512:(tc + 1) * 512],
                        lhsT=ones_f[:], rhs=tl[:],
                        start=(fc == 0), stop=(fc == FC - 1)),
                        wait=[XL, CONST], inc=[s_lnmm])
                pb.emit("scalar", lambda E, tl=tl, xq=xq: E.activation(
                    out=xq[:], in_=tl[:], func=AF.Square),
                    wait=[XL] + ([(s_lnmm, cons_mm[idx - 2])] if idx >= 2 else []),
                    inc=[s_lnact])
                pb.emit("tensor", lambda E, xq=xq, fc=fc, tc=tc: E.matmul(
                    out=sq_ps[:, tc * 512:(tc + 1) * 512],
                    lhsT=ones_f[:], rhs=xq[:],
                    start=(fc == 0), stop=(fc == FC - 1)),
                    wait=[pb.at(s_lnact), CONST], inc=[s_lnmm])
                cons_mm[idx] = s_lnmm.n
                cons_act[idx] = s_lnact.n
            if not rms_only:
                pb.emit("vector", lambda E: E.tensor_scalar_mul(
                    mrow[:], mean_ps[:], 1.0 / DM),
                    wait=[pb.at(s_lnmm)], inc=[s_ln])
                pb.emit("vector", lambda E: E.tensor_tensor(
                    out=srow[:], in0=mrow[:], in1=mrow[:], op=ALU.mult),
                    wait=[pb.at(s_ln)], inc=[s_ln])
                pb.emit("vector", lambda E: E.scalar_tensor_tensor(
                    out=srow[:], in0=sq_ps[:], scalar=1.0 / DM, in1=srow[:],
                    op0=ALU.mult, op1=ALU.subtract),
                    wait=[pb.at(s_ln)], inc=[s_ln])
            else:
                pb.emit("vector", lambda E: E.tensor_scalar_mul(
                    srow[:], sq_ps[:], 1.0 / DM),
                    wait=[pb.at(s_lnmm)], inc=[s_ln])
            eps_use = eps2_c if rms_only else eps_c
            pb.emit("scalar", lambda E, e=eps_use: E.activation(
                out=rrow[:], in_=srow[:], func=AF.Sqrt, bias=e[:]),
                wait=[pb.at(s_ln), CONST], inc=[s_lnact])
            pb.emit("vector", lambda E: E.reciprocal(out=rrow[:], in_=rrow[:]),
                    wait=[pb.at(s_lnact)], inc=[s_ln])

        def ln_apply(src_ap, rms_only, out_bf, export_dram, tt):
            (xss, xqs, zsl, srow, mrow, rrow, mb, rb) = tt[:8]
            bc1_ps, bc2_ps = tt[10], tt[11]
            RR = pb.at(s_ln)
            for tc in range(4):
                sl = slice(tc * 512, (tc + 1) * 512)
                pb.emit("tensor", lambda E, sl=sl: E.matmul(
                    out=bc1_ps[:], lhsT=onesr_f[:], rhs=rrow[:, sl],
                    start=True, stop=True),
                    wait=[RR, pb.at(s_ln), CONST], inc=[s_lnmm])
                pb.emit("vector", lambda E: E.tensor_copy(out=rb[:], in_=bc1_ps[:]),
                        wait=[pb.at(s_lnmm), pb.at(s_ln)], inc=[s_ln])
                if not rms_only:
                    pb.emit("tensor", lambda E, sl=sl: E.matmul(
                        out=bc2_ps[:], lhsT=onesr_f[:], rhs=mrow[:, sl],
                        start=True, stop=True),
                        wait=[pb.at(s_ln)], inc=[s_lnmm])
                    pb.emit("vector", lambda E: E.tensor_copy(out=mb[:], in_=bc2_ps[:]),
                            wait=[pb.at(s_lnmm)], inc=[s_ln])
                BC = pb.at(s_ln)
                for fc in range(FC):
                    idx = tc * 8 + fc
                    tl = xss[idx % 2]
                    xq = xqs[idx % 2]
                    pb.dma("sync", tl[:], src_ap[fc * 128:(fc + 1) * 128, sl],
                           wait=[pb.at(s_ln), pb.at(s_xt[idx % 2]), pb.at(s_zex)],
                           inc=[s_xt[idx % 2]])
                    XL = pb.at(s_xt[idx % 2])
                    if not rms_only:
                        pb.emit("vector", lambda E, tl=tl, xq=xq: E.tensor_tensor(
                            out=xq[:], in0=tl[:], in1=mb[:], op=ALU.subtract),
                            wait=[XL, BC, pb.at(s_ln)], inc=[s_ln])
                        srct = xq
                    else:
                        srct = tl
                    if export_dram is not None:
                        pb.emit("vector", lambda E, st=srct: E.tensor_tensor(
                            out=zsl[:], in0=st[:], in1=rb[:], op=ALU.mult),
                            wait=[pb.at(s_ln), BC, XL, pb.at(s_zex)], inc=[s_ln])
                        pb.dma("sync", export_dram[fc * 128:(fc + 1) * 128, sl],
                               zsl[:], wait=[pb.at(s_ln), pb.at(s_zex)],
                               inc=[s_zex])
                    if out_bf is not None:
                        if rms_only and two:
                            pb.emit("vector", lambda E, st=srct: E.tensor_tensor(
                                out=zsl[:], in0=st[:], in1=rb[:], op=ALU.mult),
                                wait=[pb.at(s_ln), BC, XL, pb.at(s_zex)], inc=[s_ln])
                            pb.emit("vector", lambda E, fc=fc, sl=sl: E.tensor_scalar_mul(
                                out_bf[:, fc, sl], zsl[:], g_sb[:, fc:fc + 1]),
                                wait=[pb.at(s_ln), IN0], inc=[s_ln])
                        else:
                            pb.emit("vector", lambda E, st=srct, fc=fc, sl=sl: E.tensor_tensor(
                                out=out_bf[:, fc, sl], in0=st[:], in1=rb[:],
                                op=ALU.mult),
                                wait=[pb.at(s_ln), BC, XL], inc=[s_ln])

        with nc.sbuf_tensor("xs0", [128, 512], F32) as xs0, \
             nc.sbuf_tensor("xs1", [128, 512], F32) as xs1, \
             nc.sbuf_tensor("xq0", [128, 512], F32) as xq0, \
             nc.sbuf_tensor("xq1", [128, 512], F32) as xq1, \
             nc.sbuf_tensor("zsl", [128, 512], F32) as zsl, \
             nc.sbuf_tensor("srow", [1, TOK], F32) as srow, \
             nc.sbuf_tensor("mrow", [1, TOK], F32) as mrow, \
             nc.sbuf_tensor("rrow", [1, TOK], F32) as rrow, \
             nc.sbuf_tensor("mb", [128, 512], F32) as mb, \
             nc.sbuf_tensor("rb", [128, 512], F32) as rb:
            tt = ([xs0, xs1], [xq0, xq1], zsl, srow, mrow, rrow, mb, rb,
                  None, None, None, None)
            tt = list(tt)
            passes = [(xin, False, xn_sb, None)] if not two else \
                     [(xin, False, None, zout), (zout, True, xn_sb, None)]
            for pi, (srcp, rmsp, outp, expp) in enumerate(passes):
                with nc.psum_tensor("mean_ps%d" % pi, [1, TOK], F32) as mean_ps, \
                     nc.psum_tensor("sq_ps%d" % pi, [1, TOK], F32) as sq_ps:
                    tt[8], tt[9] = mean_ps, sq_ps
                    ln_stats(srcp, rmsp, tt)
                with nc.psum_tensor("bc1_ps%d" % pi, [128, 512], F32) as bc1_ps, \
                     nc.psum_tensor("bc2_ps%d" % pi, [128, 512], F32) as bc2_ps:
                    tt[10], tt[11] = bc1_ps, bc2_ps
                    ln_apply(srcp, rmsp, outp, expp, tt)

        if "dbg_xn" in dbg_t:
            for fc in range(FC):
                pb.dma("gpsimd", dbg_t["dbg_xn"][fc * 128:(fc + 1) * 128, :],
                       xn_sb[:, fc, :], wait=[pb.at(s_ln)], inc=[s_out])
        LN_V = pb.at(s_ln)

        with nc.sbuf_tensor("cs1_sb", [128, TOK], F32) as cs1_sb, \
             nc.sbuf_tensor("cs2_sb", [128, TOK], F32) as cs2_sb, \
             nc.sbuf_tensor("wbuf0", [128, FC, 128], BF16) as wbuf0, \
             nc.sbuf_tensor("wbuf1", [128, FC, 128], BF16) as wbuf1, \
             nc.sbuf_tensor("kstage", [128, TOK], BF16) as kstage, \
             nc.sbuf_tensor("ksq", [128, TOK], BF16) as ksq, \
             nc.sbuf_tensor("statpad", [KVROWS - 2048, TOK], BF16) as statpad, \
             nc.sbuf_tensor("r1", [64, 512], F32) as r1, \
             nc.sbuf_tensor("r2", [64, 512], F32) as r2, \
             nc.sbuf_tensor("r3", [64, 512], F32) as r3, \
             nc.sbuf_tensor("r4", [64, 512], F32) as r4, \
             nc.psum_tensor("qkv_ps0", [128, 512], F32) as qkv_ps0, \
             nc.psum_tensor("qkv_ps1", [128, 512], F32) as qkv_ps1, \
             nc.psum_tensor("kn_ps", [1, TOK], F32) as kn_ps:
            wbufs = [wbuf0, wbuf1]
            pss = [qkv_ps0, qkv_ps1]
            P1D = [LN_V, pb.at(s_zex), pb.at(s_lnmm), pb.at(s_lnact)]
            if rope:
                pb.dma("sync", cs1_sb[:], cs1_d, wait=P1D, inc=[s_cs])
                pb.dma("sync", cs2_sb[:], cs2_d, inc=[s_cs])
            CS0 = pb.at(s_cs)
            for oc in range(16):
                wb = wbufs[oc % 2]
                w = [pb.at(s_mm)] if oc >= 2 else P1D
                pb.dma("gpsimd", wb[:], wqkv[:, :, oc * 128:(oc + 1) * 128]
                       .rearrange("c p n -> p c n"), wait=w, inc=[s_wb[oc % 2]])
                WB = pb.at(s_wb[oc % 2])
                dest_q = oc < 8
                for tc in range(4):
                    sl = slice(tc * 512, (tc + 1) * 512)
                    pp = pss[tc % 2]
                    for mc in range(FC):
                        pb.emit("tensor", lambda E, wb=wb, mc=mc, sl=sl, pp=pp: E.matmul(
                            out=pp[:], lhsT=wb[:, mc, :], rhs=xn_sb[:, mc, sl],
                            start=(mc == 0), stop=(mc == FC - 1)),
                            wait=([WB, LN_V, pb.at(s_ev), pb.at(s_act)]
                                  if mc == 0 else []),
                            inc=[s_mm] if mc == FC - 1 else [])
                    MM = pb.at(s_mm)
                    dst = q_sb[:, oc, sl] if dest_q else kstage[:, sl]
                    if rope:
                        pb.emit("vector", lambda E, pp=pp, sl=sl: E.tensor_tensor(
                            out=r1[:], in0=pp[0:64, :], in1=cs1_sb[0:64, sl],
                            op=ALU.mult),
                            wait=[MM, CS0, pb.at(s_ev), pb.at(s_kst)], inc=[s_ev])
                        pb.emit("vector", lambda E, pp=pp, sl=sl: E.tensor_tensor(
                            out=r2[:], in0=pp[64:128, :], in1=cs1_sb[64:128, sl],
                            op=ALU.mult), inc=[s_ev])
                        pb.emit("vector", lambda E, pp=pp, sl=sl: E.tensor_tensor(
                            out=r3[:], in0=pp[0:64, :], in1=cs2_sb[0:64, sl],
                            op=ALU.mult), inc=[s_ev])
                        pb.emit("vector", lambda E, pp=pp, sl=sl: E.tensor_tensor(
                            out=r4[:], in0=pp[64:128, :], in1=cs2_sb[64:128, sl],
                            op=ALU.mult), inc=[s_ev])
                        EVR = pb.at(s_ev)
                        pb.emit("vector", lambda E, dst=dst: E.tensor_tensor(
                            out=dst[0:64, :], in0=r1[:], in1=r2[:],
                            op=ALU.subtract), wait=[EVR], inc=[s_ev])
                        pb.emit("vector", lambda E, dst=dst: E.tensor_tensor(
                            out=dst[64:128, :], in0=r3[:], in1=r4[:],
                            op=ALU.subtract), inc=[s_ev])
                    elif qk_bias:
                        pb.emit("vector", lambda E, pp=pp, dst=dst, oc=oc: E.tensor_scalar_add(
                            dst, pp[:], qkb_sb[:, oc:oc + 1]),
                            wait=[MM, IN0, pb.at(s_kst), pb.at(s_ev)], inc=[s_ev])
                    else:
                        pb.emit("vector", lambda E, pp=pp, dst=dst: E.tensor_copy(
                            out=dst, in_=pp[:]),
                            wait=[MM, pb.at(s_kst), pb.at(s_ev)], inc=[s_ev])
                if not dest_q:
                    kc = oc - 8
                    EVK = [pb.at(s_ev), pb.at(s_act)]
                    pb.dma("sync", kv_in[kc * 128:(kc + 1) * 128, :], kstage[:],
                           wait=EVK + [pb.at(s_kst)], inc=[s_kst])
                    pb.emit("scalar", lambda E: E.activation(
                        out=ksq[:], in_=kstage[:], func=AF.Square),
                        wait=EVK + [pb.at(s_mm)], inc=[s_act])
                    KSQ = pb.at(s_act)
                    for hh in range(HPC):
                        hsl = slice(hh * HD, (hh + 1) * HD)
                        head = kc * HPC + hh
                        for t2 in range(4):
                            sl2 = slice(t2 * 512, (t2 + 1) * 512)
                            pb.emit("tensor", lambda E, hsl=hsl, sl2=sl2: E.matmul(
                                out=kn_ps[:, sl2], lhsT=ones_bf[hsl, :],
                                rhs=ksq[hsl, sl2], start=True, stop=True),
                                wait=([KSQ, pb.at(s_ev), CONST] if t2 == 0 else []),
                                inc=[s_mm] if t2 == 3 else [])
                        pb.emit("vector", lambda E, head=head: E.reduce_max(
                            out=kn_sb[:, 2 * head:2 * head + 2],
                            in_=kn_ps[:].rearrange("p (b t) -> p b t", b=2),
                            axis=AX.X),
                            wait=[pb.at(s_mm)], inc=[s_ev])
            pb.emit("gpsimd", lambda E: E.memset(statpad[:], 0.0), inc=[s_gp])
            pb.emit("vector", lambda E: E.tensor_copy(
                out=statpad[0:1, 0:32], in_=kn_sb[:]),
                wait=[pb.at(s_ev), pb.at(s_gp)], inc=[s_ev])
            pb.dma("gpsimd", kv_in[2048:KVROWS, :], statpad[:],
                   wait=[pb.at(s_ev)], inc=[s_st])
        Q_V = pb.at(s_ev)
        Q_A = pb.at(s_act)

        with nc.sbuf_tensor("vbr_sb", [1, DM], F32) as vbr_sb, \
             nc.sbuf_tensor("vbb_sb", [128, DM], F32) as vbb_sb, \
             nc.sbuf_tensor("wv0", [128, FC, 512], BF16) as wv0, \
             nc.sbuf_tensor("wv1", [128, FC, 512], BF16) as wv1, \
             nc.sbuf_tensor("vstage", [128, 512], BF16) as vstage, \
             nc.psum_tensor("v_ps0", [128, 512], F32) as v_ps0, \
             nc.psum_tensor("v_ps1", [128, 512], F32) as v_ps1, \
             nc.psum_tensor("vb_ps", [128, 512], F32) as vb_ps:
            if qk_bias:
                pb.dma("sync", vbr_sb[:], vb_d, inc=[s_in])
                VBR = pb.at(s_in)
                for vc in range(2):
                    sl = slice(vc * 512, (vc + 1) * 512)
                    pb.emit("tensor", lambda E, sl=sl: E.matmul(
                        out=vb_ps[:], lhsT=onesr_f[:], rhs=vbr_sb[:, sl],
                        start=True, stop=True),
                        wait=[VBR, pb.at(s_ev), CONST], inc=[s_mm])
                    pb.emit("vector", lambda E, sl=sl: E.tensor_copy(
                        out=vbb_sb[:, sl], in_=vb_ps[:]),
                        wait=[pb.at(s_mm)], inc=[s_ev])
            wvs = [wv0, wv1]
            vpss = [v_ps0, v_ps1]
            for vc in range(2):
                wb = wvs[vc]
                pb.dma("gpsimd", wb[:],
                       wqkv[:, :, 2 * DM + vc * 512: 2 * DM + (vc + 1) * 512]
                       .rearrange("c p n -> p c n"),
                       wait=[Q_V, Q_A, pb.at(s_mm), pb.at(s_st), pb.at(s_kst)],
                       inc=[s_wv[vc]])
                WB = pb.at(s_wv[vc])
                for tt_ in range(16):
                    pp = vpss[tt_ % 2]
                    for mc in range(FC):
                        pb.emit("tensor", lambda E, wb=wb, mc=mc, tt_=tt_, pp=pp: E.matmul(
                            out=pp[:], lhsT=xn_sb[:, mc, tt_ * 128:(tt_ + 1) * 128],
                            rhs=wb[:, mc, :], start=(mc == 0), stop=(mc == FC - 1)),
                            wait=([WB, LN_V, pb.at(s_ev)] if mc == 0 else []),
                            inc=[s_mm] if mc == FC - 1 else [])
                    MM = pb.at(s_mm)
                    if qk_bias:
                        pb.emit("vector", lambda E, pp=pp, vc=vc: E.tensor_tensor(
                            out=vstage[:], in0=pp[:],
                            in1=vbb_sb[:, vc * 512:(vc + 1) * 512], op=ALU.add),
                            wait=[MM, pb.at(s_vst), pb.at(s_ev)], inc=[s_ev])
                    else:
                        pb.emit("vector", lambda E, pp=pp: E.tensor_copy(
                            out=vstage[:], in_=pp[:]),
                            wait=[MM, pb.at(s_vst), pb.at(s_ev)], inc=[s_ev])
                    dstv = kv_in[1024 + tt_ * 64: 1024 + (tt_ + 1) * 64, :] \
                        .rearrange("r (s d) -> (r s) d", s=2)[:, vc * 512:(vc + 1) * 512]
                    pb.dma("sync", dstv, vstage[:],
                           wait=[pb.at(s_ev), pb.at(s_vst)], inc=[s_vst])

        with nc.sbuf_tensor("qs_sb", [128, 128], BF16) as qs_sb, \
             nc.psum_tensor("qn_ps", [1, TOK], F32) as qn_ps:
            for bjj in range(16):
                b, jj = bjj // 8, bjj % 8
                cols = slice(b * 1024 + jj * 128, b * 1024 + (jj + 1) * 128)
                RED = pb.at(s_ev)
                for oc in range(FC):
                    pb.emit("scalar", lambda E, oc=oc, cols=cols: E.activation(
                        out=qs_sb[:], in_=q_sb[:, oc, cols], func=AF.Square),
                        wait=[Q_V, pb.at(s_mm), RED], inc=[s_act])
                    QSQ = pb.at(s_act)
                    for hh in range(HPC):
                        hsl = slice(hh * HD, (hh + 1) * HD)
                        head = oc * HPC + hh
                        pb.emit("tensor", lambda E, hsl=hsl, head=head: E.matmul(
                            out=qn_ps[:, head * 128:(head + 1) * 128],
                            lhsT=ones_bf[hsl, :], rhs=qs_sb[hsl, :],
                            start=True, stop=True),
                            wait=[QSQ, RED, CONST], inc=[s_mm])
                pb.emit("vector", lambda E, bjj=bjj: E.reduce_max(
                    out=qmn[:, bjj * 16: bjj * 16 + H],
                    in_=qn_ps[:, 0:H * 128].rearrange("p (h t) -> p h t", h=H),
                    axis=AX.X),
                    wait=[pb.at(s_mm)], inc=[s_ev])
            pb.emit("scalar", lambda E: E.activation(
                out=qmn[:], in_=qmn[:], func=AF.Sqrt),
                wait=[pb.at(s_ev)], inc=[s_act])
        QN_A = pb.at(s_act)
        phx.close()

        pb.emit("gpsimd", lambda E: E.collective_compute(
            "AllGather", ALU.bypass, replica_groups=[list(range(NCORES))],
            ins=[kv_in.opt()], outs=[kv_all.opt()]),
            wait=[pb.at(s_st), pb.at(s_kst), pb.at(s_vst), pb.at(s_gp)],
            inc=[s_ag], chain=False)
        AGD = pb.at(s_ag)

        if "dbg_kv" in dbg_t:
            with nc.sbuf_tensor("dkv", [128, TOK], F32) as dkv:
                for rr in range(16):
                    pb.dma("gpsimd", dkv[:], kv_in[rr * 128:(rr + 1) * 128, :],
                           wait=[AGD, pb.at(s_out)], inc=[s_gpd])
                    pb.dma("sync", dbg_t["dbg_kv"][rr * 128:(rr + 1) * 128, :],
                           dkv[:], wait=[pb.at(s_gpd)], inc=[s_out])
                pb.dma("gpsimd", dkv[0:1, 0:32], kv_in[2048:2049, 0:32],
                       wait=[pb.at(s_out)], inc=[s_gpd])
                pb.dma("sync", dbg_t["dbg_kv"][2048:2049, 0:32], dkv[0:1, 0:32],
                       wait=[pb.at(s_gpd)], inc=[s_out])
        if "dbg_q" in dbg_t:
            for fc in range(FC):
                pb.dma("gpsimd", dbg_t["dbg_q"][fc * 128:(fc + 1) * 128, :],
                       q_sb[:, fc, :], wait=[Q_V, Q_A], inc=[s_out])

        with nc.sbuf_tensor("kst", [1, NCORES, 32], F32) as kst, \
             nc.sbuf_tensor("crow", [1, 16], F32) as crow, \
             nc.psum_tensor("c_ps", [128, 16], F32) as c_ps:
            pb.dma("gpsimd", kst[:],
                   kv_all_r[:, 2048:2049, 0:32].rearrange("r k s -> k r s"),
                   wait=[AGD], inc=[s_gpd])
            pb.emit("vector", lambda E: E.reduce_max(
                out=kmaxrow[:], in_=kst[:].rearrange("p r s -> p s r"),
                axis=AX.X),
                wait=[pb.at(s_gpd)], inc=[s_ev])
            pb.emit("scalar", lambda E: E.activation(
                out=kmaxrow[:], in_=kmaxrow[:], func=AF.Sqrt),
                wait=[pb.at(s_ev)], inc=[s_act])
            KMX = pb.at(s_act)
            kmx_r = kmaxrow[:].rearrange("p (h b) -> p b h", b=2)
            for bjj in range(16):
                b = bjj // 8
                pb.emit("vector", lambda E, bjj=bjj, b=b: E.tensor_tensor(
                    out=crow[:, 0:H], in0=qmn[:, bjj * 16:bjj * 16 + H],
                    in1=kmx_r[:, b, 0:H], op=ALU.mult),
                    wait=[KMX, QN_A, pb.at(s_mm)], inc=[s_ev])
                pb.emit("vector", lambda E: E.tensor_scalar_mul(
                    crow[:, 0:H], crow[:, 0:H], scale),
                    wait=[pb.at(s_ev)], inc=[s_ev])
                CRW = pb.at(s_ev)
                pb.emit("tensor", lambda E: E.matmul(
                    out=c_ps[:, 0:H], lhsT=nonesr_f[:], rhs=crow[:, 0:H],
                    start=True, stop=True), wait=[CRW, CONST], inc=[s_mm])
                pb.emit("vector", lambda E, bjj=bjj: E.tensor_copy(
                    out=c_all[:, bjj, 0:H], in_=c_ps[:, 0:H]),
                    wait=[pb.at(s_mm)], inc=[s_ev])
        C_V = pb.at(s_ev)
        if "dbg_c" in dbg_t:
            pb.dma("sync", dbg_t["dbg_c"], c_all[:], wait=[C_V], inc=[s_out])

        base_d = zout if two else xin
        ph5 = ExitStack()
        maskt_sb = ph5.enter_context(nc.sbuf_tensor("maskt_sb", [128, 8, 128], BF16))
        wo_sb = ph5.enter_context(nc.sbuf_tensor("wo_sb", [128, FC, DM], BF16))
        rl_sb = ph5.enter_context(nc.sbuf_tensor("rl_sb", [128, 1024], F32))
        otn = ph5.enter_context(nc.sbuf_tensor("otn", [128, FC, 512], BF16))
        bas0 = ph5.enter_context(nc.sbuf_tensor("bas0", [128, 512], F32))
        bas1 = ph5.enter_context(nc.sbuf_tensor("bas1", [128, 512], F32))
        ktile0 = ph5.enter_context(nc.sbuf_tensor("ktile0", [128, FC, 1024], BF16))
        ktile1 = ph5.enter_context(nc.sbuf_tensor("ktile1", [128, FC, 1024], BF16))
        vtile0 = ph5.enter_context(nc.sbuf_tensor("vtile0", [128, 8, 1024], BF16))
        vtile1 = ph5.enter_context(nc.sbuf_tensor("vtile1", [128, 8, 1024], BF16))
        pexp = [ph5.enter_context(nc.sbuf_tensor("pexp%d" % i, [128, 1024], BF16))
                for i in range(4)]
        allones = ph5.enter_context(nc.sbuf_tensor("allones", [128, 128], BF16))
        mhalf1 = ph5.enter_context(nc.sbuf_tensor("mhalf1", [128, 128], BF16))
        mhalf2 = ph5.enter_context(nc.sbuf_tensor("mhalf2", [128, 128], BF16))
        o_sb = ph5.enter_context(nc.sbuf_tensor("o_sb", [128, NSUB, 2048], F32))
        st_ps0 = ph5.enter_context(nc.psum_tensor("st_ps0", [128, 1024], F32))
        st_ps1 = ph5.enter_context(nc.psum_tensor("st_ps1", [128, 1024], F32))
        ot_ps = ph5.enter_context(nc.psum_tensor("ot_ps", [128, 2048], F32))
        if True:
            stpss = [st_ps0, st_ps1]
            op_ps = st_ps0[:, 0:512]
            P2D = [AGD, QN_A, pb.at(s_mm), pb.at(s_ev)]
            pb.dma("sync", maskt_sb[:], maskd, wait=P2D, inc=[s_mk])
            pb.dma("gpsimd", wo_sb[:], wo_d.rearrange("c p n -> p c n"),
                   wait=P2D, inc=[s_w])
            pb.emit("gpsimd", lambda E: E.memset(allones[:], 1.0))
            pb.emit("gpsimd", lambda E: E.memset(mhalf1[:], 1.0))
            pb.emit("gpsimd", lambda E: E.affine_select(
                out=mhalf1[:], in_=mhalf1[:], pattern=[[-1, 128]],
                compare_op=ALU.is_ge, fill=0.0, base=63, channel_multiplier=0))
            pb.emit("gpsimd", lambda E: E.memset(mhalf2[:], 1.0))
            pb.emit("gpsimd", lambda E: E.affine_select(
                out=mhalf2[:], in_=mhalf2[:], pattern=[[1, 128]],
                compare_op=ALU.is_ge, fill=0.0, base=-64, channel_multiplier=0),
                inc=[s_gp])
            IN5 = pb.at(s_mk)
            WO_L = pb.at(s_w)
            CONST5 = pb.at(s_gp)
            ktiles = [ktile0, ktile1]
            vtiles = [vtile0, vtile1]
            FL_tile_done = {}
            exp_war = [CONST5] * 4
            st_war = [CONST5, CONST5]
            ablate = cfg.get("ablate", "")
            merge_prev = pb.at(s_fe)
            fin_done = pb.at(s_fe)
            opd_prev = pb.at(s_fe)
            if ablate == "exp":
                for i in range(4):
                    pb.emit("gpsimd", lambda E, i=i: E.memset(pexp[i][:], 0.001),
                            inc=[s_gp] if i == 3 else [])
            for b in range(B):
                pb.emit("gpsimd", lambda E: E.memset(o_sb[:], 0.0),
                        wait=[fin_done] if b else [AGD], inc=[s_gp])
                ZRO = pb.at(s_gp)
                for j in range(NSUB):
                    par = j % 2
                    kt = ktiles[par]
                    vt = vtiles[par]
                    key = (b, j - 2) if j >= 2 else ((b - 1, j + NSUB - 2) if b else None)
                    wkt = [FL_tile_done[key]] if key in FL_tile_done else [AGD]
                    bc0 = b * 1024 + j * 128
                    for fc in range(FC):
                        pb.dma("sync",
                               kt[:, fc, :].rearrange("p (r t) -> p r t", r=NCORES),
                               kv_all_r[:, fc * 128:(fc + 1) * 128, bc0:bc0 + 128]
                               .rearrange("r k t -> k r t"),
                               wait=wkt if fc == 0 else [], inc=[s_ktp[par]])
                    vrow = 1024 + bc0 // 2
                    for u in range(NCORES):
                        pb.dma("sync", vt[:, u, :],
                               kv_all_r[u, vrow:vrow + 64, :]
                               .rearrange("k (s d) -> (k s) d", s=2),
                               wait=wkt if u == 0 else [], inc=[s_ktp[par]])
                    KT = pb.at(s_ktp[par])
                    for jj in (() if ablate == "flash" else range(j, NSUB)):
                        bjj = b * 8 + jj
                        qcols = slice(b * 1024 + jj * 128, b * 1024 + (jj + 1) * 128)
                        diag = (jj == j) and ablate != "mask"
                        first_ot = True
                        for dc in range(FC):
                            for hh in range(HPC):
                                h = dc * HPC + hh
                                hsl = slice(hh * HD, (hh + 1) * HD)
                                sp = stpss[h % 2]
                                px = pexp[h % 4]
                                for u in range(8):
                                    last = (u == 7) and not diag
                                    pb.emit("tensor", lambda E, sp=sp, kt=kt, u=u,
                                            oc=dc, hsl=hsl, qcols=qcols,
                                            diag=diag: E.matmul(
                                        out=sp[:, u * 128:(u + 1) * 128],
                                        lhsT=kt[hsl, oc, u * 128:(u + 1) * 128],
                                        rhs=q_sb[hsl, oc, qcols],
                                        start=True, stop=not diag),
                                        wait=([KT, Q_V, Q_A, st_war[h % 2]]
                                              if u == 0 else []),
                                        inc=[s_fl] if last else [])
                                    if diag:
                                        pb.emit("tensor", lambda E, sp=sp, u=u: E.matmul(
                                            out=sp[:, u * 128:(u + 1) * 128],
                                            lhsT=ident[:],
                                            rhs=maskt_sb[:, u, :],
                                            start=False, stop=True),
                                            wait=[IN5, CONST] if u == 0 else [],
                                            inc=[s_fl] if u == 7 else [])
                                SMM = pb.at(s_fl)
                                if ablate == "exp":
                                    pb.emit("scalar", lambda E, px=px: E.activation(
                                        out=px[:, 0:1], in_=c_all[:, 0, 0:1],
                                        func=AF.Exp),
                                        wait=[SMM, C_V, exp_war[h % 4]], inc=[s_fa])
                                else:
                                    pb.emit("scalar", lambda E, sp=sp, px=px, bjj=bjj,
                                            h=h: E.activation(
                                        out=px[:], in_=sp[:], func=AF.Exp, scale=scale,
                                        bias=c_all[:, bjj, h:h + 1]),
                                        wait=[SMM, C_V, exp_war[h % 4]],
                                        inc=[s_fa])
                                st_war[h % 2] = pb.at(s_fa)
                            EXPD = pb.at(s_fa)
                            for hh in (() if ablate == "ot" else range(HPC)):
                                h = dc * HPC + hh
                                px = pexp[h % 4]
                                osl = slice(hh * HD, (hh + 1) * HD) if HPC == 2 \
                                    else slice(0, 128)
                                for u in range(8):
                                    pb.emit("tensor", lambda E, vt=vt, u=u, dc=dc,
                                            px=px, osl=osl, hh=hh: E.matmul(
                                        out=ot_ps[osl, dc * 128:(dc + 1) * 128],
                                        lhsT=vt[:, u, dc * 128 + osl.start:
                                                dc * 128 + osl.stop],
                                        rhs=px[:, u * 128:(u + 1) * 128],
                                        start=(u == 0), stop=(u == 7)),
                                        wait=([EXPD, KT] +
                                              ([merge_prev] if first_ot else []))
                                        if u == 0 else [])
                                    first_ot = False
                            for hh in (() if ablate == "l" else range(HPC)):
                                h = dc * HPC + hh
                                px = pexp[h % 4]
                                lsel = allones if HPC == 1 else \
                                    (mhalf1 if hh == 0 else mhalf2)
                                for u in range(8):
                                    last = (hh == HPC - 1 and u == 7)
                                    pb.emit("tensor", lambda E, px=px, u=u, dc=dc,
                                            lsel=lsel, hh=hh: E.matmul(
                                        out=ot_ps[:, (FC + dc) * 128:
                                                  (FC + dc + 1) * 128],
                                        lhsT=lsel[:], rhs=px[:, u * 128:(u + 1) * 128],
                                        start=(hh == 0 and u == 0),
                                        stop=(hh == HPC - 1 and u == 7)),
                                        wait=[CONST5] if (hh == 0 and u == 0) else [],
                                        inc=[s_fl] if last else [])
                            exp_war[(dc * HPC) % 4] = pb.at(s_fl)
                            if HPC == 2:
                                exp_war[(dc * HPC + 1) % 4] = pb.at(s_fl)
                        OTD = pb.at(s_fl)
                        pb.emit("vector", lambda E, jj=jj: E.tensor_tensor(
                            out=o_sb[:, jj, :], in0=o_sb[:, jj, :], in1=ot_ps[:],
                            op=ALU.add),
                            wait=[OTD, ZRO, pb.at(s_fe)], inc=[s_fe])
                        merge_prev = pb.at(s_fe)
                    FL_tile_done[(b, j)] = pb.at(s_fe)
                FLB = pb.at(s_fe)
                if "dbg_o" in dbg_t:
                    pb.dma("sync", dbg_t["dbg_o"][:, b * 8:(b + 1) * 8, :],
                           o_sb[:, :, 0:1024], wait=[FLB], inc=[s_out])
                if "dbg_l" in dbg_t:
                    pb.dma("sync", dbg_t["dbg_l"][:, b * 8:(b + 1) * 8, :],
                           o_sb[:, :, 1024:1024 + 16].rearrange("p s t -> p s t")
                           if False else o_sb[:, :, 1024:1040], wait=[FLB],
                           inc=[s_out])
                resbufs = [bas0, bas1]
                for g in range(2):
                    bcols = slice(b * 1024 + g * 512, b * 1024 + (g + 1) * 512)
                    for jji in range(4):
                        jj = g * 4 + jji
                        if ablate in ("l", "rl"):
                            pb.emit("gpsimd", lambda E: E.memset(rl_sb[:], 1.0),
                                    wait=[FLB], inc=[s_fa])
                        else:
                            pb.emit("scalar", lambda E, jj=jj: E.activation(
                                out=rl_sb[:], in_=o_sb[:, jj, 1024:2048],
                                func=AF.Ln),
                                wait=[FLB, pb.at(s_fa), pb.at(s_fe), opd_prev])
                            pb.emit("scalar", lambda E: E.activation(
                                out=rl_sb[:], in_=rl_sb[:], func=AF.Exp,
                                scale=-1.0), inc=[s_fa])
                        pb.emit("vector", lambda E, jj=jj, jji=jji: E.tensor_tensor(
                            out=otn[:, :, jji * 128:(jji + 1) * 128],
                            in0=o_sb[:, jj, 0:1024]
                            .rearrange("p (c n) -> p c n", c=FC),
                            in1=rl_sb[:].rearrange("p (c n) -> p c n", c=FC),
                            op=ALU.mult),
                            wait=[pb.at(s_fa), pb.at(s_fe)], inc=[s_fe])
                    OTN = pb.at(s_fe)
                    for mc in range(FC):
                        bas = resbufs[mc % 2]
                        pb.dma("sync", bas[:], base_d[mc * 128:(mc + 1) * 128, bcols],
                               wait=[pb.at(s_fe), pb.at(s_so[mc % 2])],
                               inc=[s_bas[mc % 2]])
                        BAS = pb.at(s_bas[mc % 2])
                        for dc in range(FC):
                            pb.emit("tensor", lambda E, dc=dc, mc=mc: E.matmul(
                                out=op_ps, lhsT=wo_sb[:, dc, mc * 128:(mc + 1) * 128],
                                rhs=otn[:, dc, :], start=(dc == 0), stop=(dc == FC - 1)),
                                wait=([OTN, WO_L, pb.at(s_fe), st_war[0]]
                                      if dc == 0 else []),
                                inc=[s_fl] if dc == FC - 1 else [])
                        OPD = pb.at(s_fl)
                        if o_bias:
                            pb.emit("vector", lambda E, bas=bas, mc=mc: E.scalar_tensor_tensor(
                                out=bas[:], in0=op_ps, scalar=ob_sb[:, mc:mc + 1],
                                in1=bas[:], op0=ALU.add, op1=ALU.add),
                                wait=[OPD, BAS, IN0], inc=[s_fe])
                        else:
                            pb.emit("vector", lambda E, bas=bas: E.tensor_tensor(
                                out=bas[:], in0=bas[:], in1=op_ps, op=ALU.add),
                                wait=[OPD, BAS], inc=[s_fe])
                        pb.dma("sync", xout[mc * 128:(mc + 1) * 128, bcols], bas[:],
                               wait=[pb.at(s_fe)], inc=[s_so[mc % 2]])
                    opd_prev = pb.at(s_fe)
                    st_war[0] = pb.at(s_fe)
                fin_done = pb.at(s_fe)
        ph5.close()
        fin_waits = [pb.at(s_out), pb.at(s_st), pb.at(s_so[0]), pb.at(s_so[1]),
                     pb.at(s_zex)]
        pb.emit("sync", lambda E: E.nop(), wait=fin_waits)
        if fuse is None:
            pb.realize()
    return nc._nc



def prep_attn_inputs(cfg, weights, x_parts, pfx=None):
    H, HD = cfg["H"], cfg["HD"]
    rope = cfg["rope"]
    perm = qk_perm(HD, rope) if rope else np.arange(DM)
    wq = np.asarray(weights["wq"], dtype=np.float32)
    wk = np.asarray(weights["wk"], dtype=np.float32)
    wv = np.asarray(weights["wv"], dtype=np.float32)
    wo = np.asarray(weights["wo"], dtype=np.float32)
    wqkv = np.concatenate([wq[:, perm], wk[:, perm], wv], axis=1)
    wqkv_c = chunked_w(wqkv)
    wo_c = chunked_w(wo)
    in_maps = []
    for core in range(NCORES):
        m = {"wqkv": wqkv_c, "wo": wo_c, "maskin": make_causal_mask(core)}
        if pfx is None:
            m["xin"] = x_parts[core]
        if rope:
            cs1, cs2 = rope_tables(core, HD, rope)
            m["cossin1"] = cs1
            m["cossin2"] = cs2
        if cfg["qk_bias"]:
            bq = np.asarray(weights["bqkv"], dtype=np.float32)
            qkbias = np.concatenate([bq[0:DM][perm], bq[DM:2 * DM][perm]])
            m["qkbias"] = colsplit(qkbias)
            m["vbias"] = np.ascontiguousarray(bq[2 * DM:].reshape(1, DM))
        if cfg["o_bias"]:
            m["obias"] = colsplit(np.asarray(weights["bo"], dtype=np.float32))
        if cfg["two_norms"]:
            m["gain"] = colsplit(np.asarray(weights["gain"], dtype=np.float32))
        if pfx:
            m = {pfx + k: v for k, v in m.items()}
        in_maps.append(m)
    return in_maps





def build_moe(cfg, fuse=None):
    kind = cfg["kind"]
    DFF = cfg["dff"]
    NE = 8
    CAP = 1280.0
    SPC = cfg.get("sparse")
    BIGI = 5.0e6
    act_f = AF.Gelu if cfg["act"] == "gelu" else AF.Relu
    biases = cfg["biases"]
    rms = cfg["rms"]
    xres_f = cfg["xres"]
    NFB = DFF // 1024
    FBC = 8

    if fuse is None:
        nc = NCP(bass.Bass("TRN2", target_bir_lowering=False), "")
        xin = nc.dram_tensor("xin", [DM, TOK], F32, kind="ExternalInput").ap()
    else:
        nc = NCP(fuse["nc"], fuse["pfx"])
        xin = fuse["xin"]
    w1_d = nc.dram_tensor("w1", [NE, FC, 128, DFF], F32, kind="ExternalInput").ap()
    w2_d = nc.dram_tensor("w2", [NE, DFF // 128, 128, DM], F32,
                          kind="ExternalInput").ap()
    wr_d = nc.dram_tensor("wr", [FC, 128, NE], F32, kind="ExternalInput").ap()
    if fuse is None:
        xout = nc.dram_tensor("xout", [DM, TOK], F32, kind="ExternalOutput").ap()
    else:
        xout = fuse["xout"]
    if xres_f:
        if fuse is None:
            xres = nc.dram_tensor("xres", [DM, TOK], F32,
                                  kind="ExternalInput").ap()
        else:
            xres = fuse["xres"]
    if biases:
        b1_d = nc.dram_tensor("b1", [NE, 128, DFF // 128], F32,
                              kind="ExternalInput").ap()
        b2_d = nc.dram_tensor("b2", [NE, 128, FC], F32, kind="ExternalInput").ap()
    if rms:
        gain_d = nc.dram_tensor("gain", [128, FC], F32, kind="ExternalInput").ap()
    if kind == "cap2":
        g_loc = nc.dram_tensor("g_loc", [2 * TOK, NE], F32, kind="Internal").ap()
        g_all = nc.dram_tensor("g_all", [NCORES * 2 * TOK, NE], F32,
                               kind="Internal").ap()
    wrow_d = nc.dram_tensor("wrow_d", [NE, TOK], F32, kind="Internal").ap()
    if SPC:
        iota_d = nc.dram_tensor("iota", [128, 1], F32, kind="ExternalInput").ap()
        xn_tok_d = nc.dram_tensor("xn_tok", [TOK, DM], BF16, kind="Internal").ap()
        idx_d = nc.dram_tensor("idx_d", [NE * SPC, 2], F32, kind="Internal").ap()
        y_tok_d = nc.dram_tensor("y_tok", [TOK, DM], F32, kind="Internal").ap()

    dbg = set(cfg.get("debug", ()))
    dbg_t = {}
    def dbg_mk(name, shape):
        dbg_t[name] = nc.dram_tensor(name, shape, F32, kind="ExternalOutput").ap()
    if "w" in dbg: dbg_mk("dbg_w", [128, 16, NE])
    if "th" in dbg: dbg_mk("dbg_th", [1, NE])

    ctx = ExitStack()
    with ctx:
        if fuse is None:
            pb = PB(nc, ctx)
            pb.init_chains()
        else:
            pb = fuse["pb"]
        sbt = lambda n, s, d: ctx.enter_context(nc.sbuf_tensor(n, s, d))

        xn_sb = sbt("xn_sb", [128, FC, TOK], BF16)
        y_acc = sbt("y_acc", [128, FC, TOK], F32) if not SPC else None
        wt_sb = sbt("wt_sb", [128, 16, NE], F32)
        g2w_sb = sbt("g2w_sb", [128, 16, NE], F32)
        wr_sb = sbt("wr_sb", [128, FC, NE], F32)
        rr_sb = sbt("rr_sb", [1, TOK], F32)
        lg_sb = sbt("lg_sb", [8, TOK], F32)
        ones_f = sbt("ones_f", [128, 1], F32)
        onesr_f = sbt("onesr_f", [1, 128], F32)
        ident_f = sbt("ident_f", [128, 128], F32)
        eps_c = sbt("eps_c", [1, 1], F32)
        grow_sb = sbt("grow_sb", [1, TOK], F32)
        if SPC:
            ident_b = sbt("ident_b", [128, 128], BF16)
            ltri = sbt("ltri", [128, 128], F32)
            eoff_sb = sbt("eoff_sb", [128, NE], F32)
            iota_sb = sbt("iota_sb", [128, 1], F32)
        if biases:
            b1_sb = sbt("b1_sb", [128, NE, DFF // 128], F32)
            b2_sb = sbt("b2_sb", [128, NE, FC], F32)
        if rms:
            g_sb = sbt("g_sb", [128, FC], F32)

        s_in = pb.sem("s_in")
        s_gp = pb.sem("s_gp")
        s_ln = pb.sem("s_ln")
        s_lnmm = pb.sem("s_lnmm")
        s_lnact = pb.sem("s_lnact")
        s_xt = [pb.sem("s_xt0"), pb.sem("s_xt1")]
        s_mm = pb.sem("s_mm")
        s_ev = pb.sem("s_ev")
        s_act = pb.sem("s_act")
        s_wu = [pb.sem("s_wu0"), pb.sem("s_wu1")]
        s_wd2 = [pb.sem("s_wd0"), pb.sem("s_wd1")]
        s_ag = pb.sem("s_ag")
        s_gd = pb.sem("s_gd")
        s_wr = pb.sem("s_wr")
        s_bas = [pb.sem("s_bas0"), pb.sem("s_bas1")]
        s_so = [pb.sem("s_so0"), pb.sem("s_so1")]
        s_out = pb.sem("s_out")

        pb.emit("gpsimd", lambda E: E.memset(ones_f[:], 1.0))
        pb.emit("gpsimd", lambda E: E.memset(onesr_f[:], 1.0))
        pb.emit("gpsimd", lambda E: E.memset(
            eps_c[:], 1e-6 if rms else 1e-5))
        pb.emit("gpsimd", lambda E: E.memset(ident_f[:], 1.0))
        pb.emit("gpsimd", lambda E: E.affine_select(
            out=ident_f[:], in_=ident_f[:], pattern=[[-1, 128]],
            compare_op=ALU.is_equal, fill=0.0, base=0, channel_multiplier=1))
        if not SPC:
            pb.emit("gpsimd", lambda E: E.memset(y_acc[:], 0.0))
        if SPC:
            pb.emit("gpsimd", lambda E: E.memset(ident_b[:], 1.0))
            pb.emit("gpsimd", lambda E: E.affine_select(
                out=ident_b[:], in_=ident_b[:], pattern=[[-1, 128]],
                compare_op=ALU.is_equal, fill=0.0, base=0, channel_multiplier=1))
            pb.emit("gpsimd", lambda E: E.memset(ltri[:], 1.0))
            pb.emit("gpsimd", lambda E: E.affine_select(
                out=ltri[:], in_=ltri[:], pattern=[[1, 128]],
                compare_op=ALU.is_ge, fill=0.0, base=-1, channel_multiplier=-1))
            for e in range(NE):
                pb.emit("gpsimd", lambda E, e=e: E.memset(
                    eoff_sb[:, e:e + 1], float(e * SPC)))
        pb.emit("gpsimd", lambda E: E.memset(
            eps_c[:], 1e-6 if rms else 1e-5), inc=[s_gp])
        CONST = pb.at(s_gp)
        if SPC:
            pb.dma("sync", iota_sb[:], iota_d, inc=[s_in])
        pb.dma("sync", wr_sb[:], wr_d.rearrange("c p n -> p c n"), inc=[s_in])
        if biases:
            pb.dma("sync", b1_sb[:], b1_d.rearrange("e p n -> p e n"), inc=[s_in])
            pb.dma("sync", b2_sb[:], b2_d.rearrange("e p n -> p e n"), inc=[s_in])
        if rms:
            pb.dma("sync", g_sb[:], gain_d, inc=[s_in])
        IN0 = pb.at(s_in)
        if rms:
            for mc in range(FC):
                pb.emit("vector", lambda E, mc=mc: E.tensor_scalar_mul(
                    wr_sb[:, mc, :], wr_sb[:, mc, :], g_sb[:, mc:mc + 1]),
                    wait=[IN0], inc=[s_ev])
        WRG = pb.at(s_ev)

        with nc.sbuf_tensor("xs0", [128, 512], F32) as xs0, \
             nc.sbuf_tensor("xs1", [128, 512], F32) as xs1, \
             nc.sbuf_tensor("xq0", [128, 512], F32) as xq0, \
             nc.sbuf_tensor("xq1", [128, 512], F32) as xq1, \
             nc.sbuf_tensor("srow", [1, TOK], F32) as srow, \
             nc.sbuf_tensor("mrow", [1, TOK], F32) as mrow, \
             nc.sbuf_tensor("mb", [128, 512], F32) as mb, \
             nc.sbuf_tensor("rb", [128, 512], F32) as rb:
            xss = [xs0, xs1]
            xqs = [xq0, xq1]
            with nc.psum_tensor("mean_ps", [1, TOK], F32) as mean_ps, \
                 nc.psum_tensor("sq_ps", [1, TOK], F32) as sq_ps:
                cons_mm = {}
                cons_act = {}
                for idx in range(32):
                    fc, tc = idx // 4, idx % 4
                    tl = xss[idx % 2]
                    xq = xqs[idx % 2]
                    if idx >= 2:
                        w = [(s_lnmm, cons_mm[idx - 2]),
                             (s_lnact, cons_act[idx - 2])]
                    else:
                        w = []
                    pb.dma("sync", tl[:],
                           xin[fc * 128:(fc + 1) * 128, tc * 512:(tc + 1) * 512],
                           wait=w, inc=[s_xt[idx % 2]])
                    XL = pb.at(s_xt[idx % 2])
                    if not rms:
                        pb.emit("tensor", lambda E, tl=tl, fc=fc, tc=tc: E.matmul(
                            out=mean_ps[:, tc * 512:(tc + 1) * 512],
                            lhsT=ones_f[:], rhs=tl[:],
                            start=(fc == 0), stop=(fc == FC - 1)),
                            wait=[XL, CONST], inc=[s_lnmm])
                    pb.emit("scalar", lambda E, tl=tl, xq=xq: E.activation(
                        out=xq[:], in_=tl[:], func=AF.Square),
                        wait=[XL] + ([(s_lnmm, cons_mm[idx - 2])] if idx >= 2 else []),
                        inc=[s_lnact])
                    pb.emit("tensor", lambda E, xq=xq, fc=fc, tc=tc: E.matmul(
                        out=sq_ps[:, tc * 512:(tc + 1) * 512],
                        lhsT=ones_f[:], rhs=xq[:],
                        start=(fc == 0), stop=(fc == FC - 1)),
                        wait=[pb.at(s_lnact), CONST], inc=[s_lnmm])
                    cons_mm[idx] = s_lnmm.n
                    cons_act[idx] = s_lnact.n
                if not rms:
                    pb.emit("vector", lambda E: E.tensor_scalar_mul(
                        mrow[:], mean_ps[:], 1.0 / DM),
                        wait=[pb.at(s_lnmm)], inc=[s_ln])
                    pb.emit("vector", lambda E: E.tensor_tensor(
                        out=srow[:], in0=mrow[:], in1=mrow[:], op=ALU.mult),
                        wait=[pb.at(s_ln)], inc=[s_ln])
                    pb.emit("vector", lambda E: E.scalar_tensor_tensor(
                        out=srow[:], in0=sq_ps[:], scalar=1.0 / DM, in1=srow[:],
                        op0=ALU.mult, op1=ALU.subtract),
                        wait=[pb.at(s_ln)], inc=[s_ln])
                else:
                    pb.emit("vector", lambda E: E.tensor_scalar_mul(
                        srow[:], sq_ps[:], 1.0 / DM),
                        wait=[pb.at(s_lnmm)], inc=[s_ln])
                pb.emit("scalar", lambda E: E.activation(
                    out=rr_sb[:], in_=srow[:], func=AF.Sqrt, bias=eps_c[:]),
                    wait=[pb.at(s_ln), CONST], inc=[s_lnact])
                pb.emit("scalar", lambda E: E.activation(
                    out=rr_sb[:], in_=rr_sb[:], func=AF.Ln),
                    wait=[pb.at(s_lnact)])
                pb.emit("scalar", lambda E: E.activation(
                    out=rr_sb[:], in_=rr_sb[:], func=AF.Exp, scale=-1.0),
                    inc=[s_lnact])
                pb.emit("vector", lambda E: E.engine_nop(),
                        wait=[pb.at(s_lnact)], inc=[s_ln])
            with nc.psum_tensor("bc1_ps", [128, 512], F32) as bc1_ps, \
                 nc.psum_tensor("bc2_ps", [128, 512], F32) as bc2_ps, \
                 nc.psum_tensor("lg_ps", [8, TOK], F32) as lg_ps:
                RR = pb.at(s_ln)
                rt_mm = {}
                for tc in range(4):
                    sl = slice(tc * 512, (tc + 1) * 512)
                    pb.emit("tensor", lambda E, sl=sl: E.matmul(
                        out=bc1_ps[:], lhsT=onesr_f[:], rhs=rr_sb[:, sl],
                        start=True, stop=True),
                        wait=[RR, pb.at(s_ln), CONST], inc=[s_lnmm])
                    pb.emit("vector", lambda E: E.tensor_copy(
                        out=rb[:], in_=bc1_ps[:]),
                        wait=[pb.at(s_lnmm), pb.at(s_ln)], inc=[s_ln])
                    if not rms:
                        pb.emit("tensor", lambda E, sl=sl: E.matmul(
                            out=bc2_ps[:], lhsT=onesr_f[:], rhs=mrow[:, sl],
                            start=True, stop=True),
                            wait=[pb.at(s_ln)], inc=[s_lnmm])
                        pb.emit("vector", lambda E: E.tensor_copy(
                            out=mb[:], in_=bc2_ps[:]),
                            wait=[pb.at(s_lnmm)], inc=[s_ln])
                    BC = pb.at(s_ln)
                    for fc in range(FC):
                        idx = tc * 8 + fc
                        tl = xss[idx % 2]
                        xq = xqs[idx % 2]
                        rtw = ([(s_lnmm, rt_mm[idx - 2])] if idx >= 2 else [])
                        pb.dma("sync", tl[:], xin[fc * 128:(fc + 1) * 128, sl],
                               wait=[pb.at(s_ln), pb.at(s_xt[idx % 2])] +
                               (rtw if rms else []),
                               inc=[s_xt[idx % 2]])
                        XL = pb.at(s_xt[idx % 2])
                        if not rms:
                            pb.emit("vector", lambda E, tl=tl, xq=xq: E.tensor_tensor(
                                out=xq[:], in0=tl[:], in1=mb[:], op=ALU.subtract),
                                wait=[XL, BC, pb.at(s_ln)] + rtw, inc=[s_ln])
                            srct = xq
                            SR = pb.at(s_ln)
                        else:
                            srct = tl
                            SR = XL
                        pb.emit("tensor", lambda E, st=srct, fc=fc, sl=sl: E.matmul(
                            out=lg_ps[:, sl], lhsT=wr_sb[:, fc, :], rhs=st[:],
                            start=(fc == 0), stop=(fc == FC - 1)),
                            wait=[SR, WRG, IN0], inc=[s_lnmm])
                        rt_mm[idx] = s_lnmm.n
                        if rms:
                            pb.emit("vector", lambda E, st=srct, xq=xq, fc=fc,
                                    sl=sl: E.tensor_tensor(
                                out=xq[:], in0=st[:], in1=rb[:], op=ALU.mult),
                                wait=[pb.at(s_ln), BC, XL], inc=[s_ln])
                            pb.emit("vector", lambda E, xq=xq, fc=fc, sl=sl:
                                    E.tensor_scalar_mul(
                                xn_sb[:, fc, sl], xq[:], g_sb[:, fc:fc + 1]),
                                wait=[pb.at(s_ln), IN0], inc=[s_ln])
                        else:
                            pb.emit("vector", lambda E, st=srct, fc=fc, sl=sl:
                                    E.tensor_tensor(
                                out=xn_sb[:, fc, sl], in0=st[:], in1=rb[:],
                                op=ALU.mult),
                                wait=[pb.at(s_ln), BC, XL], inc=[s_ln])
                    pb.emit("tensor", lambda E, sl=sl: E.matmul(
                        out=bc2_ps[0:8, :], lhsT=onesr_f[:, 0:8], rhs=rr_sb[:, sl],
                        start=True, stop=True),
                        wait=[pb.at(s_lnmm), pb.at(s_ln)], inc=[s_lnmm])
                    pb.emit("vector", lambda E, sl=sl: E.tensor_copy(
                        out=lg_sb[:, sl], in_=lg_ps[:, sl]),
                        wait=[pb.at(s_lnmm)], inc=[s_ln])
                    pb.emit("vector", lambda E, sl=sl: E.tensor_tensor(
                        out=lg_sb[:, sl], in0=lg_sb[:, sl], in1=bc2_ps[0:8, :],
                        op=ALU.mult),
                        wait=[pb.at(s_ln), pb.at(s_lnmm)], inc=[s_ln])
        LN_V = pb.at(s_ln)

        with nc.sbuf_tensor("lgt", [128, 16, NE], F32) as lgt, \
             nc.sbuf_tensor("mx", [128, 16], F32) as mx, \
             nc.sbuf_tensor("sm", [128, 16], F32) as sm, \
             nc.sbuf_tensor("oh", [128, 16, NE], F32) as oh, \
             nc.sbuf_tensor("pm", [128, 16, NE], F32) as pm, \
             nc.psum_tensor("tp_ps", [128, 16 * NE], F32) as tp_ps:
            for g in range(16):
                pb.emit("tensor", lambda E, g=g: E.transpose(
                    out=tp_ps[:, g * NE:(g + 1) * NE],
                    in_=lg_sb[0:NE, g * 128:(g + 1) * 128],
                    identity=ident_f[0:NE, 0:NE]),
                    wait=[LN_V, CONST] if g == 0 else [],
                    inc=[s_mm] if g == 15 else [])
            pb.emit("vector", lambda E: E.tensor_copy(out=lgt[:], in_=tp_ps[:]
                    .rearrange("p (g e) -> p g e", e=NE)),
                    wait=[pb.at(s_mm)], inc=[s_ev])
            pb.emit("vector", lambda E: E.reduce_max(
                out=mx[:], in_=lgt[:], axis=AX.X),
                wait=[pb.at(s_ev)], inc=[s_ev])
            pb.emit("vector", lambda E: E.tensor_tensor(
                out=lgt[:], in0=lgt[:],
                in1=mx[:].unsqueeze(2).to_broadcast([128, 16, NE]),
                op=ALU.subtract),
                wait=[pb.at(s_ev)], inc=[s_ev])
            pb.emit("scalar", lambda E: E.activation(
                out=lgt[:], in_=lgt[:], func=AF.Exp),
                wait=[pb.at(s_ev)], inc=[s_act])
            pb.emit("vector", lambda E: E.reduce_sum(
                out=sm[:], in_=lgt[:], axis=AX.X),
                wait=[pb.at(s_act)], inc=[s_ev])
            pb.emit("scalar", lambda E: E.activation(
                out=sm[:], in_=sm[:], func=AF.Ln),
                wait=[pb.at(s_ev)])
            pb.emit("scalar", lambda E: E.activation(
                out=sm[:], in_=sm[:], func=AF.Exp, scale=-1.0), inc=[s_act])
            pb.emit("vector", lambda E: E.tensor_tensor(
                out=lgt[:], in0=lgt[:],
                in1=sm[:].unsqueeze(2).to_broadcast([128, 16, NE]), op=ALU.mult),
                wait=[pb.at(s_act), pb.at(s_ev)], inc=[s_ev])
            pb.emit("vector", lambda E: E.reduce_max(
                out=mx[:], in_=lgt[:], axis=AX.X),
                wait=[pb.at(s_ev)], inc=[s_ev])
            pb.emit("vector", lambda E: E.tensor_tensor(
                out=oh[:], in0=lgt[:],
                in1=mx[:].unsqueeze(2).to_broadcast([128, 16, NE]),
                op=ALU.is_ge),
                wait=[pb.at(s_ev)], inc=[s_ev])
            pb.emit("vector", lambda E: E.tensor_tensor(
                out=wt_sb[:], in0=oh[:],
                in1=mx[:].unsqueeze(2).to_broadcast([128, 16, NE]), op=ALU.mult),
                wait=[pb.at(s_ev)], inc=[s_ev])
            if kind == "cap2":
                pb.emit("vector", lambda E: E.tensor_scalar(
                    out=pm[:], in0=oh[:], scalar1=-1e9, scalar2=None,
                    op0=ALU.mult),
                    wait=[pb.at(s_ev)], inc=[s_ev])
                pb.emit("vector", lambda E: E.tensor_tensor(
                    out=pm[:], in0=pm[:], in1=lgt[:], op=ALU.add),
                    wait=[pb.at(s_ev)], inc=[s_ev])
                pb.emit("vector", lambda E: E.reduce_max(
                    out=mx[:], in_=pm[:], axis=AX.X),
                    wait=[pb.at(s_ev)], inc=[s_ev])
                pb.emit("vector", lambda E: E.tensor_tensor(
                    out=g2w_sb[:], in0=pm[:],
                    in1=mx[:].unsqueeze(2).to_broadcast([128, 16, NE]),
                    op=ALU.is_ge),
                    wait=[pb.at(s_ev)], inc=[s_ev])
                pb.emit("vector", lambda E: E.tensor_tensor(
                    out=g2w_sb[:], in0=g2w_sb[:],
                    in1=mx[:].unsqueeze(2).to_broadcast([128, 16, NE]),
                    op=ALU.mult),
                    wait=[pb.at(s_ev)], inc=[s_ev])
                pb.dma("sync", g_loc[0:TOK, :]
                       .rearrange("(g p) e -> p g e", p=128), wt_sb[:],
                       wait=[pb.at(s_ev)], inc=[s_gd])
                pb.dma("sync", g_loc[TOK:2 * TOK, :]
                       .rearrange("(g p) e -> p g e", p=128), g2w_sb[:],
                       wait=[pb.at(s_ev)], inc=[s_gd])
                pb.emit("gpsimd", lambda E: E.collective_compute(
                    "AllGather", ALU.bypass, replica_groups=[list(range(NCORES))],
                    ins=[g_loc.opt()], outs=[g_all.opt()]),
                    wait=[pb.at(s_gd)], inc=[s_ag], chain=False)
        if kind == "cap2":
            AGD = pb.at(s_ag)
            with nc.sbuf_tensor("cnt_sb", [128, 256, NE], F32) as cnt_sb, \
                 nc.sbuf_tensor("ind", [128, 256, NE], BF16) as ind, \
                 nc.sbuf_tensor("red", [128, NE], F32) as red, \
                 nc.sbuf_tensor("lo", [1, NE], F32) as lo, \
                 nc.sbuf_tensor("hi", [1, NE], F32) as hi, \
                 nc.sbuf_tensor("mid", [1, NE], F32) as mid, \
                 nc.sbuf_tensor("prd", [1, NE], F32) as prd, \
                 nc.sbuf_tensor("tv", [1, NE], F32) as tv, \
                 nc.sbuf_tensor("thb", [128, NE], F32) as thb, \
                 nc.sbuf_tensor("cnt1", [1, NE], F32) as cnt1, \
                 nc.psum_tensor("thb_ps", [128, NE], F32) as thb_ps, \
                 nc.psum_tensor("cnt_ps", [1, NE], F32) as cnt_ps:
                pb.dma("sync", cnt_sb[:],
                       g_all.rearrange("(p c) e -> p c e", p=128),
                       wait=[AGD], inc=[s_gd])
                GLD = pb.at(s_gd)
                pb.emit("vector", lambda E: E.memset(lo[:], 0.0), inc=[s_ev])
                pb.emit("vector", lambda E: E.memset(hi[:], 1.0), inc=[s_ev])
                pb.emit("vector", lambda E: E.memset(mid[:], 0.5), inc=[s_ev])
                for it in range(36):
                    pb.emit("tensor", lambda E: E.matmul(
                        out=thb_ps[:], lhsT=onesr_f[:], rhs=mid[:],
                        start=True, stop=True),
                        wait=[pb.at(s_ev), CONST], inc=[s_mm])
                    pb.emit("vector", lambda E: E.tensor_copy(
                        out=thb[:], in_=thb_ps[:]),
                        wait=[pb.at(s_mm)], inc=[s_ev])
                    pb.emit("vector", lambda E: E.tensor_tensor(
                        out=ind[:], in0=cnt_sb[:],
                        in1=thb[:].unsqueeze(1).to_broadcast([128, 256, NE]),
                        op=ALU.is_ge),
                        wait=[pb.at(s_ev), GLD], inc=[s_ev])
                    pb.emit("vector", lambda E: E.reduce_sum(
                        out=red[:], in_=ind[:].rearrange("p c e -> p e c"),
                        axis=AX.X),
                        wait=[pb.at(s_ev)], inc=[s_ev])
                    pb.emit("tensor", lambda E: E.matmul(
                        out=cnt_ps[:], lhsT=ones_f[:], rhs=red[:],
                        start=True, stop=True),
                        wait=[pb.at(s_ev), CONST], inc=[s_mm])
                    pb.emit("vector", lambda E: E.tensor_copy(
                        out=cnt1[:], in_=cnt_ps[:]),
                        wait=[pb.at(s_mm)], inc=[s_ev])
                    pb.emit("vector", lambda E: E.tensor_scalar(
                        out=prd[:], in0=cnt1[:], scalar1=CAP + 0.5, scalar2=None,
                        op0=ALU.is_ge),
                        wait=[pb.at(s_ev)], inc=[s_ev])
                    pb.emit("vector", lambda E: E.tensor_tensor(
                        out=tv[:], in0=mid[:], in1=lo[:], op=ALU.subtract),
                        wait=[pb.at(s_ev)], inc=[s_ev])
                    pb.emit("vector", lambda E: E.tensor_tensor(
                        out=tv[:], in0=tv[:], in1=prd[:], op=ALU.mult),
                        wait=[pb.at(s_ev)], inc=[s_ev])
                    pb.emit("vector", lambda E: E.tensor_tensor(
                        out=lo[:], in0=lo[:], in1=tv[:], op=ALU.add),
                        wait=[pb.at(s_ev)], inc=[s_ev])
                    pb.emit("vector", lambda E: E.tensor_tensor(
                        out=tv[:], in0=hi[:], in1=mid[:], op=ALU.subtract),
                        wait=[pb.at(s_ev)], inc=[s_ev])
                    pb.emit("vector", lambda E: E.tensor_tensor(
                        out=tv[:], in0=tv[:], in1=prd[:], op=ALU.mult),
                        wait=[pb.at(s_ev)], inc=[s_ev])
                    pb.emit("vector", lambda E: E.tensor_tensor(
                        out=hi[:], in0=mid[:], in1=tv[:], op=ALU.add),
                        wait=[pb.at(s_ev)], inc=[s_ev])
                    pb.emit("vector", lambda E: E.tensor_tensor(
                        out=mid[:], in0=lo[:], in1=hi[:], op=ALU.add),
                        wait=[pb.at(s_ev)], inc=[s_ev])
                    pb.emit("vector", lambda E: E.tensor_scalar_mul(
                        mid[:], mid[:], 0.5),
                        wait=[pb.at(s_ev)], inc=[s_ev])
                if "dbg_th" in dbg_t:
                    pb.dma("sync", dbg_t["dbg_th"], hi[:],
                           wait=[pb.at(s_ev)], inc=[s_out])
                pb.emit("tensor", lambda E: E.matmul(
                    out=thb_ps[:], lhsT=onesr_f[:], rhs=hi[:],
                    start=True, stop=True),
                    wait=[pb.at(s_ev), CONST], inc=[s_mm])
                pb.emit("vector", lambda E: E.tensor_copy(
                    out=thb[:], in_=thb_ps[:]),
                    wait=[pb.at(s_mm)], inc=[s_ev])
                with nc.sbuf_tensor("selt", [128, 16, NE], F32) as selt:
                    pb.emit("vector", lambda E: E.tensor_tensor(
                        out=selt[:], in0=wt_sb[:],
                        in1=thb[:].unsqueeze(1).to_broadcast([128, 16, NE]),
                        op=ALU.is_ge),
                        wait=[pb.at(s_ev)], inc=[s_ev])
                    pb.emit("vector", lambda E: E.tensor_tensor(
                        out=wt_sb[:], in0=wt_sb[:], in1=selt[:], op=ALU.mult),
                        wait=[pb.at(s_ev)], inc=[s_ev])
                    pb.emit("vector", lambda E: E.tensor_tensor(
                        out=selt[:], in0=g2w_sb[:],
                        in1=thb[:].unsqueeze(1).to_broadcast([128, 16, NE]),
                        op=ALU.is_ge),
                        wait=[pb.at(s_ev)], inc=[s_ev])
                    pb.emit("vector", lambda E: E.tensor_tensor(
                        out=selt[:], in0=selt[:], in1=g2w_sb[:], op=ALU.mult),
                        wait=[pb.at(s_ev)], inc=[s_ev])
                    pb.emit("vector", lambda E: E.tensor_tensor(
                        out=wt_sb[:], in0=wt_sb[:], in1=selt[:], op=ALU.add),
                        wait=[pb.at(s_ev)], inc=[s_ev])
        W_V = pb.at(s_ev)
        if "dbg_w" in dbg_t:
            pb.dma("sync", dbg_t["dbg_w"], wt_sb[:], wait=[W_V], inc=[s_out])

        if SPC:
            C = SPC
            NSG = C // 128
            IOA = bass.IndirectOffsetOnAxis
            U32 = mybir.dt.uint32
            bregs = {}

            def _mkregs(E, bregs=bregs):
                bregs["idx"] = E.to_reg(NE * SPC - 1)
                bregs["tok"] = E.to_reg(TOK - 1)
            pb.emit("gpsimd", _mkregs, chain=False)
            s_xnt = pb.sem("s_xnt")
            s_ind = pb.sem("s_ind")
            s_pf = pb.sem("s_pf")
            s_id = pb.sem("s_id")
            s_yz = pb.sem("s_yz")
            s_yl = pb.sem("s_yl")
            with nc.sbuf_tensor("xtt", [128, DM], BF16) as xtt, \
                 nc.psum_tensor("xt_ps", [128, 512], BF16) as xt_ps:
                for g in range(16):
                    for half in range(2):
                        for q in range(4):
                            fc = half * 4 + q
                            pb.emit("tensor", lambda E, g=g, q=q, fc=fc: E.transpose(
                                out=xt_ps[:, q * 128:(q + 1) * 128],
                                in_=xn_sb[:, fc, g * 128:(g + 1) * 128],
                                identity=ident_b[:]),
                                wait=([LN_V, CONST, pb.at(s_ev)]
                                      if (g == 0 and fc == 0) else
                                      ([pb.at(s_ev)] if q == 0 else [])),
                                inc=[s_mm] if q == 3 else [])
                        pb.emit("vector", lambda E, half=half: E.tensor_copy(
                            out=xtt[:, half * 512:(half + 1) * 512], in_=xt_ps[:]),
                            wait=[pb.at(s_mm), pb.at(s_xnt)], inc=[s_ev])
                    pb.dma("sync", xn_tok_d[g * 128:(g + 1) * 128, :], xtt[:],
                           wait=[pb.at(s_ev), pb.at(s_xnt)], inc=[s_xnt])
            XNT = pb.at(s_xnt)
            with nc.sbuf_tensor("m_sb", [128, 16, NE], F32) as m_sb, \
                 nc.sbuf_tensor("sA", [128, 16, NE], F32) as sA, \
                 nc.sbuf_tensor("sB", [128, 16, NE], F32) as sB, \
                 nc.sbuf_tensor("r_sb", [128, NE], F32) as r_sb, \
                 nc.sbuf_tensor("c_sb", [128, NE], F32) as c_sb, \
                 nc.sbuf_tensor("sidx", [128, 16, NE], F32) as sidx, \
                 nc.sbuf_tensor("ovf", [128, 16, NE], F32) as ovf, \
                 nc.sbuf_tensor("padt", [128, NE * SPC // 128, 2], F32) as padt, \
                 nc.sbuf_tensor("srct", [128, 2], F32) as srct, \
                 nc.sbuf_tensor("sidx_u", [128, 1], U32) as sidx_u, \
                 nc.psum_tensor("c_ps", [128, NE], F32) as c_ps:
                pb.emit("vector", lambda E: E.memset(padt[:], BIGI),
                        wait=[W_V, XNT], inc=[s_ev])
                pb.emit("vector", lambda E: E.memset(padt[:, :, 1], 0.0),
                        wait=[pb.at(s_ev)], inc=[s_ev])
                pb.dma("sync", idx_d.rearrange("(g p) c -> p g c", p=128), padt[:],
                       wait=[pb.at(s_ev)], inc=[s_pf])
                pb.emit("vector", lambda E: E.tensor_scalar(
                    out=m_sb[:], in0=wt_sb[:], scalar1=1e-30, scalar2=None,
                    op0=ALU.is_ge), wait=[W_V, pb.at(s_ev)], inc=[s_ev])
                pb.emit("vector", lambda E: E.reduce_sum(
                    out=r_sb[:], in_=m_sb[:].rearrange("p g e -> p e g"),
                    axis=AX.X), wait=[pb.at(s_ev)], inc=[s_ev])
                pb.emit("tensor", lambda E: E.matmul(
                    out=c_ps[:], lhsT=ltri[:], rhs=r_sb[:], start=True, stop=True),
                    wait=[pb.at(s_ev), CONST], inc=[s_mm])
                pb.emit("vector", lambda E: E.tensor_copy(out=c_sb[:], in_=c_ps[:]),
                        wait=[pb.at(s_mm)], inc=[s_ev])
                pb.emit("vector", lambda E: E.tensor_copy(out=sA[:], in_=m_sb[:]),
                        wait=[pb.at(s_ev)], inc=[s_ev])
                cur, nxt = sA, sB
                for k in (1, 2, 4, 8):
                    pb.emit("vector", lambda E, cur=cur, nxt=nxt, k=k: E.tensor_copy(
                        out=nxt[:, 0:k, :], in_=cur[:, 0:k, :]),
                        wait=[pb.at(s_ev)], inc=[s_ev])
                    pb.emit("vector", lambda E, cur=cur, nxt=nxt, k=k: E.tensor_tensor(
                        out=nxt[:, k:16, :], in0=cur[:, k:16, :],
                        in1=cur[:, 0:16 - k, :], op=ALU.add),
                        wait=[pb.at(s_ev)], inc=[s_ev])
                    cur, nxt = nxt, cur
                pb.emit("vector", lambda E, cur=cur: E.tensor_tensor(
                    out=sidx[:], in0=cur[:], in1=m_sb[:], op=ALU.subtract),
                    wait=[pb.at(s_ev)], inc=[s_ev])
                pb.emit("vector", lambda E: E.tensor_tensor(
                    out=sidx[:], in0=sidx[:],
                    in1=c_sb[:].unsqueeze(1).to_broadcast([128, 16, NE]),
                    op=ALU.add), wait=[pb.at(s_ev)], inc=[s_ev])
                pb.emit("vector", lambda E: E.tensor_scalar(
                    out=ovf[:], in0=sidx[:], scalar1=float(C) - 0.5, scalar2=None,
                    op0=ALU.is_ge), wait=[pb.at(s_ev)], inc=[s_ev])
                pb.emit("vector", lambda E: E.tensor_tensor(
                    out=sidx[:], in0=sidx[:],
                    in1=eoff_sb[:].unsqueeze(1).to_broadcast([128, 16, NE]),
                    op=ALU.add), wait=[pb.at(s_ev), CONST], inc=[s_ev])
                pb.emit("vector", lambda E: E.scalar_tensor_tensor(
                    out=sidx[:], in0=ovf[:], scalar=BIGI, in1=sidx[:],
                    op0=ALU.mult, op1=ALU.add), wait=[pb.at(s_ev)], inc=[s_ev])
                pb.emit("vector", lambda E: E.scalar_tensor_tensor(
                    out=sidx[:], in0=m_sb[:], scalar=-BIGI, in1=sidx[:],
                    op0=ALU.mult, op1=ALU.add), wait=[pb.at(s_ev)], inc=[s_ev])
                pb.emit("vector", lambda E: E.tensor_scalar(
                    out=sidx[:], in0=sidx[:], scalar1=BIGI, scalar2=None,
                    op0=ALU.add), wait=[pb.at(s_ev)], inc=[s_ev])
                for g in range(16):
                    pb.emit("vector", lambda E, g=g: E.tensor_scalar(
                        out=srct[:, 0:1], in0=iota_sb[:], scalar1=float(g * 128),
                        scalar2=None, op0=ALU.add),
                        wait=[pb.at(s_ev), IN0, pb.at(s_pf), pb.at(s_ind)],
                        inc=[s_ev])
                    for e in range(NE):
                        pb.emit("vector", lambda E, g=g, e=e: E.tensor_copy(
                            out=srct[:, 1:2], in_=wt_sb[:, g, e:e + 1]),
                            wait=[pb.at(s_ev), pb.at(s_ind)], inc=[s_ev])
                        pb.emit("vector", lambda E, g=g, e=e: E.tensor_copy(
                            out=sidx_u[:], in_=sidx[:, g, e:e + 1]),
                            wait=[pb.at(s_ev)], inc=[s_ev])
                        pb.emit("gpsimd", lambda E: E.indirect_dma_start(
                            out=idx_d,
                            out_offset=IOA(ap=sidx_u[:], axis=0),
                            in_=srct[:], in_offset=None,
                            bounds_check=bregs["idx"], oob_is_err=False,
                            compute_op=ALU.bypass),
                            wait=[pb.at(s_ev)], inc=[s_ind], inc_amt=16,
                            chain=False)
            IDXD = pb.at(s_ind)
            p4c = ExitStack()
            with p4c:
                sb4 = lambda n, s, d: p4c.enter_context(nc.sbuf_tensor(n, s, d))
                ps4 = lambda n: p4c.enter_context(
                    nc.psum_tensor(n, [128, 512], F32))
                wu0 = sb4("wu0", [128, FC, 128], BF16)
                wu1 = sb4("wu1", [128, FC, 128], BF16)
                wd0 = sb4("wd0", [128, FBC, 128], BF16)
                wd1 = sb4("wd1", [128, FBC, 128], BF16)
                h_sb = sb4("h_sb", [128, FBC, C], BF16)
                ye_acc = sb4("ye_acc", [128, FC, C], F32)
                xg = sb4("xg", [128, DM], BF16)
                xe = sb4("xe", [128, FC, C], BF16)
                idro = sb4("idro", [128, 4, 2], F32)
                idxu = sb4("idxu", [128, 4], U32)
                ysc = sb4("ysc", [128, DM], F32)
                zt = sb4("zt", [128, DM], F32)
                up_ps0 = ps4("up_ps0")
                up_ps1 = ps4("up_ps1")
                dn_ps0 = ps4("dn_ps0")
                dn_ps1 = ps4("dn_ps1")
                tr_ps = p4c.enter_context(
                    nc.psum_tensor("tr_ps", [128, 512], BF16))
                trf_ps = ps4("trf_ps")
                wus = [wu0, wu1]
                wds = [wd0, wd1]
                upss = [up_ps0, up_ps1]
                dnss = [dn_ps0, dn_ps1]
                pb.emit("vector", lambda E: E.memset(zt[:], 0.0),
                        wait=[IDXD, XNT], inc=[s_ev])
                pb.emit("vector", lambda E: E.memset(xg[:], 0.0), inc=[s_ev])
                ZT = pb.at(s_ev)
                for g in range(16):
                    pb.dma("sync", y_tok_d[g * 128:(g + 1) * 128, :], zt[:],
                           wait=[ZT], inc=[s_yz])
                YZ0 = pb.at(s_yz)
                uidx = 0
                didx = 0
                for e in range(NE):
                    pb.dma("sync", idro[:, 0:NSG, :],
                           idx_d[e * SPC:(e + 1) * SPC, :]
                           .rearrange("(s p) c -> p s c", p=128),
                           wait=[IDXD, pb.at(s_ev), pb.at(s_id)], inc=[s_id])
                    IDL = pb.at(s_id)
                    pb.emit("vector", lambda E: E.tensor_copy(
                        out=idxu[:, 0:NSG], in_=idro[:, 0:NSG, 0]),
                        wait=[IDL, pb.at(s_ind)], inc=[s_ev])
                    IDU = pb.at(s_ev)
                    for sg in range(NSG):
                        pb.emit("gpsimd", lambda E, sg=sg: E.indirect_dma_start(
                            out=xg[:], out_offset=None,
                            in_=xn_tok_d,
                            in_offset=IOA(ap=idxu[:, sg:sg + 1], axis=0),
                            bounds_check=bregs["tok"], oob_is_err=False,
                            compute_op=ALU.bypass),
                            wait=[IDU, XNT, pb.at(s_mm), pb.at(s_ind), ZT],
                            inc=[s_ind], inc_amt=16, chain=False)
                        XG = pb.at(s_ind)
                        for fc in range(FC):
                            pb.emit("tensor", lambda E, fc=fc: E.transpose(
                                out=tr_ps[:, (fc % 4) * 128:(fc % 4 + 1) * 128],
                                in_=xg[:, fc * 128:(fc + 1) * 128],
                                identity=ident_b[:]),
                                wait=([XG, CONST, pb.at(s_ev)] if fc == 0 else
                                      ([pb.at(s_ev)] if fc == 4 else [])),
                                inc=[s_mm] if fc % 4 == 3 else [])
                            if fc % 4 == 3:
                                pb.emit("vector", lambda E, sg=sg, fc=fc:
                                        E.tensor_copy(
                                    out=xe[:, fc - 3:fc + 1,
                                           sg * 128:(sg + 1) * 128],
                                    in_=tr_ps[:].rearrange(
                                        "p (c n) -> p c n", n=128)),
                                    wait=[pb.at(s_mm)], inc=[s_ev])
                    XE = pb.at(s_ev)
                    for fb in range(NFB):
                        for oc in range(FBC):
                            wb = wus[uidx % 2]
                            pb.dma("gpsimd", wb[:],
                                   w1_d[e, :, :, (fb * FBC + oc) * 128:
                                        (fb * FBC + oc + 1) * 128]
                                   .rearrange("c p n -> p c n"),
                                   wait=[pb.at(s_act), pb.at(s_mm)] if uidx >= 2
                                   else [LN_V, W_V, IDXD, XNT, pb.at(s_out)],
                                   inc=[s_wu[uidx % 2]])
                            WBU = pb.at(s_wu[uidx % 2])
                            pp = upss[oc % 2]
                            for mc in range(FC):
                                pb.emit("tensor", lambda E, wb=wb, mc=mc,
                                        pp=pp: E.matmul(
                                    out=pp[:, 0:C], lhsT=wb[:, mc, :],
                                    rhs=xe[:, mc, :],
                                    start=(mc == 0), stop=(mc == FC - 1)),
                                    wait=([WBU, XE, pb.at(s_act)]
                                          if mc == 0 else []),
                                    inc=[s_mm] if mc == FC - 1 else [])
                            MM = pb.at(s_mm)
                            if biases:
                                pb.emit("scalar", lambda E, pp=pp, oc=oc, fb=fb,
                                        e=e: E.activation(
                                    out=h_sb[:, oc, :], in_=pp[:, 0:C],
                                    func=act_f,
                                    bias=b1_sb[:, e, fb * FBC + oc:
                                               fb * FBC + oc + 1]),
                                    wait=[MM, IN0], inc=[s_act])
                            else:
                                pb.emit("scalar", lambda E, pp=pp,
                                        oc=oc: E.activation(
                                    out=h_sb[:, oc, :], in_=pp[:, 0:C],
                                    func=act_f),
                                    wait=[MM], inc=[s_act])
                            uidx += 1
                        HUP = pb.at(s_act)
                        for mc in range(FC):
                            wb = wds[didx % 2]
                            pb.dma("gpsimd", wb[:],
                                   w2_d[e, fb * FBC:(fb + 1) * FBC, :,
                                        mc * 128:(mc + 1) * 128]
                                   .rearrange("c p n -> p c n"),
                                   wait=[pb.at(s_mm), pb.at(s_ev)] if didx >= 2
                                   else [LN_V, W_V, IDXD, XNT, pb.at(s_out)],
                                   inc=[s_wd2[didx % 2]])
                            WBD = pb.at(s_wd2[didx % 2])
                            pp = dnss[mc % 2]
                            for fc2 in range(FBC):
                                pb.emit("tensor", lambda E, wb=wb, fc2=fc2,
                                        pp=pp: E.matmul(
                                    out=pp[:, 0:C], lhsT=wb[:, fc2, :],
                                    rhs=h_sb[:, fc2, :],
                                    start=(fc2 == 0), stop=(fc2 == FBC - 1)),
                                    wait=([WBD, HUP, pb.at(s_ev)]
                                          if fc2 == 0 else []),
                                    inc=[s_mm] if fc2 == FBC - 1 else [])
                            MMD = pb.at(s_mm)
                            if fb == 0 and biases:
                                pb.emit("vector", lambda E, pp=pp, e=e,
                                        mc=mc: E.tensor_scalar(
                                    out=ye_acc[:, mc, :], in0=pp[:, 0:C],
                                    scalar1=b2_sb[:, e, mc:mc + 1],
                                    scalar2=None, op0=ALU.add),
                                    wait=[MMD, IN0, pb.at(s_ev)], inc=[s_ev])
                            elif fb == 0:
                                pb.emit("vector", lambda E, pp=pp,
                                        mc=mc: E.tensor_copy(
                                    out=ye_acc[:, mc, :], in_=pp[:, 0:C]),
                                    wait=[MMD, pb.at(s_ev)], inc=[s_ev])
                            else:
                                pb.emit("vector", lambda E, pp=pp, mc=mc:
                                        E.tensor_tensor(
                                    out=ye_acc[:, mc, :],
                                    in0=ye_acc[:, mc, :], in1=pp[:, 0:C],
                                    op=ALU.add),
                                    wait=[MMD, pb.at(s_ev)], inc=[s_ev])
                            didx += 1
                    YE = pb.at(s_ev)
                    for sg in range(NSG):
                        for fc in range(FC):
                            pb.emit("tensor", lambda E, sg=sg, fc=fc: E.transpose(
                                out=trf_ps[:, (fc % 4) * 128:(fc % 4 + 1) * 128],
                                in_=ye_acc[:, fc, sg * 128:(sg + 1) * 128],
                                identity=ident_f[:]),
                                wait=([YE, CONST, pb.at(s_ev), pb.at(s_ind)]
                                      if fc == 0 else
                                      ([pb.at(s_ev)] if fc == 4 else [])),
                                inc=[s_mm] if fc % 4 == 3 else [])
                            if fc % 4 == 3:
                                pb.emit("vector", lambda E, fc=fc: E.tensor_copy(
                                    out=ysc[:, (fc - 3) * 128:(fc + 1) * 128],
                                    in_=trf_ps[:]),
                                    wait=[pb.at(s_mm), pb.at(s_ind)], inc=[s_ev])
                        pb.emit("vector", lambda E, sg=sg: E.tensor_scalar_mul(
                            ysc[:], ysc[:], idro[:, sg, 1:2]),
                            wait=[pb.at(s_ev), IDL], inc=[s_ev])
                        pb.emit("gpsimd", lambda E, sg=sg: E.indirect_dma_start(
                            out=y_tok_d,
                            out_offset=IOA(ap=idxu[:, sg:sg + 1], axis=0),
                            in_=ysc[:], in_offset=None,
                            bounds_check=bregs["tok"], oob_is_err=False,
                            compute_op=ALU.add),
                            wait=[pb.at(s_ev), YZ0, pb.at(s_ind)],
                            inc=[s_ind], inc_amt=16, chain=False)
            YSC = pb.at(s_ind)
            with nc.sbuf_tensor("yr", [128, 16, DM], F32) as yr, \
                 nc.sbuf_tensor("bs0", [128, 512], F32) as bs0, \
                 nc.sbuf_tensor("bs1", [128, 512], F32) as bs1, \
                 nc.sbuf_tensor("br0", [128, 512], F32) as br0, \
                 nc.sbuf_tensor("br1", [128, 512], F32) as br1, \
                 nc.psum_tensor("res_ps", [128, 512], F32) as res_ps:
                bss = [bs0, bs1]
                brs = [br0, br1]
                for g in range(16):
                    pb.dma("sync", yr[:, g, :], y_tok_d[g * 128:(g + 1) * 128, :],
                           wait=[YSC, pb.at(s_yz)] if g == 0 else [],
                           inc=[s_yl])
                YRL = pb.at(s_yl)
                for fc in range(FC):
                    for gq in range(4):
                        for g4 in range(4):
                            g = gq * 4 + g4
                            pb.emit("tensor", lambda E, g=g, g4=g4,
                                    fc=fc: E.transpose(
                                out=res_ps[:, g4 * 128:(g4 + 1) * 128],
                                in_=yr[:, g, fc * 128:(fc + 1) * 128],
                                identity=ident_f[:]),
                                wait=([YRL, CONST, pb.at(s_ev)]
                                      if g4 == 0 else []),
                                inc=[s_mm] if g4 == 3 else [])
                        sl = slice(gq * 512, (gq + 1) * 512)
                        i = fc * 4 + gq
                        bs = bss[i % 2]
                        br = brs[i % 2]
                        pb.dma("sync", bs[:], xin[fc * 128:(fc + 1) * 128, sl],
                               wait=[pb.at(s_so[i % 2]), pb.at(s_ev)],
                               inc=[s_bas[i % 2]])
                        if xres_f:
                            pb.dma("sync", br[:],
                                   xres[fc * 128:(fc + 1) * 128, sl],
                                   inc=[s_bas[i % 2]])
                        BL = pb.at(s_bas[i % 2])
                        pb.emit("vector", lambda E, bs=bs: E.tensor_tensor(
                            out=bs[:], in0=bs[:], in1=res_ps[:], op=ALU.add),
                            wait=[BL, pb.at(s_mm), pb.at(s_ev)], inc=[s_ev])
                        if xres_f:
                            pb.emit("vector", lambda E, bs=bs, br=br:
                                    E.tensor_tensor(
                                out=bs[:], in0=bs[:], in1=br[:], op=ALU.add),
                                wait=[pb.at(s_ev)], inc=[s_ev])
                        pb.dma("sync", xout[fc * 128:(fc + 1) * 128, sl], bs[:],
                               wait=[pb.at(s_ev)], inc=[s_so[i % 2]])
        else:
            with nc.sbuf_tensor("wrow_t", [16, 128], F32) as wrow_t, \
                 nc.psum_tensor("wr_ps", [16, 128], F32) as wr_ps:
                for e in range(NE):
                    pb.emit("tensor", lambda E, e=e: E.transpose(
                        out=wr_ps[:], in_=wt_sb[:, :, e], identity=ident_f[:]),
                        wait=[W_V, CONST, pb.at(s_ev)], inc=[s_mm])
                    pb.emit("vector", lambda E: E.tensor_copy(
                        out=wrow_t[:], in_=wr_ps[:]),
                        wait=[pb.at(s_mm), pb.at(s_wr)], inc=[s_ev])
                    pb.dma("sync", wrow_d[e:e + 1, :]
                           .rearrange("o (g p) -> (o g) p", p=128), wrow_t[:],
                           wait=[pb.at(s_ev), pb.at(s_wr)], inc=[s_wr])
            GRD = pb.at(s_wr)

            with nc.sbuf_tensor("wu0", [128, FC, 128], BF16) as wu0, \
                 nc.sbuf_tensor("wu1", [128, FC, 128], BF16) as wu1, \
                 nc.sbuf_tensor("wd0", [128, FBC, 128], BF16) as wd0, \
                 nc.sbuf_tensor("wd1", [128, FBC, 128], BF16) as wd1, \
                 nc.sbuf_tensor("h_sb", [128, FBC, TOK], BF16) as h_sb, \
                 nc.sbuf_tensor("grow", [1, TOK], F32) as grow, \
                 nc.sbuf_tensor("gb_sb", [128, TOK], F32) as gb_sb, \
                 nc.sbuf_tensor("yt", [128, 512], F32) as yt, \
                 nc.psum_tensor("up_ps0", [128, 512], F32) as up_ps0, \
                 nc.psum_tensor("up_ps1", [128, 512], F32) as up_ps1, \
                 nc.psum_tensor("dn_ps0", [128, 512], F32) as dn_ps0, \
                 nc.psum_tensor("dn_ps1", [128, 512], F32) as dn_ps1, \
                 nc.psum_tensor("gb_ps", [128, 512], F32) as gb_ps:
                wus = [wu0, wu1]
                wds = [wd0, wd1]
                upss = [up_ps0, up_ps1]
                dnss = [dn_ps0, dn_ps1]
                uidx = 0
                didx = 0
                for e in range(NE):
                    pb.dma("sync", grow[:], wrow_d[e:e + 1, :],
                           wait=[GRD, pb.at(s_ev), pb.at(s_mm), pb.at(s_gd)],
                           inc=[s_gd])
                    GR1 = pb.at(s_gd)
                    for tc in range(4):
                        sl = slice(tc * 512, (tc + 1) * 512)
                        pb.emit("tensor", lambda E, sl=sl: E.matmul(
                            out=gb_ps[:], lhsT=onesr_f[:], rhs=grow[:, sl],
                            start=True, stop=True),
                            wait=[GR1, CONST, pb.at(s_ev)], inc=[s_mm])
                        pb.emit("vector", lambda E, sl=sl: E.tensor_copy(
                            out=gb_sb[:, sl], in_=gb_ps[:]),
                            wait=[pb.at(s_mm)], inc=[s_ev])
                    GBD = pb.at(s_ev)
                    for fb in range(NFB):
                        for oc in range(FBC):
                            wb = wus[uidx % 2]
                            pb.dma("gpsimd", wb[:],
                                   w1_d[e, :, :, (fb * FBC + oc) * 128:
                                        (fb * FBC + oc + 1) * 128]
                                   .rearrange("c p n -> p c n"),
                                   wait=[pb.at(s_act), pb.at(s_mm)] if uidx >= 2
                                   else [LN_V, W_V, GRD, pb.at(s_out)],
                                   inc=[s_wu[uidx % 2]])
                            WBU = pb.at(s_wu[uidx % 2])
                            for tc in range(4):
                                sl = slice(tc * 512, (tc + 1) * 512)
                                pp = upss[tc % 2]
                                for mc in range(FC):
                                    pb.emit("tensor", lambda E, wb=wb, mc=mc, sl=sl,
                                            pp=pp: E.matmul(
                                        out=pp[:], lhsT=wb[:, mc, :],
                                        rhs=xn_sb[:, mc, sl],
                                        start=(mc == 0), stop=(mc == FC - 1)),
                                        wait=([WBU, LN_V, pb.at(s_act)]
                                              if mc == 0 else []),
                                        inc=[s_mm] if mc == FC - 1 else [])
                                MM = pb.at(s_mm)
                                if biases:
                                    pb.emit("scalar", lambda E, pp=pp, oc=oc, fb=fb,
                                            e=e, sl=sl: E.activation(
                                        out=h_sb[:, oc, sl], in_=pp[:], func=act_f,
                                        bias=b1_sb[:, e, fb * FBC + oc:
                                                   fb * FBC + oc + 1]),
                                        wait=[MM, IN0], inc=[s_act])
                                else:
                                    pb.emit("scalar", lambda E, pp=pp, oc=oc,
                                            sl=sl: E.activation(
                                        out=h_sb[:, oc, sl], in_=pp[:], func=act_f),
                                        wait=[MM], inc=[s_act])
                            uidx += 1
                        HUP = pb.at(s_act)
                        for mc in range(FC):
                            wb = wds[didx % 2]
                            pb.dma("gpsimd", wb[:],
                                   w2_d[e, fb * FBC:(fb + 1) * FBC, :,
                                        mc * 128:(mc + 1) * 128]
                                   .rearrange("c p n -> p c n"),
                                   wait=[pb.at(s_mm), pb.at(s_ev)] if didx >= 2
                                   else [LN_V, W_V, GRD, pb.at(s_out)],
                                   inc=[s_wd2[didx % 2]])
                            WBD = pb.at(s_wd2[didx % 2])
                            for tc in range(4):
                                sl = slice(tc * 512, (tc + 1) * 512)
                                pp = dnss[tc % 2]
                                for fc2 in range(FBC):
                                    pb.emit("tensor", lambda E, wb=wb, fc2=fc2, sl=sl,
                                            pp=pp: E.matmul(
                                        out=pp[:], lhsT=wb[:, fc2, :],
                                        rhs=h_sb[:, fc2, sl],
                                        start=(fc2 == 0), stop=(fc2 == FBC - 1)),
                                        wait=([WBD, HUP, pb.at(s_ev)]
                                              if fc2 == 0 else []),
                                        inc=[s_mm] if fc2 == FBC - 1 else [])
                                MMD = pb.at(s_mm)
                                if biases and fb == 0:
                                    pb.emit("vector", lambda E, pp=pp, e=e, mc=mc,
                                            sl=sl: E.scalar_tensor_tensor(
                                        out=yt[:], in0=pp[:],
                                        scalar=b2_sb[:, e, mc:mc + 1],
                                        in1=gb_sb[:, sl], op0=ALU.add, op1=ALU.mult),
                                        wait=[MMD, IN0, GBD, pb.at(s_ev)],
                                        inc=[s_ev])
                                else:
                                    pb.emit("vector", lambda E, pp=pp, sl=sl:
                                            E.tensor_tensor(
                                        out=yt[:], in0=pp[:], in1=gb_sb[:, sl],
                                        op=ALU.mult),
                                        wait=[MMD, GBD, pb.at(s_ev)], inc=[s_ev])
                                pb.emit("vector", lambda E, mc=mc, sl=sl:
                                        E.tensor_tensor(
                                    out=y_acc[:, mc, sl], in0=y_acc[:, mc, sl],
                                    in1=yt[:], op=ALU.add),
                                    wait=[pb.at(s_ev), CONST], inc=[s_ev])
                            didx += 1
            YD = pb.at(s_ev)

            with nc.sbuf_tensor("bs0", [128, 512], F32) as bs0, \
                 nc.sbuf_tensor("bs1", [128, 512], F32) as bs1, \
                 nc.sbuf_tensor("br0", [128, 512], F32) as br0, \
                 nc.sbuf_tensor("br1", [128, 512], F32) as br1:
                bss = [bs0, bs1]
                brs = [br0, br1]
                for i, (fc, tc) in enumerate([(f, t) for f in range(FC)
                                              for t in range(4)]):
                    sl = slice(tc * 512, (tc + 1) * 512)
                    bs = bss[i % 2]
                    br = brs[i % 2]
                    pb.dma("sync", bs[:], xin[fc * 128:(fc + 1) * 128, sl],
                           wait=[pb.at(s_so[i % 2]), pb.at(s_ev)],
                           inc=[s_bas[i % 2]])
                    if xres_f:
                        pb.dma("sync", br[:], xres[fc * 128:(fc + 1) * 128, sl],
                               inc=[s_bas[i % 2]])
                    BL = pb.at(s_bas[i % 2])
                    pb.emit("vector", lambda E, bs=bs, fc=fc, sl=sl: E.tensor_tensor(
                        out=bs[:], in0=bs[:], in1=y_acc[:, fc, sl], op=ALU.add),
                        wait=[BL, YD, pb.at(s_ev)], inc=[s_ev])
                    if xres_f:
                        pb.emit("vector", lambda E, bs=bs, br=br: E.tensor_tensor(
                            out=bs[:], in0=bs[:], in1=br[:], op=ALU.add),
                            wait=[pb.at(s_ev)], inc=[s_ev])
                    pb.dma("sync", xout[fc * 128:(fc + 1) * 128, sl], bs[:],
                           wait=[pb.at(s_ev)], inc=[s_so[i % 2]])
        pb.emit("sync", lambda E: E.nop(),
                wait=[pb.at(s_so[0]), pb.at(s_so[1]), pb.at(s_out)])
        if fuse is None:
            pb.realize()
    return nc._nc


def prep_moe_inputs(cfg, weights, x_parts, xres_parts=None, pfx=None):
    NE = 8
    DFF = cfg["dff"]
    w1 = np.ascontiguousarray(np.asarray(weights["w1"], np.float32)
                              .reshape(NE, FC, 128, DFF))
    w2 = np.ascontiguousarray(np.asarray(weights["w2"], np.float32)
                              .reshape(NE, DFF // 128, 128, DM))
    wr = chunked_w(np.asarray(weights["wr"], np.float32))
    in_maps = []
    for core in range(NCORES):
        m = {"w1": w1, "w2": w2, "wr": wr}
        if pfx is None:
            m["xin"] = x_parts[core]
            if cfg["xres"]:
                m["xres"] = xres_parts[core]
        if cfg["biases"]:
            b1 = np.asarray(weights["b1"], np.float32)
            b2 = np.asarray(weights["b2"], np.float32)
            m["b1"] = np.ascontiguousarray(
                b1.reshape(NE, DFF // 128, 128).transpose(0, 2, 1))
            m["b2"] = np.ascontiguousarray(
                b2.reshape(NE, FC, 128).transpose(0, 2, 1))
        if cfg["rms"]:
            m["gain"] = colsplit(np.asarray(weights["gain"], np.float32))
        if cfg.get("sparse"):
            m["iota"] = np.arange(128, dtype=np.float32).reshape(128, 1)
        if pfx:
            m = {pfx + k: v for k, v in m.items()}
        in_maps.append(m)
    return in_maps


MOE_CFGS = {
    "L1_ffn": dict(kind="top1", dff=4096, act="gelu", biases=True, rms=False,
                   xres=False, sparse=512),
    "L0_moe": dict(kind="cap2", dff=2048, act="relu", biases=False, rms=True,
                   xres=True, sparse=384),
}


ATTN_CFGS = {
    "L0_attn": dict(H=8, HD=128, scale=1.0 / 128.0, rope="half",
                    qk_bias=False, o_bias=False, two_norms=False),
    "L0_mha": dict(H=16, HD=64, scale=1.0 / 8.0, rope="int",
                   qk_bias=False, o_bias=False, two_norms=True),
    "L1_attn": dict(H=16, HD=64, scale=1.0 / 8.0, rope=None,
                    qk_bias=True, o_bias=True, two_norms=False),
}


def _run(nc, in_maps):
    from concourse.bass_utils import run_bass_kernel_spmd
    res = run_bass_kernel_spmd(nc, in_maps, core_ids=list(range(NCORES)))
    return res


BLOCK_ORDER = [
    ("b0_", "attn", "L0_attn"),
    ("b1_", "attn", "L0_mha"),
    ("b2_", "moe", "L0_moe"),
    ("b3_", "attn", "L1_attn"),
    ("b4_", "moe", "L1_ffn"),
]


def build_fused():
    nc = bass.Bass("TRN2", target_bir_lowering=False)
    xin = nc.dram_tensor("xin", [DM, TOK], F32, kind="ExternalInput").ap()
    xout = nc.dram_tensor("xout", [DM, TOK], F32, kind="ExternalOutput").ap()
    d = {n: nc.dram_tensor(n, [DM, TOK], F32, kind="Internal").ap()
         for n in ("d_x1", "d_z1", "d_zn", "d_x2", "d_x3")}
    io = {
        "b0_": dict(xin=xin, xout=d["d_x1"]),
        "b1_": dict(xin=d["d_x1"], xout=d["d_z1"], zout=d["d_zn"]),
        "b2_": dict(xin=d["d_z1"], xres=d["d_x1"], xout=d["d_x2"]),
        "b3_": dict(xin=d["d_x2"], xout=d["d_x3"]),
        "b4_": dict(xin=d["d_x3"], xout=xout),
    }
    ctx = ExitStack()
    with ctx:
        pb = PB(nc, ctx)
        pb.init_chains()
        for i, (pfx, kind, name) in enumerate(BLOCK_ORDER):
            cfg = dict(ATTN_CFGS[name] if kind == "attn" else MOE_CFGS[name])
            cfg["debug"] = ()
            cfg["ablate"] = ""
            fuse = dict(nc=nc, pb=pb, pfx=pfx, **io[pfx])
            if kind == "attn":
                build_attn(cfg, fuse)
            else:
                build_moe(cfg, fuse)
            if i < len(BLOCK_ORDER) - 1:
                pb.fence()
        pb.realize()
    return nc


def prep_fused_inputs(I, x0_parts):
    in_maps = [{"xin": x0_parts[c]} for c in range(NCORES)]

    def merge(parts):
        for c in range(NCORES):
            in_maps[c].update(parts[c])

    cfg = dict(ATTN_CFGS["L0_attn"])
    w = {"wq": I["L0_attn_wq"].reshape(DM, DM),
         "wk": I["L0_attn_wk"].reshape(DM, DM),
         "wv": I["L0_attn_wv"].reshape(DM, DM),
         "wo": I["L0_attn_wo"].reshape(DM, DM)}
    merge(prep_attn_inputs(cfg, w, None, pfx="b0_"))

    cfg = dict(ATTN_CFGS["L0_mha"])
    wqkv = I["L0_ffn_w_qkv"].reshape(DM, 3 * DM)
    w = {"wq": wqkv[:, 0:DM], "wk": wqkv[:, DM:2 * DM],
         "wv": wqkv[:, 2 * DM:3 * DM],
         "wo": I["L0_ffn_w_o"].reshape(DM, DM),
         "gain": I["L0_ffn_g1"].reshape(DM)}
    merge(prep_attn_inputs(cfg, w, None, pfx="b1_"))

    cfg = dict(MOE_CFGS["L0_moe"])
    w = {"wr": I["L0_ffn_w_router"].reshape(DM, 8),
         "w1": I["L0_ffn_w1"], "w2": I["L0_ffn_w2"],
         "gain": I["L0_ffn_g2"].reshape(DM)}
    merge(prep_moe_inputs(cfg, w, None, pfx="b2_"))

    cfg = dict(ATTN_CFGS["L1_attn"])
    wqkv = I["L1_attn_Wqkv"].reshape(DM, 3 * DM)
    w = {"wq": wqkv[:, 0:DM], "wk": wqkv[:, DM:2 * DM],
         "wv": wqkv[:, 2 * DM:3 * DM],
         "wo": I["L1_attn_Wproj"].reshape(DM, DM),
         "bqkv": I["L1_attn_bqkv"].reshape(3 * DM),
         "bo": I["L1_attn_bproj"].reshape(DM)}
    merge(prep_attn_inputs(cfg, w, None, pfx="b3_"))

    cfg = dict(MOE_CFGS["L1_ffn"])
    w = {"wr": I["L1_ffn_gate_w"].reshape(DM, 8),
         "w1": I["L1_ffn_w1"], "b1": I["L1_ffn_b1"],
         "w2": I["L1_ffn_w2"], "b2": I["L1_ffn_b2"]}
    merge(prep_moe_inputs(cfg, w, None, pfx="b4_"))
    return in_maps


def kernel(**inputs):
    I = {k: np.asarray(v, dtype=np.float32) for k, v in inputs.items()}
    nc = build_fused()
    in_maps = prep_fused_inputs(I, shard_stream(I["L0_attn_x"]))
    res = _run(nc, in_maps)
    return unshard_stream([res.results[c]["xout"]
                           for c in range(NCORES)]).astype(np.float32)


def kernel_multi(**inputs):
    I = {k: np.asarray(v, dtype=np.float32) for k, v in inputs.items()}
    x0_parts = shard_stream(I["L0_attn_x"])

    cfg = dict(ATTN_CFGS["L0_attn"])
    cfg["debug"] = ()
    cfg["ablate"] = ""
    nc = build_attn(cfg)
    w = {"wq": I["L0_attn_wq"].reshape(DM, DM),
         "wk": I["L0_attn_wk"].reshape(DM, DM),
         "wv": I["L0_attn_wv"].reshape(DM, DM),
         "wo": I["L0_attn_wo"].reshape(DM, DM)}
    res = _run(nc, prep_attn_inputs(cfg, w, x0_parts))
    x1_parts = [res.results[c]["xout"] for c in range(NCORES)]

    cfg = dict(ATTN_CFGS["L0_mha"])
    cfg["debug"] = ()
    cfg["ablate"] = ""
    nc = build_attn(cfg)
    wqkv = I["L0_ffn_w_qkv"].reshape(DM, 3 * DM)
    w = {"wq": wqkv[:, 0:DM], "wk": wqkv[:, DM:2 * DM],
         "wv": wqkv[:, 2 * DM:3 * DM],
         "wo": I["L0_ffn_w_o"].reshape(DM, DM),
         "gain": I["L0_ffn_g1"].reshape(DM)}
    res = _run(nc, prep_attn_inputs(cfg, w, x1_parts))
    z1_parts = [res.results[c]["xout"] for c in range(NCORES)]

    cfg = dict(MOE_CFGS["L0_moe"])
    cfg["debug"] = ()
    nc = build_moe(cfg)
    w = {"wr": I["L0_ffn_w_router"].reshape(DM, 8),
         "w1": I["L0_ffn_w1"], "w2": I["L0_ffn_w2"],
         "gain": I["L0_ffn_g2"].reshape(DM)}
    res = _run(nc, prep_moe_inputs(cfg, w, z1_parts, x1_parts))
    x2_parts = [res.results[c]["xout"] for c in range(NCORES)]

    cfg = dict(ATTN_CFGS["L1_attn"])
    cfg["debug"] = ()
    cfg["ablate"] = ""
    nc = build_attn(cfg)
    wqkv = I["L1_attn_Wqkv"].reshape(DM, 3 * DM)
    w = {"wq": wqkv[:, 0:DM], "wk": wqkv[:, DM:2 * DM],
         "wv": wqkv[:, 2 * DM:3 * DM],
         "wo": I["L1_attn_Wproj"].reshape(DM, DM),
         "bqkv": I["L1_attn_bqkv"].reshape(3 * DM),
         "bo": I["L1_attn_bproj"].reshape(DM)}
    res = _run(nc, prep_attn_inputs(cfg, w, x2_parts))
    x3_parts = [res.results[c]["xout"] for c in range(NCORES)]

    cfg = dict(MOE_CFGS["L1_ffn"])
    cfg["debug"] = ()
    nc = build_moe(cfg)
    w = {"wr": I["L1_ffn_gate_w"].reshape(DM, 8),
         "w1": I["L1_ffn_w1"], "b1": I["L1_ffn_b1"],
         "w2": I["L1_ffn_w2"], "b2": I["L1_ffn_b2"]}
    res = _run(nc, prep_moe_inputs(cfg, w, x3_parts))
    x4_parts = [res.results[c]["xout"] for c in range(NCORES)]

    return unshard_stream(x4_parts).astype(np.float32)
